# version 02
# baseline (speedup 1.0000x reference)
import contextlib
import numpy as np
import ml_dtypes
import concourse.bass as bass
import concourse.mybir as mybir
from concourse.bass_utils import run_bass_kernel_spmd

F32 = mybir.dt.float32
BF16 = mybir.dt.bfloat16
AF = mybir.ActivationFunctionType
ALU = mybir.AluOpType
AX = mybir.AxisListType

T = 2048
D = 2048
INNER = 8192
H = 16
HD = 128
NT = T // 128
KD = D // 128
KI = INNER // 128
SUP = 4
NSUP = NT // SUP
STRIPE = 1024
NSTRIPE = T // STRIPE
EPS = 1e-5
C32 = float(1.5 * 2**23)
C64 = float(1.5 * 2**52)
ATT_SCALE = float(HD) ** -0.5
N_CORES = 8

STAGE = 3
DBG_SPILL = False
B4_SERIAL = False
DBG_B4 = False


class Sem:
    def __init__(self, h):
        self.h = h
        self.n = 0

    def bump(self, k=1):
        self.n += k
        return self.n


class Prog:
    def __init__(self):
        nc = bass.Bass("TRN2", target_bir_lowering=False,
                       detect_race_conditions=False)
        self.nc = nc
        self.es = contextlib.ExitStack()
        io = {}
        io["x"] = nc.dram_tensor("x", [T, D], F32, kind="ExternalInput")
        io["w1"] = nc.dram_tensor("w1", [INNER, D], F32, kind="ExternalInput")
        io["w2"] = nc.dram_tensor("w2", [D, INNER], F32, kind="ExternalInput")
        io["wq"] = nc.dram_tensor("wq", [H * D, HD], F32, kind="ExternalInput")
        io["wk"] = nc.dram_tensor("wk", [H * D, HD], F32, kind="ExternalInput")
        io["wv"] = nc.dram_tensor("wv", [H * D, HD], F32, kind="ExternalInput")
        io["wo"] = nc.dram_tensor("wo", [D, D], F32, kind="ExternalInput")
        io["bq"] = nc.dram_tensor("bq", [H, HD], F32, kind="ExternalInput")
        io["bk"] = nc.dram_tensor("bk", [H, HD], F32, kind="ExternalInput")
        io["bv"] = nc.dram_tensor("bv", [H, HD], F32, kind="ExternalInput")
        io["bo"] = nc.dram_tensor("bo", [1, D], F32, kind="ExternalInput")
        io["identf"] = nc.dram_tensor("identf", [128, 128], F32, kind="ExternalInput")
        io["identb"] = nc.dram_tensor("identb", [128, 128], BF16, kind="ExternalInput")
        io["ones"] = nc.dram_tensor("ones", [128, 128], F32, kind="ExternalInput")
        io["caus"] = nc.dram_tensor("caus", [128, 128], F32, kind="ExternalInput")
        io["y"] = nc.dram_tensor("y", [T, D], F32, kind="ExternalOutput")
        io["w1qT"] = nc.dram_tensor("w1qT", [D, INNER], BF16)
        io["w2qT"] = nc.dram_tensor("w2qT", [INNER, D], BF16)
        io["xr1"] = nc.dram_tensor("xr1", [T, D], F32,
                                   kind="ExternalOutput" if DBG_B4 else "Internal")
        io["xr2"] = nc.dram_tensor("xr2", [T, D], F32)
        kk = "ExternalOutput" if DBG_SPILL else "Internal"
        if DBG_B4:
            io["dbgxnT"] = nc.dram_tensor("dbgxnT", [2 * 128, KD * 1024], F32,
                                          kind="ExternalOutput")
            io["dbgstat"] = nc.dram_tensor("dbgstat", [2 * 128, 104], F32,
                                           kind="ExternalOutput")
        io["qsc"] = nc.dram_tensor("qsc", [H * HD, T], F32, kind=kk)
        io["ksc"] = nc.dram_tensor("ksc", [H * HD, T], F32, kind=kk)
        io["vsc"] = nc.dram_tensor("vsc", [H * T, HD], F32, kind=kk)
        self.io = io

        self.sems = {n: self.es.enter_context(nc.semaphore(n)) for n in
                     ["qs0", "qs1", "qw0", "qw1", "qa0", "qa1", "qa2",
                      "qg0", "qg1", "qk0", "qk1", "vec", "vch", "mm", "sc"]}
        self.counts = {k: 0 for k in self.sems}

        P = {}
        def psb(name, shape, dtype=F32):
            P[name] = self.es.enter_context(nc.sbuf_tensor(name, shape, dtype))
        psb("c_identf", [128, 128])
        psb("c_identb", [128, 128], BF16)
        psb("c_ones", [128, 128])
        psb("c_flag", [128, 1])
        psb("c_wsc", [128, 4])
        psb("c_bqT", [128, H])
        psb("c_bkT", [128, H])
        psb("c_bvT", [128, H])
        psb("c_catt", [128, 1])
        nc.const_aps.register(F32, ATT_SCALE, P["c_catt"][:, 0:1])
        self.P = P

    def run_block(self, emit):
        final = {}
        with self.nc.Block() as block:
            for role, reg in [("sync", block.sync), ("gpsimd", block.gpsimd),
                              ("vector", block.vector), ("tensor", block.tensor),
                              ("scalar", block.scalar)]:
                def f(eng, role=role):
                    ctx = self._ctx(role, eng)
                    emit(role, eng, ctx)
                    final.clear()
                    final.update({k: s.n for k, s in ctx["S"].items()})
                reg(f)
        self.counts = dict(final)

    def _ctx(self, role, eng):
        sems = self.sems
        S = {k: Sem(h) for k, h in sems.items()}
        for k, v in self.counts.items():
            S[k].n = v
        ctx = {"S": S}

        def R(r):
            return role == r
        ctx["R"] = R

        def vlink(mk):
            c = S["vch"].bump()
            if R("vector"):
                mk().then_inc(sems["vch"], 1)
                eng.wait_ge(sems["vch"], c)
        ctx["vlink"] = vlink

        QOWNER = {"qs0": "sync", "qs1": "sync", "qw0": "sync", "qw1": "sync",
                  "qa0": "scalar", "qa1": "scalar", "qa2": "scalar",
                  "qg0": "gpsimd", "qg1": "gpsimd",
                  "qk0": "sync", "qk1": "sync"}

        def dma(q, dst, src):
            c = S[q].bump(16)
            if R(QOWNER[q]):
                eng.dma_start(dst, src).then_inc(sems[q], 16)
            return c
        ctx["dma"] = dma

        def w(semname, val, who):
            if R(who):
                eng.wait_ge(sems[semname], val)
        ctx["w"] = w
        return ctx



def b0_consts(prog):
    nc, io, P, sems = prog.nc, prog.io, prog.P, prog.sems

    def emit(role, eng, ctx):
        R, S, vlink, dma, w = ctx["R"], ctx["S"], ctx["vlink"], ctx["dma"], ctx["w"]
        btmp, bps = prog.b0
        for name in ["identf", "identb", "ones"]:
            dma("qs0", P["c_" + name][:], io[name][:, :])
        for i, nm in enumerate(["bq", "bk", "bv"]):
            dma("qs0", btmp[0:H, i * 128:(i + 1) * 128], io[nm][:, :])
        c_all = S["qs0"].n
        w("qs0", c_all, "vector")
        vlink(lambda: eng.tensor_scalar(P["c_flag"][:], P["c_ones"][:, 0:1],
                                        2.0**-30, -1.0, ALU.add, ALU.add))
        vlink(lambda: eng.tensor_scalar(P["c_flag"][:], P["c_flag"][:],
                                        -(2.0**30), 1.0, ALU.mult, ALU.add))
        vlink(lambda: eng.tensor_scalar(P["c_catt"][:], P["c_ones"][:, 0:1],
                                        ATT_SCALE, None, ALU.mult))
        c_tr = S["mm"].bump()
        if R("tensor"):
            eng.wait_ge(sems["qs0"], c_all)
            for i in range(3):
                li = eng.transpose(bps[:, i * 128:i * 128 + H],
                                   btmp[0:H, i * 128:(i + 1) * 128],
                                   P["c_identf"][0:H, 0:H])
            li.then_inc(sems["mm"], 1)
        c_ev = S["vec"].bump()
        if R("vector"):
            eng.wait_ge(sems["mm"], c_tr)
            for i, nm in enumerate(["c_bqT", "c_bkT", "c_bvT"]):
                li = eng.tensor_copy(P[nm][:, 0:H], bps[:, i * 128:i * 128 + H])
            li.then_inc(sems["vec"], 1)
        for who in ["sync", "scalar", "tensor", "gpsimd"]:
            w("vec", c_ev, who)

    with nc.sbuf_tensor("b0tmp", [128, 3 * 128], F32) as btmp, \
         nc.psum_tensor("b0ps", [128, 512], F32) as bps:
        prog.b0 = (btmp, bps)
        prog.run_block(emit)
    del prog.b0


def b1_wscale(prog):
    nc, io, P, sems = prog.nc, prog.io, prog.P, prog.sems

    def emit(role, eng, ctx):
        R, S, vlink, dma, w = ctx["R"], ctx["S"], ctx["vlink"], ctx["dma"], ctx["w"]
        blk, acc, tot, ps = prog.b1
        col = 0
        vred = []
        for wname, nb, ncol in [("w1", INNER // 128, D), ("w2", D // 128, INNER)]:
            for b in range(nb):
                par = col % 2
                qn = "qw0" if par == 0 else "qw1"
                if col >= 2:
                    w("vec", vred[col - 2], "sync")
                cd = dma(qn, blk[:, par * INNER:par * INNER + ncol],
                         io[wname][b * 128:(b + 1) * 128, :])
                cv = S["vec"].bump()
                if R("vector"):
                    eng.wait_ge(sems[qn], cd)
                    eng.tensor_reduce(acc[:, col:col + 1],
                                      blk[:, par * INNER:par * INNER + ncol],
                                      AX.X, ALU.add,
                                      apply_absolute_value=True).then_inc(sems["vec"], 1)
                vred.append(cv)
                col += 1
        w("vec", S["vec"].n, "vector")
        vlink(lambda: eng.tensor_reduce(tot[:, 0:1], acc[:, 0:64], AX.X, ALU.add))
        vlink(lambda: eng.tensor_reduce(tot[:, 1:2], acc[:, 64:80], AX.X, ALU.add))
        c_tot = S["vec"].bump()
        if R("vector"):
            eng.tensor_copy(tot[:, 2:3], tot[:, 0:1]).then_inc(sems["vec"], 1)
        c_bc = S["mm"].bump()
        if R("tensor"):
            eng.wait_ge(sems["vec"], c_tot)
            eng.matmul(ps[:, 0:2], P["c_ones"][:, 0:128], tot[:, 0:2],
                       start=True, stop=True).then_inc(sems["mm"], 1)
        w("mm", c_bc, "vector")
        for j in range(2):
            vlink(lambda j=j: eng.tensor_scalar(
                P["c_wsc"][:, 2 * j + 1:2 * j + 2], ps[:, j:j + 1],
                1.0 / (INNER * D), EPS, ALU.mult, ALU.max))
            vlink(lambda j=j: eng.reciprocal(
                P["c_wsc"][:, 2 * j:2 * j + 1], P["c_wsc"][:, 2 * j + 1:2 * j + 2]))
        c_done = S["vec"].bump()
        if R("vector"):
            eng.tensor_copy(tot[:, 2:3], P["c_wsc"][:, 0:1]).then_inc(sems["vec"], 1)
        for who in ["sync", "scalar", "tensor", "gpsimd"]:
            w("vec", c_done, who)

    with nc.sbuf_tensor("b1blk", [128, 2 * INNER], F32) as blk, \
         nc.sbuf_tensor("b1acc", [128, 80], F32) as acc, \
         nc.sbuf_tensor("b1tot", [128, 3], F32) as tot, \
         nc.psum_tensor("b1ps", [128, 512], F32) as ps:
        prog.b1 = (blk, acc, tot, ps)
        prog.run_block(emit)
    del prog.b1


def b2_wquant(prog):
    nc, io, P, sems = prog.nc, prog.io, prog.P, prog.sems

    def emit(role, eng, ctx):
        R, S, vlink, dma, w = ctx["R"], ctx["S"], ctx["vlink"], ctx["dma"], ctx["w"]
        pan, tern, panT, tmpa, tmpb, trp = prog.b2
        jobs = [("w1", pc, INNER // 128) for pc in range(KD)] + \
               [("w2", pc, KD) for pc in range(KI)]
        st_after = []
        evac_cnt = []
        for jn, (wname, pc, nib) in enumerate(jobs):
            par = jn % 2
            wsccol = 0 if wname == "w1" else 2
            src = io["w1"] if wname == "w1" else io["w2"]
            dst = io["w1qT"] if wname == "w1" else io["w2qT"]
            qn = "qw0" if par == 0 else "qw1"
            qsn = "qa1" if par == 0 else "qa2"
            if jn >= 2:
                w(qsn, st_after[jn - 2], "sync")
            cd = dma(qn,
                     pan[:, par * INNER:par * INNER + nib * 128].rearrange(
                         "p (i c) -> p i c", c=128),
                     src[:, pc * 128:(pc + 1) * 128].rearrange(
                         "(i p) c -> p i c", p=128))
            c_tern = S["vec"].bump()
            if R("vector"):
                eng.wait_ge(sems[qn], cd)
                for ch in range((nib * 128 + 2047) // 2048):
                    a = ch * 2048
                    b = min((ch + 1) * 2048, nib * 128)
                    n = b - a
                    psl = slice(par * INNER + a, par * INNER + b)
                    eng.tensor_scalar(tmpa[:, 0:n], pan[:, psl],
                                      P["c_wsc"][:, wsccol:wsccol + 1], 0.5,
                                      ALU.mult, ALU.is_gt)
                    eng.tensor_scalar(tmpb[:, 0:n], pan[:, psl],
                                      P["c_wsc"][:, wsccol:wsccol + 1], -0.5,
                                      ALU.mult, ALU.is_lt)
                    li = eng.tensor_sub(tern[:, a:b], tmpa[:, 0:n], tmpb[:, 0:n])
                li.then_inc(sems["vec"], 1)
            for ib in range(nib):
                slot = len(evac_cnt) % 2
                c_tr = S["mm"].bump()
                if R("tensor"):
                    if ib == 0:
                        eng.wait_ge(sems["vec"], c_tern)
                    if len(evac_cnt) >= 2:
                        eng.wait_ge(sems["vec"], evac_cnt[-2])
                    eng.transpose(trp[:, slot * 1024:slot * 1024 + 128],
                                  tern[:, ib * 128:(ib + 1) * 128],
                                  P["c_identb"][:]).then_inc(sems["mm"], 1)
                c_ev = S["vec"].bump()
                if R("vector"):
                    eng.wait_ge(sems["mm"], c_tr)
                    eng.tensor_copy(
                        panT[:, par * INNER + ib * 128:par * INNER + (ib + 1) * 128],
                        trp[:, slot * 1024:slot * 1024 + 128]).then_inc(sems["vec"], 1)
                evac_cnt.append(c_ev)
            w("vec", evac_cnt[-1], "scalar")
            cs = dma(qsn, dst[pc * 128:(pc + 1) * 128, :],
                     panT[:, par * INNER:par * INNER + nib * 128])
            st_after.append(cs)
        for who in ["sync", "vector", "tensor", "scalar", "gpsimd"]:
            w("qa1", S["qa1"].n, who)
            w("qa2", S["qa2"].n, who)

    with nc.sbuf_tensor("b2pan", [128, 2 * INNER], F32) as pan, \
         nc.sbuf_tensor("b2tern", [128, INNER], BF16) as tern, \
         nc.sbuf_tensor("b2panT", [128, 2 * INNER], BF16) as panT, \
         nc.sbuf_tensor("b2tmpa", [128, 2048], F32) as tmpa, \
         nc.sbuf_tensor("b2tmpb", [128, 2048], F32) as tmpb, \
         nc.psum_tensor("b2ps", [128, 2048], BF16) as trp:
        prog.b2 = (pan, tern, panT, tmpa, tmpb, trp)
        prog.run_block(emit)
    del prog.b2


def ffn_pass(prog, src_name, dst_name, tag):
    nc, io, P, sems = prog.nc, prog.io, prog.P, prog.sems
    src_dram, dst_dram = io[src_name], io[dst_name]

    def emit(role, eng, ctx):
        R, S, vlink, dma, w = ctx["R"], ctx["S"], ctx["vlink"], ctx["dma"], ctx["w"]
        (g, xq, xqT, w1g, w2s, hqT, tmpa, tmpb, sgrid, ot,
         psA, psB, psC) = prog.fb
        stat = sgrid[:, 0:40]
        sqg = sgrid[:, 40:104]
        mxg = sgrid[:, 104:168]
        xt = ot[:, 0:D]
        sqdump = ot[:, D:D + 512]
        prev = {"v": 0, "s": 0, "st": 0, "stt": 0, "ffnout": 0}
        for sup in range(NSUP):
            t0 = sup * SUP
            stat_done = []
            for t in range(SUP):
                if t >= 1:
                    w("vec", stat_done[t - 1][0], "sync")
                    w("sc", stat_done[t - 1][1], "sync")
                elif sup > 0:
                    w("vec", prev["v"], "sync")
                    w("sc", prev["s"], "sync")
                    w("qa0", prev["st"], "sync")
                cd = dma("qs0", xt, src_dram[(t0 + t) * 128:(t0 + t + 1) * 128, :])
                cv = S["vec"].bump()
                if R("vector"):
                    eng.wait_ge(sems["qs0"], cd)
                    eng.reduce_sum(stat[:, t:t + 1], xt, AX.X)
                    eng.reduce_max(stat[:, 4 + t:5 + t], xt, AX.X)
                    eng.tensor_reduce(stat[:, 8 + t:9 + t], xt, AX.X,
                                      ALU.min).then_inc(sems["vec"], 1)
                cs = S["sc"].bump()
                if R("scalar"):
                    eng.wait_ge(sems["qs0"], cd)
                    for ch in range(4):
                        li = eng.activation(
                            sqdump[:, 0:512],
                            xt[:, ch * 512:(ch + 1) * 512], AF.Square,
                            accum_out=sqg[:, t * 16 + ch:t * 16 + ch + 1])
                    li.then_inc(sems["sc"], 1)
                stat_done.append((cv, cs))
            w("vec", S["vec"].n, "vector")
            w("sc", S["sc"].n, "vector")
            for t in range(SUP):
                vlink(lambda t=t: eng.reduce_sum(stat[:, 12 + t:13 + t],
                                                 sqg[:, t * 16:t * 16 + 4], AX.X))
            vlink(lambda: eng.tensor_scalar(stat[:, 16:20], stat[:, 0:4],
                                            1.0 / D, None, ALU.mult))
            vlink(lambda: eng.tensor_tensor(stat[:, 36:40], stat[:, 16:20],
                                            stat[:, 16:20], ALU.mult))
            vlink(lambda: eng.scalar_tensor_tensor(
                stat[:, 36:40], stat[:, 12:16], 1.0 / D, stat[:, 36:40],
                ALU.mult, ALU.subtract))
            c_var = S["vec"].bump()
            if R("vector"):
                eng.tensor_copy(stat[:, 32:33], stat[:, 36:37]).then_inc(sems["vec"], 1)
            c_sd = S["sc"].bump()
            if R("scalar"):
                eng.wait_ge(sems["vec"], c_var)
                eng.activation(stat[:, 36:40], stat[:, 36:40],
                               AF.Sqrt).then_inc(sems["sc"], 1)
            w("sc", c_sd, "vector")
            vlink(lambda: eng.tensor_sub(stat[:, 20:24], stat[:, 4:8], stat[:, 16:20]))
            vlink(lambda: eng.tensor_sub(stat[:, 32:36], stat[:, 16:20], stat[:, 8:12]))
            vlink(lambda: eng.tensor_max(stat[:, 20:24], stat[:, 20:24], stat[:, 32:36]))
            vlink(lambda: eng.reciprocal(stat[:, 32:36], stat[:, 36:40]))
            vlink(lambda: eng.tensor_mul(stat[:, 20:24], stat[:, 20:24], stat[:, 32:36]))
            vlink(lambda: eng.tensor_scalar(stat[:, 20:24], stat[:, 20:24], EPS,
                                            None, ALU.max))
            vlink(lambda: eng.reciprocal(stat[:, 28:32], stat[:, 20:24]))
            vlink(lambda: eng.tensor_scalar(stat[:, 28:32], stat[:, 28:32], 127.0,
                                            None, ALU.mult))
            vlink(lambda: eng.tensor_mul(stat[:, 28:32], stat[:, 28:32],
                                         stat[:, 32:36]))
            vlink(lambda: eng.tensor_scalar(stat[:, 24:28], stat[:, 20:24],
                                            1.0 / 127.0, None, ALU.mult))
            vlink(lambda: eng.tensor_scalar(stat[:, 24:28], stat[:, 24:28],
                                            P["c_wsc"][:, 1:2], None, ALU.mult))
            c_xchain = S["vec"].bump()
            if R("vector"):
                eng.tensor_copy(stat[:, 32:33], stat[:, 24:25]).then_inc(sems["vec"], 1)
            xqT_done = 0
            xev = []
            qprev = 0
            for t in range(SUP):
                if t >= 1:
                    w("vec", qprev, "sync")
                else:
                    w("vec", stat_done[-1][0], "sync")
                    w("sc", stat_done[-1][1], "sync")
                cd = dma("qs0", xt, src_dram[(t0 + t) * 128:(t0 + t + 1) * 128, :])
                for grp in range(8):
                    cv = S["vec"].bump()
                    if R("vector"):
                        if grp == 0:
                            eng.wait_ge(sems["qs0"], cd)
                        xsl = xt[:, grp * 256:(grp + 1) * 256]
                        eng.tensor_scalar(tmpa[:, 0:256], xsl,
                                          stat[:, 16 + t:17 + t],
                                          stat[:, 28 + t:29 + t],
                                          ALU.subtract, ALU.mult)
                        eng.tensor_scalar(tmpb[:, 0:256], tmpa[:, 0:256],
                                          C64, -C64, ALU.add, ALU.add)
                        eng.tensor_scalar(tmpa[:, 0:256], tmpa[:, 0:256],
                                          C32, -C32, ALU.add, ALU.add)
                        eng.scalar_tensor_tensor(
                            xq[:, 0:256], tmpa[:, 0:256], P["c_flag"][:, 0:1],
                            tmpb[:, 0:256], ALU.mult, ALU.add).then_inc(sems["vec"], 1)
                    for kk in range(2):
                        k = grp * 2 + kk
                        nx = len(xev)
                        slot = nx % 2
                        c_tr = S["mm"].bump()
                        if R("tensor"):
                            eng.wait_ge(sems["vec"], cv)
                            if sup > 0 and t == 0 and grp == 0 and kk == 0:
                                eng.wait_ge(sems["vec"], prev["stt"])
                            if nx >= 2:
                                eng.wait_ge(sems["vec"], xev[nx - 2])
                            eng.transpose(psC[:, slot * 1024:slot * 1024 + 128],
                                          xq[:, kk * 128:(kk + 1) * 128],
                                          P["c_identb"][:]).then_inc(sems["mm"], 1)
                        c_ev = S["vec"].bump()
                        if R("vector"):
                            eng.wait_ge(sems["mm"], c_tr)
                            eng.tensor_copy(
                                xqT[:, k * 512 + t * 128:k * 512 + (t + 1) * 128],
                                psC[:, slot * 1024:slot * 1024 + 128]
                            ).then_inc(sems["vec"], 1)
                        xev.append(c_ev)
                qprev = S["vec"].n
                xqT_done = xev[-1]
            mm_evac = {}
            w1g_used = {}
            for gg in range(16):
                par = gg % 2
                qn = "qw0" if par == 0 else "qw1"
                if gg >= 2:
                    w("mm", w1g_used[gg - 2], "sync")
                cd = dma(qn, w1g[:, par * KD * 512:(par + 1) * KD * 512].rearrange(
                             "p (k c) -> p k c", c=512),
                         io["w1qT"][:, gg * 512:(gg + 1) * 512].rearrange(
                             "(k p) c -> p k c", p=128))
                for t in range(SUP):
                    c_mm = S["mm"].bump()
                    if R("tensor"):
                        if t == 0:
                            eng.wait_ge(sems[qn], cd)
                            if gg == 0:
                                eng.wait_ge(sems["vec"], xqT_done)
                                if sup > 0:
                                    eng.wait_ge(sems["vec"], prev["ffnout"])
                        if gg >= 1:
                            eng.wait_ge(sems["sc"], mm_evac[(gg - 1, t)])
                        for k in range(KD):
                            li = eng.matmul(
                                psA[:, t * 512:(t + 1) * 512],
                                xqT[:, k * 512 + t * 128:k * 512 + (t + 1) * 128],
                                w1g[:, par * KD * 512 + k * 512:
                                    par * KD * 512 + (k + 1) * 512],
                                start=(k == 0), stop=(k == KD - 1))
                        li.then_inc(sems["mm"], 1)
                    gsl = g[:, t * INNER + gg * 512:t * INNER + (gg + 1) * 512]
                    cs = S["sc"].bump()
                    if R("scalar"):
                        eng.wait_ge(sems["mm"], c_mm)
                        if gg == 0 and t == 0:
                            eng.wait_ge(sems["vec"], c_xchain)
                        eng.activation(gsl, psA[:, t * 512:(t + 1) * 512],
                                       AF.Gelu, scale=stat[:, 24 + t:25 + t])
                        eng.activation(gsl, gsl, AF.Gelu)
                        eng.activation(sqdump[:, 0:512], gsl, AF.Square,
                                       accum_out=sqg[:, t * 16 + gg:t * 16 + gg + 1]
                                       ).then_inc(sems["sc"], 1)
                    mm_evac[(gg, t)] = cs
                    cv = S["vec"].bump()
                    if R("vector"):
                        eng.wait_ge(sems["sc"], cs)
                        eng.reduce_max(mxg[:, t * 16 + gg:t * 16 + gg + 1],
                                       gsl, AX.X,
                                       apply_absolute_value=True).then_inc(sems["vec"], 1)
                w1g_used[gg] = S["mm"].n
            w("vec", S["vec"].n, "vector")
            w("sc", S["sc"].n, "vector")
            for t in range(SUP):
                vlink(lambda t=t: eng.reduce_sum(stat[:, 0 + t:1 + t],
                                                 sqg[:, t * 16:(t + 1) * 16], AX.X))
                vlink(lambda t=t: eng.reduce_max(stat[:, 4 + t:5 + t],
                                                 mxg[:, t * 16:(t + 1) * 16], AX.X))
            c_hn = S["vec"].bump()
            if R("vector"):
                eng.tensor_copy(stat[:, 32:33], stat[:, 0:1]).then_inc(sems["vec"], 1)
            c_hsd = S["sc"].bump()
            if R("scalar"):
                eng.wait_ge(sems["vec"], c_hn)
                eng.activation(stat[:, 8:12], stat[:, 0:4],
                               AF.Sqrt).then_inc(sems["sc"], 1)
            w("sc", c_hsd, "vector")
            vlink(lambda: eng.tensor_scalar(stat[:, 8:12], stat[:, 8:12], 1e-12,
                                            None, ALU.max))
            vlink(lambda: eng.reciprocal(stat[:, 12:16], stat[:, 8:12]))
            vlink(lambda: eng.tensor_scalar(stat[:, 12:16], stat[:, 12:16],
                                            float(np.sqrt(INNER)), None, ALU.mult))
            vlink(lambda: eng.tensor_mul(stat[:, 20:24], stat[:, 4:8],
                                         stat[:, 12:16]))
            vlink(lambda: eng.tensor_scalar(stat[:, 20:24], stat[:, 20:24], EPS,
                                            None, ALU.max))
            vlink(lambda: eng.reciprocal(stat[:, 28:32], stat[:, 20:24]))
            vlink(lambda: eng.tensor_scalar(stat[:, 28:32], stat[:, 28:32],
                                            127.0, None, ALU.mult))
            vlink(lambda: eng.tensor_mul(stat[:, 28:32], stat[:, 28:32],
                                         stat[:, 12:16]))
            vlink(lambda: eng.tensor_scalar(stat[:, 24:28], stat[:, 20:24],
                                            0.5 / 127.0, None, ALU.mult))
            vlink(lambda: eng.tensor_scalar(stat[:, 24:28], stat[:, 24:28],
                                            P["c_wsc"][:, 3:4], None, ALU.mult))
            rnd = []
            for t in range(SUP):
                cv = S["vec"].bump()
                if R("vector"):
                    for ch in range(16):
                        gsl = g[:, t * INNER + ch * 512:t * INNER + (ch + 1) * 512]
                        eng.tensor_scalar(tmpa[:, 0:512], gsl,
                                          stat[:, 28 + t:29 + t], None, ALU.mult)
                        eng.tensor_scalar(tmpb[:, 0:512], tmpa[:, 0:512],
                                          C64, -C64, ALU.add, ALU.add)
                        eng.tensor_scalar(tmpa[:, 0:512], tmpa[:, 0:512],
                                          C32, -C32, ALU.add, ALU.add)
                        li = eng.scalar_tensor_tensor(
                            gsl, tmpa[:, 0:512], P["c_flag"][:, 0:1],
                            tmpb[:, 0:512], ALU.mult, ALU.add)
                    li.then_inc(sems["vec"], 1)
                rnd.append(cv)
            last_add = 0
            sts = []
            slab_gate = []
            for half in range(2):
                tA_, tB_ = t0 + 2 * half, t0 + 2 * half + 1
                xres = []
                for tt in range(2):
                    if half == 0:
                        w("vec", rnd[-1], "gpsimd")
                        if sup > 0:
                            w("qa0", prev["st"], "gpsimd")
                    else:
                        w("qa0", sts[-1], "gpsimd")
                    xres.append(dma("qg0", ot[:, tt * D:(tt + 1) * D],
                                    src_dram[(tA_ + tt) * 128:(tA_ + tt + 1) * 128, :]))
                for dmh in range(2):
                    tr_evs = []
                    mm2_end = None
                    for icb in range(32):
                        par2 = len(slab_gate) % 2
                        qn2 = "qw0" if par2 == 0 else "qw1"
                        if len(slab_gate) >= 2:
                            w("mm", slab_gate[-2], "sync")
                        cd = dma(qn2,
                                 w2s[:, par2 * 2 * 1024:(par2 + 1) * 2 * 1024].rearrange(
                                     "p (i c) -> p i c", c=1024),
                                 io["w2qT"][icb * 256:(icb + 1) * 256,
                                            dmh * 1024:(dmh + 1) * 1024].rearrange(
                                     "(i p) c -> p i c", p=128))
                        for ici in range(2):
                            ic = icb * 2 + ici
                            parh = ic % 2
                            evs2 = []
                            for tt in range(2):
                                t = tA_ + tt - t0
                                ntr = len(tr_evs)
                                slot = ntr % 2
                                c_tr = S["mm"].bump()
                                if R("tensor"):
                                    if half == 0 and dmh == 0 and ic == 0 and tt == 0:
                                        for r in rnd:
                                            eng.wait_ge(sems["vec"], r)
                                        eng.wait_ge(sems["sc"],
                                                    mm_evac[(15, SUP - 1)])
                                    if ntr >= 2:
                                        eng.wait_ge(sems["vec"], tr_evs[ntr - 2])
                                    eng.transpose(
                                        psB[:, slot * 512:slot * 512 + 128],
                                        g[:, t * INNER + ic * 128:
                                          t * INNER + (ic + 1) * 128],
                                        P["c_identf"][:]).then_inc(sems["mm"], 1)
                                c_ev = S["vec"].bump()
                                if R("vector"):
                                    eng.wait_ge(sems["mm"], c_tr)
                                    eng.tensor_copy(
                                        hqT[:, parh * 256 + tt * 128:
                                            parh * 256 + (tt + 1) * 128],
                                        psB[:, slot * 512:slot * 512 + 128]
                                    ).then_inc(sems["vec"], 1)
                                evs2.append(c_ev)
                                tr_evs.append(c_ev)
                            c_mm = S["mm"].bump()
                            if R("tensor"):
                                eng.wait_ge(sems["vec"], evs2[-1])
                                if ici == 0:
                                    eng.wait_ge(sems[qn2], cd)
                                if ic == 0:
                                    eng.wait_ge(sems["vec"], last_add)
                                for tt in range(2):
                                    for nn in range(2):
                                        li = eng.matmul(
                                            psA[:, (tt * 2 + nn) * 512:
                                                (tt * 2 + nn + 1) * 512],
                                            hqT[:, parh * 256 + tt * 128:
                                                parh * 256 + (tt + 1) * 128],
                                            w2s[:, par2 * 2 * 1024 + ici * 1024 +
                                                nn * 512:
                                                par2 * 2 * 1024 + ici * 1024 +
                                                (nn + 1) * 512],
                                            start=(ic == 0), stop=(ic == KI - 1))
                                li.then_inc(sems["mm"], 1)
                            mm2_end = S["mm"].n
                        slab_gate.append(S["mm"].n)
                    for tt in range(2):
                        t = tA_ + tt - t0
                        for nn in range(2):
                            c_add = S["vec"].bump()
                            if R("vector"):
                                eng.wait_ge(sems["mm"], mm2_end)
                                if dmh == 0 and tt == 0 and nn == 0:
                                    eng.wait_ge(sems["qg0"], xres[-1])
                                osl = ot[:, tt * D + dmh * 1024 + nn * 512:
                                         tt * D + dmh * 1024 + (nn + 1) * 512]
                                eng.scalar_tensor_tensor(
                                    osl, psA[:, (tt * 2 + nn) * 512:
                                             (tt * 2 + nn + 1) * 512],
                                    stat[:, 24 + t:25 + t], osl,
                                    ALU.mult, ALU.add).then_inc(sems["vec"], 1)
                            last_add = c_add
                w("vec", last_add, "scalar")
                for tt in range(2):
                    sts.append(dma("qa0",
                                   dst_dram[(tA_ + tt) * 128:(tA_ + tt + 1) * 128, :],
                                   ot[:, tt * D:(tt + 1) * D]))
            prev = {"v": S["vec"].n, "s": S["sc"].n, "st": sts[-1],
                    "stt": S["vec"].n, "ffnout": last_add}
        for who in ["sync", "vector", "tensor", "scalar", "gpsimd"]:
            w("qa0", S["qa0"].n, who)

    with contextlib.ExitStack() as stk:
        sb, pt = prog.nc.sbuf_tensor, prog.nc.psum_tensor
        prog.fb = (
            stk.enter_context(sb(f"g{tag}", [128, SUP * INNER], F32)),
            stk.enter_context(sb(f"xq{tag}", [128, 256], BF16)),
            stk.enter_context(sb(f"xqT{tag}", [128, KD * 512], BF16)),
            stk.enter_context(sb(f"w1g{tag}", [128, 2 * KD * 512], BF16)),
            stk.enter_context(sb(f"w2s{tag}", [128, 2 * 2 * 1024], BF16)),
            stk.enter_context(sb(f"hqT{tag}", [128, 2 * 256], BF16)),
            stk.enter_context(sb(f"tA{tag}", [128, 512], F32)),
            stk.enter_context(sb(f"tB{tag}", [128, 512], F32)),
            stk.enter_context(sb(f"sg{tag}", [128, 168], F32)),
            stk.enter_context(sb(f"ot{tag}", [128, 2 * D], F32)),
            stk.enter_context(pt(f"pA{tag}", [128, SUP * 512], F32)),
            stk.enter_context(pt(f"pB{tag}", [128, 1024], F32)),
            stk.enter_context(pt(f"pC{tag}", [128, 2048], BF16)),
        )
        prog.run_block(emit)
    del prog.fb


def b4_qkv(prog, src_name):
    nc, io, P, sems = prog.nc, prog.io, prog.P, prog.sems
    src_dram = io[src_name]

    def emit(role, eng, ctx):
        R, S, vlink, dma, w = ctx["R"], ctx["S"], ctx["vlink"], ctx["dma"], ctx["w"]
        (xt, xnb, xnT, wb, stage, sgrid, psQ, psK, psV, psT) = prog.b4
        if B4_SERIAL:
            _dma0 = ctx["dma"]
            def dma(q, dst, srcap):
                for sname in ["vec", "sc", "mm"]:
                    w(sname, S[sname].n, {"qs0": "sync", "qs1": "sync",
                                          "qa0": "scalar", "qg0": "gpsimd",
                                          "qk0": "sync", "qk1": "sync"}[q])
                return _dma0(q, dst, srcap)
            _sb = Sem.bump
            def serial_guard(which):
                for sname in ["vec", "sc", "mm"]:
                    if sname != which:
                        w(sname, S[sname].n, {"vec": "vector", "sc": "scalar",
                                              "mm": "tensor"}[which])
            class _SemWrap:
                pass
            ctx["serial_guard"] = serial_guard
        else:
            ctx["serial_guard"] = lambda which: None
        serial_guard = ctx["serial_guard"]
        stat = sgrid[:, 0:40]
        sqg = sgrid[:, 40:104]
        GT = 8
        prevv = {"v": 0, "s": 0}
        for gp in range(2):
            t0 = gp * GT
            stat_done = []
            for t in range(GT):
                if t >= 1:
                    w("vec", stat_done[t - 1][0], "sync")
                    w("sc", stat_done[t - 1][1], "sync")
                elif gp > 0:
                    w("vec", prevv["v"], "sync")
                    w("sc", prevv["s"], "sync")
                cd = dma("qs0", xt[:, 0:D],
                         src_dram[(t0 + t) * 128:(t0 + t + 1) * 128, :])
                serial_guard("vec")
                cv = S["vec"].bump()
                if R("vector"):
                    eng.wait_ge(sems["qs0"], cd)
                    eng.reduce_sum(stat[:, t:t + 1], xt[:, 0:D],
                                   AX.X).then_inc(sems["vec"], 1)
                cs = S["sc"].bump()
                if R("scalar"):
                    eng.wait_ge(sems["qs0"], cd)
                    for ch in range(4):
                        li = eng.activation(
                            stage[:, 0:512],
                            xt[:, ch * 512:(ch + 1) * 512], AF.Square,
                            accum_out=sqg[:, t * 4 + ch:t * 4 + ch + 1])
                    li.then_inc(sems["sc"], 1)
                stat_done.append((cv, cs))
            w("vec", S["vec"].n, "vector")
            w("sc", S["sc"].n, "vector")
            for t in range(GT):
                vlink(lambda t=t: eng.reduce_sum(stat[:, 16 + t:17 + t],
                                                 sqg[:, t * 4:t * 4 + 4], AX.X))
            vlink(lambda: eng.tensor_scalar(stat[:, 0:8], stat[:, 0:8],
                                            1.0 / D, None, ALU.mult))
            vlink(lambda: eng.tensor_tensor(stat[:, 24:32], stat[:, 0:8],
                                            stat[:, 0:8], ALU.mult))
            vlink(lambda: eng.scalar_tensor_tensor(
                stat[:, 24:32], stat[:, 16:24], 1.0 / D, stat[:, 24:32],
                ALU.mult, ALU.subtract))
            vlink(lambda: eng.tensor_scalar(stat[:, 24:32], stat[:, 24:32],
                                            EPS, None, ALU.add))
            c_var = S["vec"].bump()
            if R("vector"):
                eng.tensor_copy(stat[:, 32:33], stat[:, 24:25]).then_inc(sems["vec"], 1)
            c_sd = S["sc"].bump()
            if R("scalar"):
                eng.wait_ge(sems["vec"], c_var)
                eng.activation(stat[:, 24:32], stat[:, 24:32],
                               AF.Sqrt).then_inc(sems["sc"], 1)
            w("sc", c_sd, "vector")
            vlink(lambda: eng.reciprocal(stat[:, 8:16], stat[:, 24:32]))
            xev = []
            qprev = 0
            for t in range(GT):
                if t >= 1:
                    w("vec", qprev, "sync")
                else:
                    w("vec", stat_done[-1][0], "sync")
                    w("sc", stat_done[-1][1], "sync")
                cd = dma("qs0", xt[:, 0:D],
                         src_dram[(t0 + t) * 128:(t0 + t + 1) * 128, :])
                serial_guard("vec")
                cv = S["vec"].bump()
                if R("vector"):
                    eng.wait_ge(sems["qs0"], cd)
                    eng.tensor_scalar(xnb[:, 0:D], xt[:, 0:D], stat[:, t:t + 1],
                                      stat[:, 8 + t:9 + t],
                                      ALU.subtract, ALU.mult).then_inc(sems["vec"], 1)
                qprev = cv
                for k in range(KD):
                    nx = len(xev)
                    slot = nx % 2
                    serial_guard("mm")
                    c_tr = S["mm"].bump()
                    if R("tensor"):
                        eng.wait_ge(sems["vec"], cv)
                        if nx >= 2:
                            eng.wait_ge(sems["vec"], xev[nx - 2])
                        eng.transpose(psT[:, slot * 512:slot * 512 + 128],
                                      xnb[:, k * 128:(k + 1) * 128],
                                      P["c_identf"][:]).then_inc(sems["mm"], 1)
                    serial_guard("vec")
                    c_ev = S["vec"].bump()
                    if R("vector"):
                        eng.wait_ge(sems["mm"], c_tr)
                        eng.tensor_copy(
                            xnT[:, k * 1024 + t * 128:k * 1024 + (t + 1) * 128],
                            psT[:, slot * 512:slot * 512 + 128]).then_inc(sems["vec"], 1)
                    xev.append(c_ev)
            xnT_done = xev[-1]
            if DBG_B4:
                w("vec", xnT_done, "gpsimd")
                dma("qg0", io["dbgxnT"][gp * 128:(gp + 1) * 128, :],
                    xnT[:, 0:KD * 1024])
                dma("qg0", io["dbgstat"][gp * 128:(gp + 1) * 128, :],
                    sgrid[:, 0:104])
            wprev = {}
            stq_gate = []
            for h in range(H):
                parw = h % 2
                if h >= 2:
                    w("mm", wprev[(h - 2, "mm")], "sync")
                qwn = "qw0" if parw == 0 else "qw1"
                cds = []
                for wi, wn in enumerate(["wq", "wk", "wv"]):
                    cds.append(dma(
                        qwn,
                        wb[:, parw * 3 * KD * 128 + wi * KD * 128:
                           parw * 3 * KD * 128 + (wi + 1) * KD * 128].rearrange(
                            "p (k c) -> p k c", c=128),
                        io[wn][h * D:(h + 1) * D, :].rearrange(
                            "(k p) c -> p k c", p=128)))
                for wi, (ps_, dstd, bT) in enumerate(
                        [(psQ, io["qsc"], "c_bqT"), (psK, io["ksc"], "c_bkT")]):
                    serial_guard("mm")
                    c_mm = S["mm"].bump()
                    if R("tensor"):
                        eng.wait_ge(sems[qwn], cds[-1])
                        if h == 0:
                            eng.wait_ge(sems["vec"], xnT_done)
                        if h >= 1:
                            eng.wait_ge(sems["sc"], wprev[(h - 1, "sc")])
                        for nn in range(2):
                            for k in range(KD):
                                li = eng.matmul(
                                    ps_[:, nn * 512:(nn + 1) * 512],
                                    wb[:, parw * 3 * KD * 128 + wi * KD * 128 +
                                       k * 128:parw * 3 * KD * 128 +
                                       wi * KD * 128 + (k + 1) * 128],
                                    xnT[:, k * 1024 + nn * 512:
                                        k * 1024 + (nn + 1) * 512],
                                    start=(k == 0), stop=(k == KD - 1))
                        li.then_inc(sems["mm"], 1)
                    serial_guard("sc")
                    cs = S["sc"].bump()
                    if R("scalar"):
                        eng.wait_ge(sems["mm"], c_mm)
                        if len(stq_gate) >= 3:
                            eng.wait_ge(sems[stq_gate[-3][0]], stq_gate[-3][1])
                        eng.activation(stage[:, wi * 1024:(wi + 1) * 1024],
                                       ps_[:, 0:1024], AF.Identity,
                                       bias=P[bT][:, h:h + 1]).then_inc(sems["sc"], 1)
                    w("sc", cs, "scalar")
                    qan = ["qa0", "qa1"][wi]
                    stq_gate.append((qan, dma(qan, dstd[h * 128:(h + 1) * 128,
                                              gp * 1024:(gp + 1) * 1024],
                                              stage[:, wi * 1024:(wi + 1) * 1024])))
                vev = []
                for t in range(GT):
                    slotv = t % 2
                    serial_guard("mm")
                    c_mm = S["mm"].bump()
                    if R("tensor"):
                        if t >= 2:
                            eng.wait_ge(sems["sc"], vev[t - 2])
                        if t <= 1 and h >= 1:
                            eng.wait_ge(sems["sc"], wprev[(h - 1, "sc")])
                        for k in range(KD):
                            li = eng.matmul(
                                psV[:, slotv * 512:slotv * 512 + 128],
                                xnT[:, k * 1024 + t * 128:k * 1024 + (t + 1) * 128],
                                wb[:, parw * 3 * KD * 128 + 2 * KD * 128 +
                                   k * 128:parw * 3 * KD * 128 +
                                   2 * KD * 128 + (k + 1) * 128],
                                start=(k == 0), stop=(k == KD - 1))
                        li.then_inc(sems["mm"], 1)
                    serial_guard("sc")
                    cs = S["sc"].bump()
                    if R("scalar"):
                        eng.wait_ge(sems["mm"], c_mm)
                        if t == 0 and len(stq_gate) >= 3:
                            eng.wait_ge(sems[stq_gate[-3][0]], stq_gate[-3][1])
                        eng.activation(stage[:, 2048 + t * 128:2048 + (t + 1) * 128],
                                       psV[:, slotv * 512:slotv * 512 + 128],
                                       AF.Identity).then_inc(sems["sc"], 1)
                    vev.append(cs)
                w("sc", vev[-1], "scalar")
                stq_gate.append(("qa2", dma(
                    "qa2", io["vsc"][h * T + gp * 1024:h * T + (gp + 1) * 1024, :
                                     ].rearrange("(t p) c -> p t c", p=128),
                    stage[:, 2048:2048 + GT * 128].rearrange(
                        "p (t c) -> p t c", c=128))))
                wprev[(h, "mm")] = S["mm"].n
                wprev[(h, "sc")] = S["sc"].n
            prevv = {"v": S["vec"].n, "s": S["sc"].n}
        for who in ["sync", "vector", "tensor", "scalar", "gpsimd"]:
            w("qa0", S["qa0"].n, who)
            w("qa1", S["qa1"].n, who)
            w("qa2", S["qa2"].n, who)

    with contextlib.ExitStack() as stk:
        sb, pt = prog.nc.sbuf_tensor, prog.nc.psum_tensor
        prog.b4 = (
            stk.enter_context(sb("b4xt", [128, D], F32)),
            stk.enter_context(sb("b4xn", [128, D], F32)),
            stk.enter_context(sb("b4xnT", [128, KD * 1024], F32)),
            stk.enter_context(sb("b4wb", [128, 2 * 3 * KD * 128], F32)),
            stk.enter_context(sb("b4stage", [128, 3072], F32)),
            stk.enter_context(sb("b4sg", [128, 40 + 64], F32)),
            stk.enter_context(pt("b4psQ", [128, 1024], F32)),
            stk.enter_context(pt("b4psK", [128, 1024], F32)),
            stk.enter_context(pt("b4psV", [128, 1024], F32)),
            stk.enter_context(pt("b4psT", [128, 1024], F32)),
        )
        prog.run_block(emit)
    del prog.b4


def b5_attn(prog, src_name, dst_name):
    nc, io, P, sems = prog.nc, prog.io, prog.P, prog.sems
    src_dram, dst_dram = io[src_name], io[dst_name]

    def emit(role, eng, ctx):
        R, S, vlink, dma, w = ctx["R"], ctx["S"], ctx["vlink"], ctx["dma"], ctx["w"]
        (kh, vh, qh, pbuf, ptb, ystr, dstat, diag, caus, borep,
         xsl, ost, wos, psA, psB, psD, psE) = prog.b5
        c_c = dma("qs0", caus[:], io["caus"][:, :])
        c_b = dma("qs0", borep[0:1, 0:D], io["bo"][:, :])
        c_bc = S["mm"].bump()
        if R("tensor"):
            eng.wait_ge(sems["qs0"], max(c_c, c_b))
            for nn in range(4):
                li = eng.matmul(psA[:, nn * 512:(nn + 1) * 512],
                                P["c_ones"][0:1, 0:128],
                                borep[0:1, nn * 512:(nn + 1) * 512],
                                start=True, stop=True)
            li.then_inc(sems["mm"], 1)
        c_bv = S["vec"].bump()
        if R("vector"):
            eng.wait_ge(sems["mm"], c_bc)
            eng.tensor_copy(borep[:], psA[:, 0:2048]).then_inc(sems["vec"], 1)
        hk_prev = {}
        last_wo_add = c_bv
        ost_gate = []
        xsl_gate = []
        for sp in range(NSTRIPE):
            q0 = sp * (STRIPE // 128)
            for h in range(H):
                hg = sp * H + h
                parh = hg % 2
                if hg >= 2:
                    w("mm", hk_prev[(hg - 2, "mm")], "sync")
                    w("sc", hk_prev[(hg - 2, "sc")], "sync")
                qk = "qk0" if parh == 0 else "qk1"
                dma(qk, kh[:, parh * T:(parh + 1) * T],
                    io["ksc"][h * 128:(h + 1) * 128, :])
                dma(qk, vh[:, parh * NT * 128:(parh + 1) * NT * 128].rearrange(
                        "p (t c) -> p t c", c=128),
                    io["vsc"][h * T:(h + 1) * T, :].rearrange(
                        "(t p) c -> p t c", p=128))
                cdq = dma(qk, qh[:, parh * STRIPE:(parh + 1) * STRIPE],
                          io["qsc"][h * 128:(h + 1) * 128,
                                    sp * STRIPE:(sp + 1) * STRIPE])
                first_of_head = True
                for qi_l in range(STRIPE // 128):
                    qi = q0 + qi_l
                    nkeys = (qi + 1) * 128
                    nck = (nkeys + 511) // 512
                    c_s = S["mm"].bump()
                    if R("tensor"):
                        if first_of_head:
                            eng.wait_ge(sems[qk], cdq)
                            if sp == 0 and h == 0:
                                eng.wait_ge(sems["vec"], c_bv)
                        eng.wait_ge(sems["sc"], S["sc"].n)
                        if first_of_head and h == 0:
                            eng.wait_ge(sems["vec"], last_wo_add)
                        for ck in range(nck):
                            a = ck * 512
                            b = min(nkeys, a + 512)
                            li = eng.matmul(
                                psA[:, a:b],
                                qh[:, parh * STRIPE + qi_l * 128:
                                   parh * STRIPE + (qi_l + 1) * 128],
                                kh[:, parh * T + a:parh * T + b],
                                start=True, stop=True)
                        li.then_inc(sems["mm"], 1)
                    first_of_head = False
                    c_mask = S["vec"].bump()
                    if R("vector"):
                        eng.wait_ge(sems["mm"], c_s)
                        eng.tensor_add(psA[:, qi * 128:nkeys],
                                       psA[:, qi * 128:nkeys],
                                       caus[:]).then_inc(sems["vec"], 1)
                    vlink(lambda: eng.reduce_max(dstat[:, 0:1],
                                                 psA[:, 0:nkeys], AX.X))
                    vlink(lambda: eng.tensor_scalar(dstat[:, 1:2], dstat[:, 0:1],
                                                    -ATT_SCALE, None, ALU.mult))
                    c_negm = S["vch"].n
                    cs_exp = S["sc"].bump()
                    if R("scalar"):
                        eng.wait_ge(sems["vch"], c_negm)
                        eng.activation(pbuf[:, 0:nkeys], psA[:, 0:nkeys],
                                       AF.Exp, bias=dstat[:, 1:2],
                                       scale=ATT_SCALE,
                                       accum_out=dstat[:, 2:3]).then_inc(sems["sc"], 1)
                    w("sc", cs_exp, "vector")
                    vlink(lambda: eng.reciprocal(dstat[:, 3:4], dstat[:, 2:3]))
                    vlink(lambda: eng.tensor_scalar(pbuf[:, 0:nkeys],
                                                    pbuf[:, 0:nkeys],
                                                    dstat[:, 3:4], None, ALU.mult))
                    c_diag = S["vch"].n
                    pev = []
                    for kc in range(qi + 1):
                        slot = kc % 2
                        c_tr = S["mm"].bump()
                        if R("tensor"):
                            if kc == 0:
                                eng.wait_ge(sems["vch"], c_diag)
                            if kc >= 2:
                                eng.wait_ge(sems["vec"], pev[kc - 2])
                            eng.transpose(psB[:, slot * 512:slot * 512 + 128],
                                          pbuf[:, kc * 128:(kc + 1) * 128],
                                          P["c_identf"][:]).then_inc(sems["mm"], 1)
                        c_ev = S["vec"].bump()
                        if R("vector"):
                            eng.wait_ge(sems["mm"], c_tr)
                            eng.tensor_copy(ptb[:, kc * 128:(kc + 1) * 128],
                                            psB[:, slot * 512:slot * 512 + 128]
                                            ).then_inc(sems["vec"], 1)
                        pev.append(c_ev)
                    c_av = S["mm"].bump()
                    if R("tensor"):
                        eng.wait_ge(sems["vec"], pev[-1])
                        for kc in range(qi + 1):
                            li = eng.matmul(
                                psD[:, 0:128],
                                vh[:, parh * NT * 128 + kc * 128:
                                   parh * NT * 128 + (kc + 1) * 128],
                                ptb[:, kc * 128:(kc + 1) * 128],
                                start=(kc == 0), stop=(kc == qi))
                        li.then_inc(sems["mm"], 1)
                    cs_y = S["sc"].bump()
                    if R("scalar"):
                        eng.wait_ge(sems["mm"], c_av)
                        eng.activation(ystr[:, h * STRIPE + qi_l * 128:
                                            h * STRIPE + (qi_l + 1) * 128],
                                       psD[:, 0:128], AF.Identity,
                                       bias=P["c_bvT"][:, h:h + 1]).then_inc(sems["sc"], 1)
                hk_prev[(hg, "mm")] = S["mm"].n
                hk_prev[(hg, "sc")] = S["sc"].n
            c_heads = S["sc"].n
            wo_banks = [psA[:, i * 512:(i + 1) * 512] for i in range(4)] + \
                       [psB[:, i * 512:(i + 1) * 512] for i in range(2)] + \
                       [psD[:, 0:512], psE[:, 0:512]]
            wo_gate = []
            for nn in range(4):
                wo_last = None
                for k in range(KD):
                    j = len(wo_gate)
                    parw = j % 2
                    qwn = "qw0" if parw == 0 else "qw1"
                    if j >= 2:
                        w("mm", wo_gate[j - 2], "sync")
                    cd = dma(qwn, wos[:, parw * 512:(parw + 1) * 512],
                             io["wo"][k * 128:(k + 1) * 128,
                                      nn * 512:(nn + 1) * 512])
                    c_mm = S["mm"].bump()
                    if R("tensor"):
                        eng.wait_ge(sems[qwn], cd)
                        if k == 0:
                            if nn == 0:
                                eng.wait_ge(sems["sc"], c_heads)
                            eng.wait_ge(sems["vec"], last_wo_add)
                        for qb in range(8):
                            li = eng.matmul(
                                wo_banks[qb],
                                ystr[:, k * STRIPE + qb * 128:
                                     k * STRIPE + (qb + 1) * 128],
                                wos[:, parw * 512:(parw + 1) * 512],
                                start=(k == 0), stop=(k == KD - 1))
                        li.then_inc(sems["mm"], 1)
                    wo_gate.append(S["mm"].n)
                wo_last = wo_gate[-1]
                for qb in range(8):
                    parx = len(ost_gate) % 2
                    qgn = "qg0" if parx == 0 else "qg1"
                    qan = "qa0" if parx == 0 else "qa1"
                    if len(xsl_gate) >= 2:
                        w("vec", xsl_gate[-2], "gpsimd")
                    cdx = dma(qgn, xsl[:, parx * 512:(parx + 1) * 512],
                              src_dram[(q0 + qb) * 128:(q0 + qb + 1) * 128,
                                       nn * 512:(nn + 1) * 512])
                    c_a1 = S["vec"].bump(2)
                    if R("vector"):
                        eng.wait_ge(sems["mm"], wo_last)
                        eng.wait_ge(sems[qgn], cdx)
                        if len(ost_gate) >= 2:
                            eng.wait_ge(sems[qan], ost_gate[-2])
                        eng.tensor_add(ost[:, parx * 512:(parx + 1) * 512],
                                       wo_banks[qb],
                                       xsl[:, parx * 512:(parx + 1) * 512])
                        eng.tensor_add(ost[:, parx * 512:(parx + 1) * 512],
                                       ost[:, parx * 512:(parx + 1) * 512],
                                       borep[:, nn * 512:(nn + 1) * 512]
                                       ).then_inc(sems["vec"], 2)
                    last_wo_add = c_a1
                    xsl_gate.append(c_a1)
                    w("vec", c_a1, "scalar")
                    cst = dma(qan,
                              dst_dram[(q0 + qb) * 128:(q0 + qb + 1) * 128,
                                       nn * 512:(nn + 1) * 512],
                              ost[:, parx * 512:(parx + 1) * 512])
                    ost_gate.append(cst)
        for who in ["sync", "vector", "tensor", "scalar", "gpsimd"]:
            w("qa0", S["qa0"].n, who)
            w("qa1", S["qa1"].n, who)

    with contextlib.ExitStack() as stk:
        sb, pt = prog.nc.sbuf_tensor, prog.nc.psum_tensor
        prog.b5 = (
            stk.enter_context(sb("b5kh", [128, 2 * T], F32)),
            stk.enter_context(sb("b5vh", [128, 2 * NT * 128], F32)),
            stk.enter_context(sb("b5qh", [128, 2 * STRIPE], F32)),
            stk.enter_context(sb("b5p", [128, T], F32)),
            stk.enter_context(sb("b5pt", [128, T], F32)),
            stk.enter_context(sb("b5y", [128, H * STRIPE], F32)),
            stk.enter_context(sb("b5ds", [128, 8], F32)),
            stk.enter_context(sb("b5diag", [128, 128], F32)),
            stk.enter_context(sb("b5caus", [128, 128], F32)),
            stk.enter_context(sb("b5bo", [128, D], F32)),
            stk.enter_context(sb("b5xs", [128, 2 * 512], F32)),
            stk.enter_context(sb("b5os", [128, 2 * 512], F32)),
            stk.enter_context(sb("b5wo", [128, 2 * 512], F32)),
            stk.enter_context(pt("b5psA", [128, 2048], F32)),
            stk.enter_context(pt("b5psB", [128, 1024], F32)),
            stk.enter_context(pt("b5psD", [128, 512], F32)),
            stk.enter_context(pt("b5psE", [128, 512], F32)),
        )
        prog.run_block(emit)
    del prog.b5


_CACHE = {}


def _build():
    if "nc" in _CACHE:
        return _CACHE["nc"]
    prog = Prog()
    b0_consts(prog)
    b1_wscale(prog)
    b2_wquant(prog)
    if STAGE == 1:
        ffn_pass(prog, "x", "y", "f1")
    elif STAGE == 2:
        ffn_pass(prog, "x", "xr1", "f1")
        b4_qkv(prog, "xr1")
        b5_attn(prog, "xr1", "y")
    else:
        ffn_pass(prog, "x", "xr1", "f1")
        b4_qkv(prog, "xr1")
        b5_attn(prog, "xr1", "xr2")
        ffn_pass(prog, "xr2", "y", "f3")
    prog.es.close()
    _CACHE["nc"] = prog.nc
    return prog.nc


def kernel(**inputs):
    nc = _build()
    x = np.ascontiguousarray(np.asarray(inputs["L0_attn_x"], np.float32))
    sh = dict(
        w1=np.ascontiguousarray(np.asarray(inputs["L0_ffn_w1"], np.float32)),
        w2=np.ascontiguousarray(np.asarray(inputs["L0_ffn_w2"], np.float32)),
        wq=np.ascontiguousarray(np.asarray(inputs["L0_attn_Wq"], np.float32).reshape(H * D, HD)),
        wk=np.ascontiguousarray(np.asarray(inputs["L0_attn_Wk"], np.float32).reshape(H * D, HD)),
        wv=np.ascontiguousarray(np.asarray(inputs["L0_attn_Wv"], np.float32).reshape(H * D, HD)),
        wo=np.ascontiguousarray(np.asarray(inputs["L0_attn_Wo"], np.float32)),
        bq=np.ascontiguousarray(np.asarray(inputs["L0_attn_bq"], np.float32)),
        bk=np.ascontiguousarray(np.asarray(inputs["L0_attn_bk"], np.float32)),
        bv=np.ascontiguousarray(np.asarray(inputs["L0_attn_bv"], np.float32)),
        bo=np.ascontiguousarray(np.asarray(inputs["L0_attn_bo"], np.float32).reshape(1, D)),
        identf=np.eye(128, dtype=np.float32),
        identb=np.eye(128).astype(ml_dtypes.bfloat16),
        ones=np.ones((128, 128), np.float32),
        caus=np.ascontiguousarray(np.where(
            np.arange(128)[:, None] >= np.arange(128)[None, :],
            np.float32(0), np.finfo(np.float32).min).astype(np.float32)),
    )
    in_maps = [dict(sh, x=np.ascontiguousarray(x[c])) for c in range(N_CORES)]
    res = run_bass_kernel_spmd(nc, in_maps, core_ids=list(range(N_CORES)))
    return np.stack([r["y"] for r in res.results], axis=0)


# version 05
# speedup vs baseline: 1.0342x; 1.0342x over previous
import contextlib
import numpy as np
import ml_dtypes
import concourse.bass as bass
import concourse.mybir as mybir
from concourse.bass_utils import run_bass_kernel_spmd

F32 = mybir.dt.float32
BF16 = mybir.dt.bfloat16
AF = mybir.ActivationFunctionType
ALU = mybir.AluOpType
AX = mybir.AxisListType

T = 2048
D = 2048
INNER = 8192
H = 16
HD = 128
NT = T // 128
KD = D // 128
KI = INNER // 128
SUP = 4
NSUP = NT // SUP
STRIPE = 1024
NSTRIPE = T // STRIPE
EPS = 1e-5
C32 = float(1.5 * 2**23)
C64 = float(1.5 * 2**52)
ATT_SCALE = float(HD) ** -0.5
N_CORES = 8

STAGE = 3
DBG_SPILL = False
B4_SERIAL = False
DBG_B4 = False


class Sem:
    def __init__(self, h):
        self.h = h
        self.n = 0

    def bump(self, k=1):
        self.n += k
        return self.n


class Prog:
    def __init__(self):
        nc = bass.Bass("TRN2", target_bir_lowering=False,
                       detect_race_conditions=False)
        self.nc = nc
        self.es = contextlib.ExitStack()
        io = {}
        io["x"] = nc.dram_tensor("x", [T, D], F32, kind="ExternalInput")
        io["w1"] = nc.dram_tensor("w1", [INNER, D], F32, kind="ExternalInput")
        io["w2"] = nc.dram_tensor("w2", [D, INNER], F32, kind="ExternalInput")
        io["wq"] = nc.dram_tensor("wq", [H * D, HD], F32, kind="ExternalInput")
        io["wk"] = nc.dram_tensor("wk", [H * D, HD], F32, kind="ExternalInput")
        io["wv"] = nc.dram_tensor("wv", [H * D, HD], F32, kind="ExternalInput")
        io["wo"] = nc.dram_tensor("wo", [D, D], F32, kind="ExternalInput")
        io["bq"] = nc.dram_tensor("bq", [H, HD], F32, kind="ExternalInput")
        io["bk"] = nc.dram_tensor("bk", [H, HD], F32, kind="ExternalInput")
        io["bv"] = nc.dram_tensor("bv", [H, HD], F32, kind="ExternalInput")
        io["bo"] = nc.dram_tensor("bo", [1, D], F32, kind="ExternalInput")
        io["identf"] = nc.dram_tensor("identf", [128, 128], F32, kind="ExternalInput")
        io["identb"] = nc.dram_tensor("identb", [128, 128], BF16, kind="ExternalInput")
        io["ones"] = nc.dram_tensor("ones", [128, 128], F32, kind="ExternalInput")
        io["caus"] = nc.dram_tensor("caus", [128, 128], F32, kind="ExternalInput")
        io["y"] = nc.dram_tensor("y", [T, D], F32, kind="ExternalOutput")
        io["w1qT"] = nc.dram_tensor("w1qT", [D, INNER], BF16)
        io["w2qT"] = nc.dram_tensor("w2qT", [INNER, D], BF16)
        io["xr1"] = nc.dram_tensor("xr1", [T, D], F32,
                                   kind="ExternalOutput" if DBG_B4 else "Internal")
        io["xr2"] = nc.dram_tensor("xr2", [T, D], F32)
        kk = "ExternalOutput" if DBG_SPILL else "Internal"
        if DBG_B4:
            io["dbgxnT"] = nc.dram_tensor("dbgxnT", [2 * 128, KD * 1024], F32,
                                          kind="ExternalOutput")
            io["dbgstat"] = nc.dram_tensor("dbgstat", [2 * 128, 104], F32,
                                           kind="ExternalOutput")
        io["qsc"] = nc.dram_tensor("qsc", [H * HD, T], F32, kind=kk)
        io["ksc"] = nc.dram_tensor("ksc", [H * HD, T], F32, kind=kk)
        io["vsc"] = nc.dram_tensor("vsc", [H * T, HD], F32, kind=kk)
        self.io = io

        self.sems = {n: self.es.enter_context(nc.semaphore(n)) for n in
                     ["qs0", "qs1", "qw0", "qw1", "qa0", "qa1", "qa2",
                      "qg0", "qg1", "qk0", "qk1", "vec", "vch", "mm", "sc"]}
        self.counts = {k: 0 for k in self.sems}

        P = {}
        def psb(name, shape, dtype=F32):
            P[name] = self.es.enter_context(nc.sbuf_tensor(name, shape, dtype))
        psb("c_identf", [128, 128])
        psb("c_identb", [128, 128], BF16)
        psb("c_ones", [128, 128])
        psb("c_flag", [128, 1])
        psb("c_wsc", [128, 4])
        psb("c_bqT", [128, H])
        psb("c_bkT", [128, H])
        psb("c_bvT", [128, H])
        psb("c_catt", [128, 1])
        nc.const_aps.register(F32, ATT_SCALE, P["c_catt"][:, 0:1])
        self.P = P

    def run_block(self, emit):
        final = {}
        with self.nc.Block() as block:
            for role, reg in [("sync", block.sync), ("gpsimd", block.gpsimd),
                              ("vector", block.vector), ("tensor", block.tensor),
                              ("scalar", block.scalar)]:
                def f(eng, role=role):
                    ctx = self._ctx(role, eng)
                    emit(role, eng, ctx)
                    final.clear()
                    final.update({k: s.n for k, s in ctx["S"].items()})
                reg(f)
        self.counts = dict(final)

    def _ctx(self, role, eng):
        sems = self.sems
        S = {k: Sem(h) for k, h in sems.items()}
        for k, v in self.counts.items():
            S[k].n = v
        ctx = {"S": S}

        def R(r):
            return role == r
        ctx["R"] = R

        def vlink(mk):
            c = S["vch"].bump()
            if R("vector"):
                mk().then_inc(sems["vch"], 1)
                eng.wait_ge(sems["vch"], c)
        ctx["vlink"] = vlink

        QOWNER = {"qs0": "sync", "qs1": "sync", "qw0": "sync", "qw1": "sync",
                  "qa0": "scalar", "qa1": "scalar", "qa2": "scalar",
                  "qg0": "gpsimd", "qg1": "gpsimd",
                  "qk0": "sync", "qk1": "sync"}

        def dma(q, dst, src):
            c = S[q].bump(16)
            if R(QOWNER[q]):
                eng.dma_start(dst, src).then_inc(sems[q], 16)
            return c
        ctx["dma"] = dma

        def w(semname, val, who):
            if R(who):
                eng.wait_ge(sems[semname], val)
        ctx["w"] = w
        return ctx



def b0_consts(prog):
    nc, io, P, sems = prog.nc, prog.io, prog.P, prog.sems

    def emit(role, eng, ctx):
        R, S, vlink, dma, w = ctx["R"], ctx["S"], ctx["vlink"], ctx["dma"], ctx["w"]
        btmp, bps = prog.b0
        for name in ["identf", "identb", "ones"]:
            dma("qs0", P["c_" + name][:], io[name][:, :])
        for i, nm in enumerate(["bq", "bk", "bv"]):
            dma("qs0", btmp[0:H, i * 128:(i + 1) * 128], io[nm][:, :])
        c_all = S["qs0"].n
        w("qs0", c_all, "vector")
        vlink(lambda: eng.tensor_scalar(P["c_flag"][:], P["c_ones"][:, 0:1],
                                        2.0**-30, -1.0, ALU.add, ALU.add))
        vlink(lambda: eng.tensor_scalar(P["c_flag"][:], P["c_flag"][:],
                                        -(2.0**30), 1.0, ALU.mult, ALU.add))
        vlink(lambda: eng.tensor_scalar(P["c_catt"][:], P["c_ones"][:, 0:1],
                                        ATT_SCALE, None, ALU.mult))
        c_tr = S["mm"].bump()
        if R("tensor"):
            eng.wait_ge(sems["qs0"], c_all)
            for i in range(3):
                li = eng.transpose(bps[:, i * 128:i * 128 + H],
                                   btmp[0:H, i * 128:(i + 1) * 128],
                                   P["c_identf"][0:H, 0:H])
            li.then_inc(sems["mm"], 1)
        c_ev = S["vec"].bump()
        if R("vector"):
            eng.wait_ge(sems["mm"], c_tr)
            for i, nm in enumerate(["c_bqT", "c_bkT", "c_bvT"]):
                li = eng.tensor_copy(P[nm][:, 0:H], bps[:, i * 128:i * 128 + H])
            li.then_inc(sems["vec"], 1)
        for who in ["sync", "scalar", "tensor", "gpsimd"]:
            w("vec", c_ev, who)

    with nc.sbuf_tensor("b0tmp", [128, 3 * 128], F32) as btmp, \
         nc.psum_tensor("b0ps", [128, 512], F32) as bps:
        prog.b0 = (btmp, bps)
        prog.run_block(emit)
    del prog.b0


def b1_wscale(prog):
    nc, io, P, sems = prog.nc, prog.io, prog.P, prog.sems

    def emit(role, eng, ctx):
        R, S, vlink, dma, w = ctx["R"], ctx["S"], ctx["vlink"], ctx["dma"], ctx["w"]
        blk, acc, tot, ps = prog.b1
        col = 0
        vred = []
        for wname, nb, ncol in [("w1", INNER // 128, D), ("w2", D // 128, INNER)]:
            for b in range(nb):
                par = col % 2
                qn = "qw0" if par == 0 else "qw1"
                if col >= 2:
                    w("vec", vred[col - 2], "sync")
                cd = dma(qn, blk[:, par * INNER:par * INNER + ncol],
                         io[wname][b * 128:(b + 1) * 128, :])
                cv = S["vec"].bump()
                if R("vector"):
                    eng.wait_ge(sems[qn], cd)
                    eng.tensor_reduce(acc[:, col:col + 1],
                                      blk[:, par * INNER:par * INNER + ncol],
                                      AX.X, ALU.add,
                                      apply_absolute_value=True).then_inc(sems["vec"], 1)
                vred.append(cv)
                col += 1
        w("vec", S["vec"].n, "vector")
        vlink(lambda: eng.tensor_reduce(tot[:, 0:1], acc[:, 0:64], AX.X, ALU.add))
        vlink(lambda: eng.tensor_reduce(tot[:, 1:2], acc[:, 64:80], AX.X, ALU.add))
        c_tot = S["vec"].bump()
        if R("vector"):
            eng.tensor_copy(tot[:, 2:3], tot[:, 0:1]).then_inc(sems["vec"], 1)
        c_bc = S["mm"].bump()
        if R("tensor"):
            eng.wait_ge(sems["vec"], c_tot)
            eng.matmul(ps[:, 0:2], P["c_ones"][:, 0:128], tot[:, 0:2],
                       start=True, stop=True).then_inc(sems["mm"], 1)
        w("mm", c_bc, "vector")
        for j in range(2):
            vlink(lambda j=j: eng.tensor_scalar(
                P["c_wsc"][:, 2 * j + 1:2 * j + 2], ps[:, j:j + 1],
                1.0 / (INNER * D), EPS, ALU.mult, ALU.max))
            vlink(lambda j=j: eng.reciprocal(
                P["c_wsc"][:, 2 * j:2 * j + 1], P["c_wsc"][:, 2 * j + 1:2 * j + 2]))
        c_done = S["vec"].bump()
        if R("vector"):
            eng.tensor_copy(tot[:, 2:3], P["c_wsc"][:, 0:1]).then_inc(sems["vec"], 1)
        for who in ["sync", "scalar", "tensor", "gpsimd"]:
            w("vec", c_done, who)

    with nc.sbuf_tensor("b1blk", [128, 2 * INNER], F32) as blk, \
         nc.sbuf_tensor("b1acc", [128, 80], F32) as acc, \
         nc.sbuf_tensor("b1tot", [128, 3], F32) as tot, \
         nc.psum_tensor("b1ps", [128, 512], F32) as ps:
        prog.b1 = (blk, acc, tot, ps)
        prog.run_block(emit)
    del prog.b1


def b2_wquant(prog):
    nc, io, P, sems = prog.nc, prog.io, prog.P, prog.sems

    def emit(role, eng, ctx):
        R, S, vlink, dma, w = ctx["R"], ctx["S"], ctx["vlink"], ctx["dma"], ctx["w"]
        pan, tern, panT, tmpa, tmpb, trp = prog.b2
        jobs = [("w1", pc, INNER // 128) for pc in range(KD)] + \
               [("w2", pc, KD) for pc in range(KI)]
        st_after = []
        evac_cnt = []
        for jn, (wname, pc, nib) in enumerate(jobs):
            par = jn % 2
            wsccol = 0 if wname == "w1" else 2
            src = io["w1"] if wname == "w1" else io["w2"]
            dst = io["w1qT"] if wname == "w1" else io["w2qT"]
            qn = "qw0" if par == 0 else "qw1"
            qsn = "qa1" if par == 0 else "qa2"
            if jn >= 2:
                w(qsn, st_after[jn - 2], "sync")
            cd = dma(qn,
                     pan[:, par * INNER:par * INNER + nib * 128].rearrange(
                         "p (i c) -> p i c", c=128),
                     src[:, pc * 128:(pc + 1) * 128].rearrange(
                         "(i p) c -> p i c", p=128))
            c_tern = S["vec"].bump()
            if R("vector"):
                eng.wait_ge(sems[qn], cd)
                if jn >= 1:
                    eng.wait_ge(sems["mm"], tern_tr_last)
                for ch in range((nib * 128 + 2047) // 2048):
                    a = ch * 2048
                    b = min((ch + 1) * 2048, nib * 128)
                    n = b - a
                    psl = slice(par * INNER + a, par * INNER + b)
                    eng.tensor_scalar(tmpa[:, 0:n], pan[:, psl],
                                      P["c_wsc"][:, wsccol:wsccol + 1], 0.5,
                                      ALU.mult, ALU.is_gt)
                    eng.tensor_scalar(tmpb[:, 0:n], pan[:, psl],
                                      P["c_wsc"][:, wsccol:wsccol + 1], -0.5,
                                      ALU.mult, ALU.is_lt)
                    li = eng.tensor_sub(tern[:, a:b], tmpa[:, 0:n], tmpb[:, 0:n])
                li.then_inc(sems["vec"], 1)
            for ib in range(nib):
                slot = len(evac_cnt) % 2
                c_tr = S["mm"].bump()
                if R("tensor"):
                    if ib == 0:
                        eng.wait_ge(sems["vec"], c_tern)
                    if len(evac_cnt) >= 2:
                        eng.wait_ge(sems["sc"], evac_cnt[-2])
                    eng.transpose(trp[:, slot * 1024:slot * 1024 + 128],
                                  tern[:, ib * 128:(ib + 1) * 128],
                                  P["c_identb"][:]).then_inc(sems["mm"], 1)
                c_ev = S["sc"].bump()
                if R("scalar"):
                    eng.wait_ge(sems["mm"], c_tr)
                    eng.activation(
                        panT[:, par * INNER + ib * 128:par * INNER + (ib + 1) * 128],
                        trp[:, slot * 1024:slot * 1024 + 128],
                        AF.Identity).then_inc(sems["sc"], 1)
                evac_cnt.append(c_ev)
            tern_tr_last = S["mm"].n
            w("sc", evac_cnt[-1], "scalar")
            cs = dma(qsn, dst[pc * 128:(pc + 1) * 128, :],
                     panT[:, par * INNER:par * INNER + nib * 128])
            st_after.append(cs)
        for who in ["sync", "vector", "tensor", "scalar", "gpsimd"]:
            w("qa1", S["qa1"].n, who)
            w("qa2", S["qa2"].n, who)

    with nc.sbuf_tensor("b2pan", [128, 2 * INNER], F32) as pan, \
         nc.sbuf_tensor("b2tern", [128, INNER], BF16) as tern, \
         nc.sbuf_tensor("b2panT", [128, 2 * INNER], BF16) as panT, \
         nc.sbuf_tensor("b2tmpa", [128, 2048], F32) as tmpa, \
         nc.sbuf_tensor("b2tmpb", [128, 2048], F32) as tmpb, \
         nc.psum_tensor("b2ps", [128, 2048], BF16) as trp:
        prog.b2 = (pan, tern, panT, tmpa, tmpb, trp)
        prog.run_block(emit)
    del prog.b2


def ffn_pass(prog, src_name, dst_name, tag):
    nc, io, P, sems = prog.nc, prog.io, prog.P, prog.sems
    src_dram, dst_dram = io[src_name], io[dst_name]

    def emit(role, eng, ctx):
        R, S, vlink, dma, w = ctx["R"], ctx["S"], ctx["vlink"], ctx["dma"], ctx["w"]
        (g, xq, xqT, w1g, w2s, hqT, tmpa, tmpb, sgrid, ot,
         psA, psB, psC) = prog.fb
        stat = sgrid[:, 0:40]
        sqg = sgrid[:, 40:104]
        mxg = sgrid[:, 104:168]
        xt = ot[:, 0:D]
        sqdump = ot[:, D:D + 512]
        prev = {"v": 0, "s": 0, "st": 0, "stt": 0, "ffnout": 0}
        last_xq_tr = None
        for sup in range(NSUP):
            t0 = sup * SUP
            stat_done = []
            for t in range(SUP):
                if t >= 1:
                    w("vec", stat_done[t - 1][0], "sync")
                    w("sc", stat_done[t - 1][1], "sync")
                elif sup > 0:
                    w("vec", prev["v"], "sync")
                    w("sc", prev["s"], "sync")
                    w("qa0", prev["st"], "sync")
                cd = dma("qs0", xt, src_dram[(t0 + t) * 128:(t0 + t + 1) * 128, :])
                cv = S["vec"].bump()
                if R("vector"):
                    eng.wait_ge(sems["qs0"], cd)
                    eng.reduce_sum(stat[:, t:t + 1], xt, AX.X)
                    eng.reduce_max(stat[:, 4 + t:5 + t], xt, AX.X)
                    eng.tensor_reduce(stat[:, 8 + t:9 + t], xt, AX.X,
                                      ALU.min).then_inc(sems["vec"], 1)
                cs = S["sc"].bump()
                if R("scalar"):
                    eng.wait_ge(sems["qs0"], cd)
                    for ch in range(4):
                        li = eng.activation(
                            sqdump[:, 0:512],
                            xt[:, ch * 512:(ch + 1) * 512], AF.Square,
                            accum_out=sqg[:, t * 16 + ch:t * 16 + ch + 1])
                    li.then_inc(sems["sc"], 1)
                stat_done.append((cv, cs))
            w("vec", S["vec"].n, "vector")
            w("sc", S["sc"].n, "vector")
            for t in range(SUP):
                vlink(lambda t=t: eng.reduce_sum(stat[:, 12 + t:13 + t],
                                                 sqg[:, t * 16:t * 16 + 4], AX.X))
            vlink(lambda: eng.tensor_scalar(stat[:, 16:20], stat[:, 0:4],
                                            1.0 / D, None, ALU.mult))
            vlink(lambda: eng.tensor_tensor(stat[:, 36:40], stat[:, 16:20],
                                            stat[:, 16:20], ALU.mult))
            vlink(lambda: eng.scalar_tensor_tensor(
                stat[:, 36:40], stat[:, 12:16], 1.0 / D, stat[:, 36:40],
                ALU.mult, ALU.subtract))
            c_var = S["vec"].bump()
            if R("vector"):
                eng.tensor_copy(stat[:, 32:33], stat[:, 36:37]).then_inc(sems["vec"], 1)
            c_sd = S["sc"].bump()
            if R("scalar"):
                eng.wait_ge(sems["vec"], c_var)
                eng.activation(stat[:, 36:40], stat[:, 36:40],
                               AF.Sqrt).then_inc(sems["sc"], 1)
            w("sc", c_sd, "vector")
            vlink(lambda: eng.tensor_sub(stat[:, 20:24], stat[:, 4:8], stat[:, 16:20]))
            vlink(lambda: eng.tensor_sub(stat[:, 32:36], stat[:, 16:20], stat[:, 8:12]))
            vlink(lambda: eng.tensor_max(stat[:, 20:24], stat[:, 20:24], stat[:, 32:36]))
            vlink(lambda: eng.reciprocal(stat[:, 32:36], stat[:, 36:40]))
            vlink(lambda: eng.tensor_mul(stat[:, 20:24], stat[:, 20:24], stat[:, 32:36]))
            vlink(lambda: eng.tensor_scalar(stat[:, 20:24], stat[:, 20:24], EPS,
                                            None, ALU.max))
            vlink(lambda: eng.reciprocal(stat[:, 28:32], stat[:, 20:24]))
            vlink(lambda: eng.tensor_scalar(stat[:, 28:32], stat[:, 28:32], 127.0,
                                            None, ALU.mult))
            vlink(lambda: eng.tensor_mul(stat[:, 28:32], stat[:, 28:32],
                                         stat[:, 32:36]))
            vlink(lambda: eng.tensor_scalar(stat[:, 24:28], stat[:, 20:24],
                                            1.0 / 127.0, None, ALU.mult))
            vlink(lambda: eng.tensor_scalar(stat[:, 24:28], stat[:, 24:28],
                                            P["c_wsc"][:, 1:2], None, ALU.mult))
            c_xchain = S["vec"].bump()
            if R("vector"):
                eng.tensor_copy(stat[:, 32:33], stat[:, 24:25]).then_inc(sems["vec"], 1)
            xqT_done = 0
            xev = []
            qprev = 0
            for t in range(SUP):
                if t >= 1:
                    w("vec", qprev, "sync")
                else:
                    w("vec", stat_done[-1][0], "sync")
                    w("sc", stat_done[-1][1], "sync")
                cd = dma("qs0", xt, src_dram[(t0 + t) * 128:(t0 + t + 1) * 128, :])
                for grp in range(8):
                    cv = S["vec"].bump()
                    if R("vector"):
                        if grp == 0:
                            eng.wait_ge(sems["qs0"], cd)
                        if last_xq_tr is not None:
                            eng.wait_ge(sems["mm"], last_xq_tr)
                        xsl = xt[:, grp * 256:(grp + 1) * 256]
                        eng.tensor_scalar(tmpa[:, 0:256], xsl,
                                          stat[:, 16 + t:17 + t],
                                          stat[:, 28 + t:29 + t],
                                          ALU.subtract, ALU.mult)
                        eng.tensor_scalar(tmpb[:, 0:256], tmpa[:, 0:256],
                                          C64, -C64, ALU.add, ALU.add)
                        eng.tensor_scalar(tmpa[:, 0:256], tmpa[:, 0:256],
                                          C32, -C32, ALU.add, ALU.add)
                        eng.scalar_tensor_tensor(
                            xq[:, 0:256], tmpa[:, 0:256], P["c_flag"][:, 0:1],
                            tmpb[:, 0:256], ALU.mult, ALU.add).then_inc(sems["vec"], 1)
                    for kk in range(2):
                        k = grp * 2 + kk
                        nx = len(xev)
                        slot = nx % 2
                        c_tr = S["mm"].bump()
                        if R("tensor"):
                            eng.wait_ge(sems["vec"], cv)
                            if sup > 0 and t == 0 and grp == 0 and kk == 0:
                                eng.wait_ge(sems["sc"], prev["stt"])
                            if nx >= 2:
                                eng.wait_ge(sems["sc"], xev[nx - 2])
                            eng.transpose(psC[:, slot * 1024:slot * 1024 + 128],
                                          xq[:, kk * 128:(kk + 1) * 128],
                                          P["c_identb"][:]).then_inc(sems["mm"], 1)
                        c_ev = S["sc"].bump()
                        if R("scalar"):
                            eng.wait_ge(sems["mm"], c_tr)
                            eng.activation(
                                xqT[:, k * 512 + t * 128:k * 512 + (t + 1) * 128],
                                psC[:, slot * 1024:slot * 1024 + 128],
                                AF.Identity).then_inc(sems["sc"], 1)
                        xev.append(c_ev)
                    last_xq_tr = c_tr
                qprev = S["vec"].n
                xqT_done = xev[-1]
            mm_evac = {}
            w1g_used = {}
            for gg in range(16):
                par = gg % 2
                qn = "qw0" if par == 0 else "qw1"
                if gg >= 2:
                    w("mm", w1g_used[gg - 2], "sync")
                cd = dma(qn, w1g[:, par * KD * 512:(par + 1) * KD * 512].rearrange(
                             "p (k c) -> p k c", c=512),
                         io["w1qT"][:, gg * 512:(gg + 1) * 512].rearrange(
                             "(k p) c -> p k c", p=128))
                for t in range(SUP):
                    c_mm = S["mm"].bump()
                    if R("tensor"):
                        if t == 0:
                            eng.wait_ge(sems[qn], cd)
                            if gg == 0:
                                eng.wait_ge(sems["sc"], xqT_done)
                                if sup > 0:
                                    eng.wait_ge(sems["vec"], prev["ffnout"])
                        if gg >= 1:
                            eng.wait_ge(sems["sc"], mm_evac[(gg - 1, t)])
                        for k in range(KD):
                            li = eng.matmul(
                                psA[:, t * 512:(t + 1) * 512],
                                xqT[:, k * 512 + t * 128:k * 512 + (t + 1) * 128],
                                w1g[:, par * KD * 512 + k * 512:
                                    par * KD * 512 + (k + 1) * 512],
                                start=(k == 0), stop=(k == KD - 1))
                        li.then_inc(sems["mm"], 1)
                    gsl = g[:, t * INNER + gg * 512:t * INNER + (gg + 1) * 512]
                    cs = S["sc"].bump()
                    if R("scalar"):
                        eng.wait_ge(sems["mm"], c_mm)
                        if gg == 0 and t == 0:
                            eng.wait_ge(sems["vec"], c_xchain)
                        eng.activation(gsl, psA[:, t * 512:(t + 1) * 512],
                                       AF.Gelu, scale=stat[:, 24 + t:25 + t])
                        eng.activation(gsl, gsl, AF.Gelu)
                        eng.activation(sqdump[:, 0:512], gsl, AF.Square,
                                       accum_out=sqg[:, t * 16 + gg:t * 16 + gg + 1]
                                       ).then_inc(sems["sc"], 1)
                    mm_evac[(gg, t)] = cs
                    cv = S["vec"].bump()
                    if R("vector"):
                        eng.wait_ge(sems["sc"], cs)
                        eng.reduce_max(mxg[:, t * 16 + gg:t * 16 + gg + 1],
                                       gsl, AX.X,
                                       apply_absolute_value=True).then_inc(sems["vec"], 1)
                w1g_used[gg] = S["mm"].n
            w("vec", S["vec"].n, "vector")
            w("sc", S["sc"].n, "vector")
            for t in range(SUP):
                vlink(lambda t=t: eng.reduce_sum(stat[:, 0 + t:1 + t],
                                                 sqg[:, t * 16:(t + 1) * 16], AX.X))
                vlink(lambda t=t: eng.reduce_max(stat[:, 4 + t:5 + t],
                                                 mxg[:, t * 16:(t + 1) * 16], AX.X))
            c_hn = S["vec"].bump()
            if R("vector"):
                eng.tensor_copy(stat[:, 32:33], stat[:, 0:1]).then_inc(sems["vec"], 1)
            c_hsd = S["sc"].bump()
            if R("scalar"):
                eng.wait_ge(sems["vec"], c_hn)
                eng.activation(stat[:, 8:12], stat[:, 0:4],
                               AF.Sqrt).then_inc(sems["sc"], 1)
            w("sc", c_hsd, "vector")
            vlink(lambda: eng.tensor_scalar(stat[:, 8:12], stat[:, 8:12], 1e-12,
                                            None, ALU.max))
            vlink(lambda: eng.reciprocal(stat[:, 12:16], stat[:, 8:12]))
            vlink(lambda: eng.tensor_scalar(stat[:, 12:16], stat[:, 12:16],
                                            float(np.sqrt(INNER)), None, ALU.mult))
            vlink(lambda: eng.tensor_mul(stat[:, 20:24], stat[:, 4:8],
                                         stat[:, 12:16]))
            vlink(lambda: eng.tensor_scalar(stat[:, 20:24], stat[:, 20:24], EPS,
                                            None, ALU.max))
            vlink(lambda: eng.reciprocal(stat[:, 28:32], stat[:, 20:24]))
            vlink(lambda: eng.tensor_scalar(stat[:, 28:32], stat[:, 28:32],
                                            127.0, None, ALU.mult))
            vlink(lambda: eng.tensor_mul(stat[:, 28:32], stat[:, 28:32],
                                         stat[:, 12:16]))
            vlink(lambda: eng.tensor_scalar(stat[:, 24:28], stat[:, 20:24],
                                            0.5 / 127.0, None, ALU.mult))
            vlink(lambda: eng.tensor_scalar(stat[:, 24:28], stat[:, 24:28],
                                            P["c_wsc"][:, 3:4], None, ALU.mult))
            rnd = []
            for t in range(SUP):
                cv = S["vec"].bump()
                if R("vector"):
                    for ch in range(16):
                        gsl = g[:, t * INNER + ch * 512:t * INNER + (ch + 1) * 512]
                        eng.tensor_scalar(tmpa[:, 0:512], gsl,
                                          stat[:, 28 + t:29 + t], None, ALU.mult)
                        eng.tensor_scalar(tmpb[:, 0:512], tmpa[:, 0:512],
                                          C64, -C64, ALU.add, ALU.add)
                        eng.tensor_scalar(tmpa[:, 0:512], tmpa[:, 0:512],
                                          C32, -C32, ALU.add, ALU.add)
                        li = eng.scalar_tensor_tensor(
                            gsl, tmpa[:, 0:512], P["c_flag"][:, 0:1],
                            tmpb[:, 0:512], ALU.mult, ALU.add)
                    li.then_inc(sems["vec"], 1)
                rnd.append(cv)
            last_add = 0
            sts = []
            slab_gate = []
            for half in range(2):
                tA_, tB_ = t0 + 2 * half, t0 + 2 * half + 1
                xres = []
                for tt in range(2):
                    if half == 0:
                        w("vec", rnd[-1], "gpsimd")
                        if sup > 0:
                            w("qa0", prev["st"], "gpsimd")
                    else:
                        w("qa0", sts[-1], "gpsimd")
                    xres.append(dma("qg0", ot[:, tt * D:(tt + 1) * D],
                                    src_dram[(tA_ + tt) * 128:(tA_ + tt + 1) * 128, :]))
                for dmh in range(2):
                    tr_evs = []
                    mm2_end = None
                    for icb in range(32):
                        par2 = len(slab_gate) % 2
                        qn2 = "qw0" if par2 == 0 else "qw1"
                        if len(slab_gate) >= 2:
                            w("mm", slab_gate[-2], "sync")
                        cd = dma(qn2,
                                 w2s[:, par2 * 2 * 1024:(par2 + 1) * 2 * 1024].rearrange(
                                     "p (i c) -> p i c", c=1024),
                                 io["w2qT"][icb * 256:(icb + 1) * 256,
                                            dmh * 1024:(dmh + 1) * 1024].rearrange(
                                     "(i p) c -> p i c", p=128))
                        for ici in range(2):
                            ic = icb * 2 + ici
                            parh = ic % 2
                            evs2 = []
                            for tt in range(2):
                                t = tA_ + tt - t0
                                ntr = len(tr_evs)
                                slot = ntr % 2
                                c_tr = S["mm"].bump()
                                if R("tensor"):
                                    if dmh == 0 and ic == 0 and tt == 0:
                                        eng.wait_ge(sems["vec"], rnd[2 * half])
                                        eng.wait_ge(sems["vec"], rnd[2 * half + 1])
                                        if half == 0:
                                            eng.wait_ge(sems["sc"],
                                                        mm_evac[(15, SUP - 1)])
                                    if ntr >= 2:
                                        eng.wait_ge(sems["sc"], tr_evs[ntr - 2])
                                    eng.transpose(
                                        psB[:, slot * 512:slot * 512 + 128],
                                        g[:, t * INNER + ic * 128:
                                          t * INNER + (ic + 1) * 128],
                                        P["c_identf"][:]).then_inc(sems["mm"], 1)
                                c_ev = S["sc"].bump()
                                if R("scalar"):
                                    eng.wait_ge(sems["mm"], c_tr)
                                    eng.activation(
                                        hqT[:, parh * 256 + tt * 128:
                                            parh * 256 + (tt + 1) * 128],
                                        psB[:, slot * 512:slot * 512 + 128],
                                        AF.Identity).then_inc(sems["sc"], 1)
                                evs2.append(c_ev)
                                tr_evs.append(c_ev)
                            c_mm = S["mm"].bump()
                            if R("tensor"):
                                eng.wait_ge(sems["sc"], evs2[-1])
                                if ici == 0:
                                    eng.wait_ge(sems[qn2], cd)
                                if ic == 0:
                                    eng.wait_ge(sems["vec"], last_add)
                                for tt in range(2):
                                    for nn in range(2):
                                        li = eng.matmul(
                                            psA[:, (tt * 2 + nn) * 512:
                                                (tt * 2 + nn + 1) * 512],
                                            hqT[:, parh * 256 + tt * 128:
                                                parh * 256 + (tt + 1) * 128],
                                            w2s[:, par2 * 2 * 1024 + ici * 1024 +
                                                nn * 512:
                                                par2 * 2 * 1024 + ici * 1024 +
                                                (nn + 1) * 512],
                                            start=(ic == 0), stop=(ic == KI - 1))
                                li.then_inc(sems["mm"], 1)
                            mm2_end = S["mm"].n
                        slab_gate.append(S["mm"].n)
                    for tt in range(2):
                        t = tA_ + tt - t0
                        for nn in range(2):
                            c_add = S["vec"].bump()
                            if R("vector"):
                                eng.wait_ge(sems["mm"], mm2_end)
                                if dmh == 0 and tt == 0 and nn == 0:
                                    eng.wait_ge(sems["qg0"], xres[-1])
                                osl = ot[:, tt * D + dmh * 1024 + nn * 512:
                                         tt * D + dmh * 1024 + (nn + 1) * 512]
                                eng.scalar_tensor_tensor(
                                    osl, psA[:, (tt * 2 + nn) * 512:
                                             (tt * 2 + nn + 1) * 512],
                                    stat[:, 24 + t:25 + t], osl,
                                    ALU.mult, ALU.add).then_inc(sems["vec"], 1)
                            last_add = c_add
                w("vec", last_add, "scalar")
                for tt in range(2):
                    sts.append(dma("qa0",
                                   dst_dram[(tA_ + tt) * 128:(tA_ + tt + 1) * 128, :],
                                   ot[:, tt * D:(tt + 1) * D]))
            prev = {"v": S["vec"].n, "s": S["sc"].n, "st": sts[-1],
                    "stt": S["sc"].n, "ffnout": last_add}
        for who in ["sync", "vector", "tensor", "scalar", "gpsimd"]:
            w("qa0", S["qa0"].n, who)

    with contextlib.ExitStack() as stk:
        sb, pt = prog.nc.sbuf_tensor, prog.nc.psum_tensor
        prog.fb = (
            stk.enter_context(sb(f"g{tag}", [128, SUP * INNER], F32)),
            stk.enter_context(sb(f"xq{tag}", [128, 256], BF16)),
            stk.enter_context(sb(f"xqT{tag}", [128, KD * 512], BF16)),
            stk.enter_context(sb(f"w1g{tag}", [128, 2 * KD * 512], BF16)),
            stk.enter_context(sb(f"w2s{tag}", [128, 2 * 2 * 1024], BF16)),
            stk.enter_context(sb(f"hqT{tag}", [128, 2 * 256], BF16)),
            stk.enter_context(sb(f"tA{tag}", [128, 512], F32)),
            stk.enter_context(sb(f"tB{tag}", [128, 512], F32)),
            stk.enter_context(sb(f"sg{tag}", [128, 168], F32)),
            stk.enter_context(sb(f"ot{tag}", [128, 2 * D], F32)),
            stk.enter_context(pt(f"pA{tag}", [128, SUP * 512], F32)),
            stk.enter_context(pt(f"pB{tag}", [128, 1024], F32)),
            stk.enter_context(pt(f"pC{tag}", [128, 2048], BF16)),
        )
        prog.run_block(emit)
    del prog.fb


def b4_qkv(prog, src_name):
    nc, io, P, sems = prog.nc, prog.io, prog.P, prog.sems
    src_dram = io[src_name]

    def emit(role, eng, ctx):
        R, S, vlink, dma, w = ctx["R"], ctx["S"], ctx["vlink"], ctx["dma"], ctx["w"]
        (xt, xnb, xnT, wb, stage, sgrid, psQ, psK, psV, psT) = prog.b4
        if B4_SERIAL:
            _dma0 = ctx["dma"]
            def dma(q, dst, srcap):
                for sname in ["vec", "sc", "mm"]:
                    w(sname, S[sname].n, {"qs0": "sync", "qs1": "sync",
                                          "qa0": "scalar", "qg0": "gpsimd",
                                          "qk0": "sync", "qk1": "sync"}[q])
                return _dma0(q, dst, srcap)
            _sb = Sem.bump
            def serial_guard(which):
                for sname in ["vec", "sc", "mm"]:
                    if sname != which:
                        w(sname, S[sname].n, {"vec": "vector", "sc": "scalar",
                                              "mm": "tensor"}[which])
            class _SemWrap:
                pass
            ctx["serial_guard"] = serial_guard
        else:
            ctx["serial_guard"] = lambda which: None
        serial_guard = ctx["serial_guard"]
        stat = sgrid[:, 0:40]
        sqg = sgrid[:, 40:104]
        GT = 8
        prevv = {"v": 0, "s": 0}
        for gp in range(2):
            t0 = gp * GT
            stat_done = []
            for t in range(GT):
                if t >= 1:
                    w("vec", stat_done[t - 1][0], "sync")
                    w("sc", stat_done[t - 1][1], "sync")
                elif gp > 0:
                    w("vec", prevv["v"], "sync")
                    w("sc", prevv["s"], "sync")
                cd = dma("qs0", xt[:, 0:D],
                         src_dram[(t0 + t) * 128:(t0 + t + 1) * 128, :])
                serial_guard("vec")
                cv = S["vec"].bump()
                if R("vector"):
                    eng.wait_ge(sems["qs0"], cd)
                    eng.reduce_sum(stat[:, t:t + 1], xt[:, 0:D],
                                   AX.X).then_inc(sems["vec"], 1)
                cs = S["sc"].bump()
                if R("scalar"):
                    eng.wait_ge(sems["qs0"], cd)
                    for ch in range(4):
                        li = eng.activation(
                            stage[:, 0:512],
                            xt[:, ch * 512:(ch + 1) * 512], AF.Square,
                            accum_out=sqg[:, t * 4 + ch:t * 4 + ch + 1])
                    li.then_inc(sems["sc"], 1)
                stat_done.append((cv, cs))
            w("vec", S["vec"].n, "vector")
            w("sc", S["sc"].n, "vector")
            for t in range(GT):
                vlink(lambda t=t: eng.reduce_sum(stat[:, 16 + t:17 + t],
                                                 sqg[:, t * 4:t * 4 + 4], AX.X))
            vlink(lambda: eng.tensor_scalar(stat[:, 0:8], stat[:, 0:8],
                                            1.0 / D, None, ALU.mult))
            vlink(lambda: eng.tensor_tensor(stat[:, 24:32], stat[:, 0:8],
                                            stat[:, 0:8], ALU.mult))
            vlink(lambda: eng.scalar_tensor_tensor(
                stat[:, 24:32], stat[:, 16:24], 1.0 / D, stat[:, 24:32],
                ALU.mult, ALU.subtract))
            vlink(lambda: eng.tensor_scalar(stat[:, 24:32], stat[:, 24:32],
                                            EPS, None, ALU.add))
            c_var = S["vec"].bump()
            if R("vector"):
                eng.tensor_copy(stat[:, 32:33], stat[:, 24:25]).then_inc(sems["vec"], 1)
            c_sd = S["sc"].bump()
            if R("scalar"):
                eng.wait_ge(sems["vec"], c_var)
                eng.activation(stat[:, 24:32], stat[:, 24:32],
                               AF.Sqrt).then_inc(sems["sc"], 1)
            w("sc", c_sd, "vector")
            vlink(lambda: eng.reciprocal(stat[:, 8:16], stat[:, 24:32]))
            xev = []
            qprev = 0
            for t in range(GT):
                if t >= 1:
                    w("vec", qprev, "sync")
                else:
                    w("vec", stat_done[-1][0], "sync")
                    w("sc", stat_done[-1][1], "sync")
                cd = dma("qs0", xt[:, 0:D],
                         src_dram[(t0 + t) * 128:(t0 + t + 1) * 128, :])
                serial_guard("vec")
                cv = S["vec"].bump()
                if R("vector"):
                    eng.wait_ge(sems["qs0"], cd)
                    eng.tensor_scalar(xnb[:, 0:D], xt[:, 0:D], stat[:, t:t + 1],
                                      stat[:, 8 + t:9 + t],
                                      ALU.subtract, ALU.mult).then_inc(sems["vec"], 1)
                qprev = cv
                for k in range(KD):
                    nx = len(xev)
                    slot = nx % 2
                    serial_guard("mm")
                    c_tr = S["mm"].bump()
                    if R("tensor"):
                        eng.wait_ge(sems["vec"], cv)
                        if nx >= 2:
                            eng.wait_ge(sems["vec"], xev[nx - 2])
                        eng.transpose(psT[:, slot * 512:slot * 512 + 128],
                                      xnb[:, k * 128:(k + 1) * 128],
                                      P["c_identf"][:]).then_inc(sems["mm"], 1)
                    serial_guard("vec")
                    c_ev = S["vec"].bump()
                    if R("vector"):
                        eng.wait_ge(sems["mm"], c_tr)
                        eng.tensor_copy(
                            xnT[:, k * 1024 + t * 128:k * 1024 + (t + 1) * 128],
                            psT[:, slot * 512:slot * 512 + 128]).then_inc(sems["vec"], 1)
                    xev.append(c_ev)
            xnT_done = xev[-1]
            if DBG_B4:
                w("vec", xnT_done, "gpsimd")
                dma("qg0", io["dbgxnT"][gp * 128:(gp + 1) * 128, :],
                    xnT[:, 0:KD * 1024])
                dma("qg0", io["dbgstat"][gp * 128:(gp + 1) * 128, :],
                    sgrid[:, 0:104])
            wprev = {}
            stq_gate = []
            for h in range(H):
                parw = h % 2
                if h >= 2:
                    w("mm", wprev[(h - 2, "mm")], "sync")
                qwn = "qw0" if parw == 0 else "qw1"
                cds = []
                for wi, wn in enumerate(["wq", "wk", "wv"]):
                    cds.append(dma(
                        qwn,
                        wb[:, parw * 3 * KD * 128 + wi * KD * 128:
                           parw * 3 * KD * 128 + (wi + 1) * KD * 128].rearrange(
                            "p (k c) -> p k c", c=128),
                        io[wn][h * D:(h + 1) * D, :].rearrange(
                            "(k p) c -> p k c", p=128)))
                for wi, (ps_, dstd, bT) in enumerate(
                        [(psQ, io["qsc"], "c_bqT"), (psK, io["ksc"], "c_bkT")]):
                    serial_guard("mm")
                    c_mm = S["mm"].bump()
                    if R("tensor"):
                        eng.wait_ge(sems[qwn], cds[-1])
                        if h == 0:
                            eng.wait_ge(sems["vec"], xnT_done)
                        if h >= 1:
                            eng.wait_ge(sems["sc"], wprev[(h - 1, "sc")])
                        for nn in range(2):
                            for k in range(KD):
                                li = eng.matmul(
                                    ps_[:, nn * 512:(nn + 1) * 512],
                                    wb[:, parw * 3 * KD * 128 + wi * KD * 128 +
                                       k * 128:parw * 3 * KD * 128 +
                                       wi * KD * 128 + (k + 1) * 128],
                                    xnT[:, k * 1024 + nn * 512:
                                        k * 1024 + (nn + 1) * 512],
                                    start=(k == 0), stop=(k == KD - 1))
                        li.then_inc(sems["mm"], 1)
                    serial_guard("sc")
                    cs = S["sc"].bump()
                    if R("scalar"):
                        eng.wait_ge(sems["mm"], c_mm)
                        if len(stq_gate) >= 3:
                            eng.wait_ge(sems[stq_gate[-3][0]], stq_gate[-3][1])
                        eng.activation(stage[:, wi * 1024:(wi + 1) * 1024],
                                       ps_[:, 0:1024], AF.Identity,
                                       bias=P[bT][:, h:h + 1]).then_inc(sems["sc"], 1)
                    w("sc", cs, "scalar")
                    qan = ["qa0", "qa1"][wi]
                    stq_gate.append((qan, dma(qan, dstd[h * 128:(h + 1) * 128,
                                              gp * 1024:(gp + 1) * 1024],
                                              stage[:, wi * 1024:(wi + 1) * 1024])))
                vev = []
                for t in range(GT):
                    slotv = t % 2
                    serial_guard("mm")
                    c_mm = S["mm"].bump()
                    if R("tensor"):
                        if t >= 2:
                            eng.wait_ge(sems["sc"], vev[t - 2])
                        if t <= 1 and h >= 1:
                            eng.wait_ge(sems["sc"], wprev[(h - 1, "sc")])
                        for k in range(KD):
                            li = eng.matmul(
                                psV[:, slotv * 512:slotv * 512 + 128],
                                xnT[:, k * 1024 + t * 128:k * 1024 + (t + 1) * 128],
                                wb[:, parw * 3 * KD * 128 + 2 * KD * 128 +
                                   k * 128:parw * 3 * KD * 128 +
                                   2 * KD * 128 + (k + 1) * 128],
                                start=(k == 0), stop=(k == KD - 1))
                        li.then_inc(sems["mm"], 1)
                    serial_guard("sc")
                    cs = S["sc"].bump()
                    if R("scalar"):
                        eng.wait_ge(sems["mm"], c_mm)
                        if t == 0 and len(stq_gate) >= 3:
                            eng.wait_ge(sems[stq_gate[-3][0]], stq_gate[-3][1])
                        eng.activation(stage[:, 2048 + t * 128:2048 + (t + 1) * 128],
                                       psV[:, slotv * 512:slotv * 512 + 128],
                                       AF.Identity).then_inc(sems["sc"], 1)
                    vev.append(cs)
                w("sc", vev[-1], "scalar")
                stq_gate.append(("qa2", dma(
                    "qa2", io["vsc"][h * T + gp * 1024:h * T + (gp + 1) * 1024, :
                                     ].rearrange("(t p) c -> p t c", p=128),
                    stage[:, 2048:2048 + GT * 128].rearrange(
                        "p (t c) -> p t c", c=128))))
                wprev[(h, "mm")] = S["mm"].n
                wprev[(h, "sc")] = S["sc"].n
            prevv = {"v": S["vec"].n, "s": S["sc"].n}
        for who in ["sync", "vector", "tensor", "scalar", "gpsimd"]:
            w("qa0", S["qa0"].n, who)
            w("qa1", S["qa1"].n, who)
            w("qa2", S["qa2"].n, who)

    with contextlib.ExitStack() as stk:
        sb, pt = prog.nc.sbuf_tensor, prog.nc.psum_tensor
        prog.b4 = (
            stk.enter_context(sb("b4xt", [128, D], F32)),
            stk.enter_context(sb("b4xn", [128, D], F32)),
            stk.enter_context(sb("b4xnT", [128, KD * 1024], F32)),
            stk.enter_context(sb("b4wb", [128, 2 * 3 * KD * 128], F32)),
            stk.enter_context(sb("b4stage", [128, 3072], F32)),
            stk.enter_context(sb("b4sg", [128, 40 + 64], F32)),
            stk.enter_context(pt("b4psQ", [128, 1024], F32)),
            stk.enter_context(pt("b4psK", [128, 1024], F32)),
            stk.enter_context(pt("b4psV", [128, 1024], F32)),
            stk.enter_context(pt("b4psT", [128, 1024], F32)),
        )
        prog.run_block(emit)
    del prog.b4


def b5_attn(prog, src_name, dst_name):
    nc, io, P, sems = prog.nc, prog.io, prog.P, prog.sems
    src_dram, dst_dram = io[src_name], io[dst_name]

    def emit(role, eng, ctx):
        R, S, vlink, dma, w = ctx["R"], ctx["S"], ctx["vlink"], ctx["dma"], ctx["w"]
        (kh, vh, qh, pbuf, ptb, ystr, dstat, diag, caus, borep,
         xsl, ost, wos, psA, psB, psD, psE) = prog.b5
        c_c = dma("qs0", caus[:], io["caus"][:, :])
        c_b = dma("qs0", borep[0:1, 0:D], io["bo"][:, :])
        c_bc = S["mm"].bump()
        if R("tensor"):
            eng.wait_ge(sems["qs0"], max(c_c, c_b))
            for nn in range(4):
                li = eng.matmul(psA[:, nn * 512:(nn + 1) * 512],
                                P["c_ones"][0:1, 0:128],
                                borep[0:1, nn * 512:(nn + 1) * 512],
                                start=True, stop=True)
            li.then_inc(sems["mm"], 1)
        c_bv = S["vec"].bump()
        if R("vector"):
            eng.wait_ge(sems["mm"], c_bc)
            eng.tensor_copy(borep[:], psA[:, 0:2048]).then_inc(sems["vec"], 1)
        hk_prev = {}
        last_wo_add = c_bv
        ost_gate = []
        xsl_gate = []
        for sp in range(NSTRIPE):
            q0 = sp * (STRIPE // 128)
            for h in range(H):
                hg = sp * H + h
                parh = hg % 2
                if hg >= 2:
                    w("mm", hk_prev[(hg - 2, "mm")], "sync")
                    w("sc", hk_prev[(hg - 2, "sc")], "sync")
                qk = "qk0" if parh == 0 else "qk1"
                dma(qk, kh[:, parh * T:(parh + 1) * T],
                    io["ksc"][h * 128:(h + 1) * 128, :])
                dma(qk, vh[:, parh * NT * 128:(parh + 1) * NT * 128].rearrange(
                        "p (t c) -> p t c", c=128),
                    io["vsc"][h * T:(h + 1) * T, :].rearrange(
                        "(t p) c -> p t c", p=128))
                cdq = dma(qk, qh[:, parh * STRIPE:(parh + 1) * STRIPE],
                          io["qsc"][h * 128:(h + 1) * 128,
                                    sp * STRIPE:(sp + 1) * STRIPE])
                first_of_head = True
                for qi_l in range(STRIPE // 128):
                    qi = q0 + qi_l
                    nkeys = (qi + 1) * 128
                    nck = (nkeys + 511) // 512
                    c_s = S["mm"].bump()
                    if R("tensor"):
                        if first_of_head:
                            eng.wait_ge(sems[qk], cdq)
                            if sp == 0 and h == 0:
                                eng.wait_ge(sems["vec"], c_bv)
                        eng.wait_ge(sems["sc"], S["sc"].n)
                        if first_of_head and h == 0:
                            eng.wait_ge(sems["vec"], last_wo_add)
                        for ck in range(nck):
                            a = ck * 512
                            b = min(nkeys, a + 512)
                            li = eng.matmul(
                                psA[:, a:b],
                                qh[:, parh * STRIPE + qi_l * 128:
                                   parh * STRIPE + (qi_l + 1) * 128],
                                kh[:, parh * T + a:parh * T + b],
                                start=True, stop=True)
                        li.then_inc(sems["mm"], 1)
                    first_of_head = False
                    c_mask = S["vec"].bump()
                    if R("vector"):
                        eng.wait_ge(sems["mm"], c_s)
                        eng.tensor_add(psA[:, qi * 128:nkeys],
                                       psA[:, qi * 128:nkeys],
                                       caus[:]).then_inc(sems["vec"], 1)
                    vlink(lambda: eng.reduce_max(dstat[:, 0:1],
                                                 psA[:, 0:nkeys], AX.X))
                    vlink(lambda: eng.tensor_scalar(dstat[:, 1:2], dstat[:, 0:1],
                                                    -ATT_SCALE, None, ALU.mult))
                    c_negm = S["vch"].n
                    cs_exp = S["sc"].bump()
                    if R("scalar"):
                        eng.wait_ge(sems["vch"], c_negm)
                        eng.activation(pbuf[:, 0:nkeys], psA[:, 0:nkeys],
                                       AF.Exp, bias=dstat[:, 1:2],
                                       scale=ATT_SCALE,
                                       accum_out=dstat[:, 2:3]).then_inc(sems["sc"], 1)
                    w("sc", cs_exp, "vector")
                    vlink(lambda: eng.reciprocal(dstat[:, 3:4], dstat[:, 2:3]))
                    vlink(lambda: eng.tensor_scalar(pbuf[:, 0:nkeys],
                                                    pbuf[:, 0:nkeys],
                                                    dstat[:, 3:4], None, ALU.mult))
                    c_diag = S["vch"].n
                    pev = []
                    for kc in range(qi + 1):
                        slot = kc % 3
                        tdst = (psB[:, slot * 512:slot * 512 + 128] if slot < 2
                                else psE[:, 0:128])
                        c_tr = S["mm"].bump()
                        if R("tensor"):
                            if kc == 0:
                                eng.wait_ge(sems["vch"], c_diag)
                            if kc >= 3:
                                eng.wait_ge(sems["vec"], pev[kc - 3])
                            eng.transpose(tdst,
                                          pbuf[:, kc * 128:(kc + 1) * 128],
                                          P["c_identf"][:]).then_inc(sems["mm"], 1)
                        c_ev = S["vec"].bump()
                        if R("vector"):
                            eng.wait_ge(sems["mm"], c_tr)
                            eng.tensor_copy(ptb[:, kc * 128:(kc + 1) * 128],
                                            tdst).then_inc(sems["vec"], 1)
                        pev.append(c_ev)
                    c_av = S["mm"].bump()
                    if R("tensor"):
                        eng.wait_ge(sems["vec"], pev[-1])
                        for kc in range(qi + 1):
                            li = eng.matmul(
                                psD[:, 0:128],
                                vh[:, parh * NT * 128 + kc * 128:
                                   parh * NT * 128 + (kc + 1) * 128],
                                ptb[:, kc * 128:(kc + 1) * 128],
                                start=(kc == 0), stop=(kc == qi))
                        li.then_inc(sems["mm"], 1)
                    cs_y = S["sc"].bump()
                    if R("scalar"):
                        eng.wait_ge(sems["mm"], c_av)
                        eng.activation(ystr[:, h * STRIPE + qi_l * 128:
                                            h * STRIPE + (qi_l + 1) * 128],
                                       psD[:, 0:128], AF.Identity,
                                       bias=P["c_bvT"][:, h:h + 1]).then_inc(sems["sc"], 1)
                hk_prev[(hg, "mm")] = S["mm"].n
                hk_prev[(hg, "sc")] = S["sc"].n
            c_heads = S["sc"].n
            wo_banks = [psA[:, i * 512:(i + 1) * 512] for i in range(4)] + \
                       [psB[:, i * 512:(i + 1) * 512] for i in range(2)] + \
                       [psD[:, 0:512], psE[:, 0:512]]
            wo_gate = []
            for nn in range(4):
                wo_last = None
                for k in range(KD):
                    j = len(wo_gate)
                    parw = j % 2
                    qwn = "qw0" if parw == 0 else "qw1"
                    if j >= 2:
                        w("mm", wo_gate[j - 2], "sync")
                    cd = dma(qwn, wos[:, parw * 512:(parw + 1) * 512],
                             io["wo"][k * 128:(k + 1) * 128,
                                      nn * 512:(nn + 1) * 512])
                    c_mm = S["mm"].bump()
                    if R("tensor"):
                        eng.wait_ge(sems[qwn], cd)
                        if k == 0:
                            if nn == 0:
                                eng.wait_ge(sems["sc"], c_heads)
                            eng.wait_ge(sems["vec"], last_wo_add)
                        for qb in range(8):
                            li = eng.matmul(
                                wo_banks[qb],
                                ystr[:, k * STRIPE + qb * 128:
                                     k * STRIPE + (qb + 1) * 128],
                                wos[:, parw * 512:(parw + 1) * 512],
                                start=(k == 0), stop=(k == KD - 1))
                        li.then_inc(sems["mm"], 1)
                    wo_gate.append(S["mm"].n)
                wo_last = wo_gate[-1]
                for qb in range(8):
                    parx = len(ost_gate) % 2
                    qgn = "qg0" if parx == 0 else "qg1"
                    qan = "qa0" if parx == 0 else "qa1"
                    if len(xsl_gate) >= 2:
                        w("vec", xsl_gate[-2], "gpsimd")
                    cdx = dma(qgn, xsl[:, parx * 512:(parx + 1) * 512],
                              src_dram[(q0 + qb) * 128:(q0 + qb + 1) * 128,
                                       nn * 512:(nn + 1) * 512])
                    c_a1 = S["vec"].bump(2)
                    if R("vector"):
                        eng.wait_ge(sems["mm"], wo_last)
                        eng.wait_ge(sems[qgn], cdx)
                        if len(ost_gate) >= 2:
                            eng.wait_ge(sems[qan], ost_gate[-2])
                        eng.tensor_add(ost[:, parx * 512:(parx + 1) * 512],
                                       wo_banks[qb],
                                       xsl[:, parx * 512:(parx + 1) * 512])
                        eng.tensor_add(ost[:, parx * 512:(parx + 1) * 512],
                                       ost[:, parx * 512:(parx + 1) * 512],
                                       borep[:, nn * 512:(nn + 1) * 512]
                                       ).then_inc(sems["vec"], 2)
                    last_wo_add = c_a1
                    xsl_gate.append(c_a1)
                    w("vec", c_a1, "scalar")
                    cst = dma(qan,
                              dst_dram[(q0 + qb) * 128:(q0 + qb + 1) * 128,
                                       nn * 512:(nn + 1) * 512],
                              ost[:, parx * 512:(parx + 1) * 512])
                    ost_gate.append(cst)
        for who in ["sync", "vector", "tensor", "scalar", "gpsimd"]:
            w("qa0", S["qa0"].n, who)
            w("qa1", S["qa1"].n, who)

    with contextlib.ExitStack() as stk:
        sb, pt = prog.nc.sbuf_tensor, prog.nc.psum_tensor
        prog.b5 = (
            stk.enter_context(sb("b5kh", [128, 2 * T], F32)),
            stk.enter_context(sb("b5vh", [128, 2 * NT * 128], F32)),
            stk.enter_context(sb("b5qh", [128, 2 * STRIPE], F32)),
            stk.enter_context(sb("b5p", [128, T], F32)),
            stk.enter_context(sb("b5pt", [128, T], F32)),
            stk.enter_context(sb("b5y", [128, H * STRIPE], F32)),
            stk.enter_context(sb("b5ds", [128, 8], F32)),
            stk.enter_context(sb("b5diag", [128, 128], F32)),
            stk.enter_context(sb("b5caus", [128, 128], F32)),
            stk.enter_context(sb("b5bo", [128, D], F32)),
            stk.enter_context(sb("b5xs", [128, 2 * 512], F32)),
            stk.enter_context(sb("b5os", [128, 2 * 512], F32)),
            stk.enter_context(sb("b5wo", [128, 2 * 512], F32)),
            stk.enter_context(pt("b5psA", [128, 2048], F32)),
            stk.enter_context(pt("b5psB", [128, 1024], F32)),
            stk.enter_context(pt("b5psD", [128, 512], F32)),
            stk.enter_context(pt("b5psE", [128, 512], F32)),
        )
        prog.run_block(emit)
    del prog.b5


_CACHE = {}


def _build():
    if "nc" in _CACHE:
        return _CACHE["nc"]
    prog = Prog()
    b0_consts(prog)
    b1_wscale(prog)
    b2_wquant(prog)
    if STAGE == 1:
        ffn_pass(prog, "x", "y", "f1")
    elif STAGE == 2:
        ffn_pass(prog, "x", "xr1", "f1")
        b4_qkv(prog, "xr1")
        b5_attn(prog, "xr1", "y")
    else:
        ffn_pass(prog, "x", "xr1", "f1")
        b4_qkv(prog, "xr1")
        b5_attn(prog, "xr1", "xr2")
        ffn_pass(prog, "xr2", "y", "f3")
    prog.es.close()
    _CACHE["nc"] = prog.nc
    return prog.nc


def kernel(**inputs):
    nc = _build()
    x = np.ascontiguousarray(np.asarray(inputs["L0_attn_x"], np.float32))
    sh = dict(
        w1=np.ascontiguousarray(np.asarray(inputs["L0_ffn_w1"], np.float32)),
        w2=np.ascontiguousarray(np.asarray(inputs["L0_ffn_w2"], np.float32)),
        wq=np.ascontiguousarray(np.asarray(inputs["L0_attn_Wq"], np.float32).reshape(H * D, HD)),
        wk=np.ascontiguousarray(np.asarray(inputs["L0_attn_Wk"], np.float32).reshape(H * D, HD)),
        wv=np.ascontiguousarray(np.asarray(inputs["L0_attn_Wv"], np.float32).reshape(H * D, HD)),
        wo=np.ascontiguousarray(np.asarray(inputs["L0_attn_Wo"], np.float32)),
        bq=np.ascontiguousarray(np.asarray(inputs["L0_attn_bq"], np.float32)),
        bk=np.ascontiguousarray(np.asarray(inputs["L0_attn_bk"], np.float32)),
        bv=np.ascontiguousarray(np.asarray(inputs["L0_attn_bv"], np.float32)),
        bo=np.ascontiguousarray(np.asarray(inputs["L0_attn_bo"], np.float32).reshape(1, D)),
        identf=np.eye(128, dtype=np.float32),
        identb=np.eye(128).astype(ml_dtypes.bfloat16),
        ones=np.ones((128, 128), np.float32),
        caus=np.ascontiguousarray(np.where(
            np.arange(128)[:, None] >= np.arange(128)[None, :],
            np.float32(0), np.finfo(np.float32).min).astype(np.float32)),
    )
    in_maps = [dict(sh, x=np.ascontiguousarray(x[c])) for c in range(N_CORES)]
    res = run_bass_kernel_spmd(nc, in_maps, core_ids=list(range(N_CORES)))
    return np.stack([r["y"] for r in res.results], axis=0)


# version 06
# speedup vs baseline: 1.0682x; 1.0329x over previous
import contextlib
import numpy as np
import ml_dtypes
import concourse.bass as bass
import concourse.mybir as mybir
from concourse.bass_utils import run_bass_kernel_spmd

F32 = mybir.dt.float32
BF16 = mybir.dt.bfloat16
AF = mybir.ActivationFunctionType
ALU = mybir.AluOpType
AX = mybir.AxisListType

T = 2048
D = 2048
INNER = 8192
H = 16
HD = 128
NT = T // 128
KD = D // 128
KI = INNER // 128
SUP = 4
NSUP = NT // SUP
STRIPE = 1024
NSTRIPE = T // STRIPE
EPS = 1e-5
C32 = float(1.5 * 2**23)
C64 = float(1.5 * 2**52)
ATT_SCALE = float(HD) ** -0.5
N_CORES = 8

STAGE = 3
DBG_SPILL = False
B4_SERIAL = False
DBG_B4 = False


class Sem:
    def __init__(self, h):
        self.h = h
        self.n = 0

    def bump(self, k=1):
        self.n += k
        return self.n


class Prog:
    def __init__(self):
        nc = bass.Bass("TRN2", target_bir_lowering=False,
                       detect_race_conditions=False)
        self.nc = nc
        self.es = contextlib.ExitStack()
        io = {}
        io["x"] = nc.dram_tensor("x", [T, D], F32, kind="ExternalInput")
        io["w1"] = nc.dram_tensor("w1", [INNER, D], F32, kind="ExternalInput")
        io["w2"] = nc.dram_tensor("w2", [D, INNER], F32, kind="ExternalInput")
        io["wq"] = nc.dram_tensor("wq", [H * D, HD], F32, kind="ExternalInput")
        io["wk"] = nc.dram_tensor("wk", [H * D, HD], F32, kind="ExternalInput")
        io["wv"] = nc.dram_tensor("wv", [H * D, HD], F32, kind="ExternalInput")
        io["wo"] = nc.dram_tensor("wo", [D, D], F32, kind="ExternalInput")
        io["bq"] = nc.dram_tensor("bq", [H, HD], F32, kind="ExternalInput")
        io["bk"] = nc.dram_tensor("bk", [H, HD], F32, kind="ExternalInput")
        io["bv"] = nc.dram_tensor("bv", [H, HD], F32, kind="ExternalInput")
        io["bo"] = nc.dram_tensor("bo", [1, D], F32, kind="ExternalInput")
        io["identf"] = nc.dram_tensor("identf", [128, 128], F32, kind="ExternalInput")
        io["identb"] = nc.dram_tensor("identb", [128, 128], BF16, kind="ExternalInput")
        io["ones"] = nc.dram_tensor("ones", [128, 128], F32, kind="ExternalInput")
        io["caus"] = nc.dram_tensor("caus", [128, 128], F32, kind="ExternalInput")
        io["y"] = nc.dram_tensor("y", [T, D], F32, kind="ExternalOutput")
        io["w1qT"] = nc.dram_tensor("w1qT", [D, INNER], BF16)
        io["w2qT"] = nc.dram_tensor("w2qT", [INNER, D], BF16)
        io["xr1"] = nc.dram_tensor("xr1", [T, D], F32,
                                   kind="ExternalOutput" if DBG_B4 else "Internal")
        io["xr2"] = nc.dram_tensor("xr2", [T, D], F32)
        kk = "ExternalOutput" if DBG_SPILL else "Internal"
        if DBG_B4:
            io["dbgxnT"] = nc.dram_tensor("dbgxnT", [2 * 128, KD * 1024], F32,
                                          kind="ExternalOutput")
            io["dbgstat"] = nc.dram_tensor("dbgstat", [2 * 128, 104], F32,
                                           kind="ExternalOutput")
        io["qsc"] = nc.dram_tensor("qsc", [H * HD, T], F32, kind=kk)
        io["ksc"] = nc.dram_tensor("ksc", [H * HD, T], F32, kind=kk)
        io["vsc"] = nc.dram_tensor("vsc", [H * T, HD], F32, kind=kk)
        self.io = io

        self.sems = {n: self.es.enter_context(nc.semaphore(n)) for n in
                     ["qs0", "qs1", "qw0", "qw1", "qa0", "qa1", "qa2",
                      "qg0", "qg1", "qk0", "qk1", "vec", "vch", "mm", "sc"]}
        self.counts = {k: 0 for k in self.sems}

        P = {}
        def psb(name, shape, dtype=F32):
            P[name] = self.es.enter_context(nc.sbuf_tensor(name, shape, dtype))
        psb("c_identf", [128, 128])
        psb("c_identb", [128, 128], BF16)
        psb("c_ones", [128, 128])
        psb("c_flag", [128, 1])
        psb("c_wsc", [128, 4])
        psb("c_bqT", [128, H])
        psb("c_bkT", [128, H])
        psb("c_bvT", [128, H])
        psb("c_catt", [128, 1])
        nc.const_aps.register(F32, ATT_SCALE, P["c_catt"][:, 0:1])
        self.P = P

    def run_block(self, emit):
        final = {}
        with self.nc.Block() as block:
            for role, reg in [("sync", block.sync), ("gpsimd", block.gpsimd),
                              ("vector", block.vector), ("tensor", block.tensor),
                              ("scalar", block.scalar)]:
                def f(eng, role=role):
                    ctx = self._ctx(role, eng)
                    emit(role, eng, ctx)
                    final.clear()
                    final.update({k: s.n for k, s in ctx["S"].items()})
                reg(f)
        self.counts = dict(final)

    def _ctx(self, role, eng):
        sems = self.sems
        S = {k: Sem(h) for k, h in sems.items()}
        for k, v in self.counts.items():
            S[k].n = v
        ctx = {"S": S}

        def R(r):
            return role == r
        ctx["R"] = R

        def vlink(mk):
            c = S["vch"].bump()
            if R("vector"):
                mk().then_inc(sems["vch"], 1)
                eng.wait_ge(sems["vch"], c)
        ctx["vlink"] = vlink

        QOWNER = {"qs0": "sync", "qs1": "sync", "qw0": "sync", "qw1": "sync",
                  "qa0": "scalar", "qa1": "scalar", "qa2": "scalar",
                  "qg0": "gpsimd", "qg1": "gpsimd",
                  "qk0": "sync", "qk1": "sync"}

        def dma(q, dst, src):
            c = S[q].bump(16)
            if R(QOWNER[q]):
                eng.dma_start(dst, src).then_inc(sems[q], 16)
            return c
        ctx["dma"] = dma

        def w(semname, val, who):
            if R(who):
                eng.wait_ge(sems[semname], val)
        ctx["w"] = w
        return ctx



def b0_consts(prog):
    nc, io, P, sems = prog.nc, prog.io, prog.P, prog.sems

    def emit(role, eng, ctx):
        R, S, vlink, dma, w = ctx["R"], ctx["S"], ctx["vlink"], ctx["dma"], ctx["w"]
        btmp, bps = prog.b0
        for name in ["identf", "identb", "ones"]:
            dma("qs0", P["c_" + name][:], io[name][:, :])
        for i, nm in enumerate(["bq", "bk", "bv"]):
            dma("qs0", btmp[0:H, i * 128:(i + 1) * 128], io[nm][:, :])
        c_all = S["qs0"].n
        w("qs0", c_all, "vector")
        vlink(lambda: eng.tensor_scalar(P["c_flag"][:], P["c_ones"][:, 0:1],
                                        2.0**-30, -1.0, ALU.add, ALU.add))
        vlink(lambda: eng.tensor_scalar(P["c_flag"][:], P["c_flag"][:],
                                        -(2.0**30), 1.0, ALU.mult, ALU.add))
        vlink(lambda: eng.tensor_scalar(P["c_catt"][:], P["c_ones"][:, 0:1],
                                        ATT_SCALE, None, ALU.mult))
        c_tr = S["mm"].bump()
        if R("tensor"):
            eng.wait_ge(sems["qs0"], c_all)
            for i in range(3):
                li = eng.transpose(bps[:, i * 128:i * 128 + H],
                                   btmp[0:H, i * 128:(i + 1) * 128],
                                   P["c_identf"][0:H, 0:H])
            li.then_inc(sems["mm"], 1)
        c_ev = S["vec"].bump()
        if R("vector"):
            eng.wait_ge(sems["mm"], c_tr)
            for i, nm in enumerate(["c_bqT", "c_bkT", "c_bvT"]):
                li = eng.tensor_copy(P[nm][:, 0:H], bps[:, i * 128:i * 128 + H])
            li.then_inc(sems["vec"], 1)
        for who in ["sync", "scalar", "tensor", "gpsimd"]:
            w("vec", c_ev, who)

    with nc.sbuf_tensor("b0tmp", [128, 3 * 128], F32) as btmp, \
         nc.psum_tensor("b0ps", [128, 512], F32) as bps:
        prog.b0 = (btmp, bps)
        prog.run_block(emit)
    del prog.b0


def b1_wscale(prog):
    nc, io, P, sems = prog.nc, prog.io, prog.P, prog.sems

    def emit(role, eng, ctx):
        R, S, vlink, dma, w = ctx["R"], ctx["S"], ctx["vlink"], ctx["dma"], ctx["w"]
        blk, acc, tot, ps = prog.b1
        col = 0
        vred = []
        for wname, nb, ncol in [("w1", INNER // 128, D), ("w2", D // 128, INNER)]:
            for b in range(nb):
                par = col % 2
                qn = "qw0" if par == 0 else "qw1"
                if col >= 2:
                    w("vec", vred[col - 2], "sync")
                cd = dma(qn, blk[:, par * INNER:par * INNER + ncol],
                         io[wname][b * 128:(b + 1) * 128, :])
                cv = S["vec"].bump()
                if R("vector"):
                    eng.wait_ge(sems[qn], cd)
                    eng.tensor_reduce(acc[:, col:col + 1],
                                      blk[:, par * INNER:par * INNER + ncol],
                                      AX.X, ALU.add,
                                      apply_absolute_value=True).then_inc(sems["vec"], 1)
                vred.append(cv)
                col += 1
        w("vec", S["vec"].n, "vector")
        vlink(lambda: eng.tensor_reduce(tot[:, 0:1], acc[:, 0:64], AX.X, ALU.add))
        vlink(lambda: eng.tensor_reduce(tot[:, 1:2], acc[:, 64:80], AX.X, ALU.add))
        c_tot = S["vec"].bump()
        if R("vector"):
            eng.tensor_copy(tot[:, 2:3], tot[:, 0:1]).then_inc(sems["vec"], 1)
        c_bc = S["mm"].bump()
        if R("tensor"):
            eng.wait_ge(sems["vec"], c_tot)
            eng.matmul(ps[:, 0:2], P["c_ones"][:, 0:128], tot[:, 0:2],
                       start=True, stop=True).then_inc(sems["mm"], 1)
        w("mm", c_bc, "vector")
        for j in range(2):
            vlink(lambda j=j: eng.tensor_scalar(
                P["c_wsc"][:, 2 * j + 1:2 * j + 2], ps[:, j:j + 1],
                1.0 / (INNER * D), EPS, ALU.mult, ALU.max))
            vlink(lambda j=j: eng.reciprocal(
                P["c_wsc"][:, 2 * j:2 * j + 1], P["c_wsc"][:, 2 * j + 1:2 * j + 2]))
        c_done = S["vec"].bump()
        if R("vector"):
            eng.tensor_copy(tot[:, 2:3], P["c_wsc"][:, 0:1]).then_inc(sems["vec"], 1)
        for who in ["sync", "scalar", "tensor", "gpsimd"]:
            w("vec", c_done, who)

    with nc.sbuf_tensor("b1blk", [128, 2 * INNER], F32) as blk, \
         nc.sbuf_tensor("b1acc", [128, 80], F32) as acc, \
         nc.sbuf_tensor("b1tot", [128, 3], F32) as tot, \
         nc.psum_tensor("b1ps", [128, 512], F32) as ps:
        prog.b1 = (blk, acc, tot, ps)
        prog.run_block(emit)
    del prog.b1


def b2_wquant(prog):
    nc, io, P, sems = prog.nc, prog.io, prog.P, prog.sems

    def emit(role, eng, ctx):
        R, S, vlink, dma, w = ctx["R"], ctx["S"], ctx["vlink"], ctx["dma"], ctx["w"]
        pan, tern, panT, tmpa, tmpb, trp = prog.b2
        jobs = [("w1", pc, INNER // 128) for pc in range(KD)] + \
               [("w2", pc, KD) for pc in range(KI)]
        st_after = []
        evac_cnt = []
        for jn, (wname, pc, nib) in enumerate(jobs):
            par = jn % 2
            wsccol = 0 if wname == "w1" else 2
            src = io["w1"] if wname == "w1" else io["w2"]
            dst = io["w1qT"] if wname == "w1" else io["w2qT"]
            qn = "qw0" if par == 0 else "qw1"
            qsn = "qa1" if par == 0 else "qa2"
            if jn >= 2:
                w(qsn, st_after[jn - 2], "sync")
            cd = dma(qn,
                     pan[:, par * INNER:par * INNER + nib * 128].rearrange(
                         "p (i c) -> p i c", c=128),
                     src[:, pc * 128:(pc + 1) * 128].rearrange(
                         "(i p) c -> p i c", p=128))
            c_tern = S["vec"].bump()
            if R("vector"):
                eng.wait_ge(sems[qn], cd)
                if jn >= 1:
                    eng.wait_ge(sems["mm"], tern_tr_last)
                for ch in range((nib * 128 + 2047) // 2048):
                    a = ch * 2048
                    b = min((ch + 1) * 2048, nib * 128)
                    n = b - a
                    psl = slice(par * INNER + a, par * INNER + b)
                    eng.tensor_scalar(tmpa[:, 0:n], pan[:, psl],
                                      P["c_wsc"][:, wsccol:wsccol + 1], 0.5,
                                      ALU.mult, ALU.is_gt)
                    eng.tensor_scalar(tmpb[:, 0:n], pan[:, psl],
                                      P["c_wsc"][:, wsccol:wsccol + 1], -0.5,
                                      ALU.mult, ALU.is_lt)
                    li = eng.tensor_sub(tern[:, a:b], tmpa[:, 0:n], tmpb[:, 0:n])
                li.then_inc(sems["vec"], 1)
            for ib in range(nib):
                slot = len(evac_cnt) % 2
                c_tr = S["mm"].bump()
                if R("tensor"):
                    if ib == 0:
                        eng.wait_ge(sems["vec"], c_tern)
                    if len(evac_cnt) >= 2:
                        eng.wait_ge(sems["sc"], evac_cnt[-2])
                    eng.transpose(trp[:, slot * 1024:slot * 1024 + 128],
                                  tern[:, ib * 128:(ib + 1) * 128],
                                  P["c_identb"][:]).then_inc(sems["mm"], 1)
                c_ev = S["sc"].bump()
                if R("scalar"):
                    eng.wait_ge(sems["mm"], c_tr)
                    eng.activation(
                        panT[:, par * INNER + ib * 128:par * INNER + (ib + 1) * 128],
                        trp[:, slot * 1024:slot * 1024 + 128],
                        AF.Identity).then_inc(sems["sc"], 1)
                evac_cnt.append(c_ev)
            tern_tr_last = S["mm"].n
            w("sc", evac_cnt[-1], "scalar")
            cs = dma(qsn, dst[pc * 128:(pc + 1) * 128, :],
                     panT[:, par * INNER:par * INNER + nib * 128])
            st_after.append(cs)
        for who in ["sync", "vector", "tensor", "scalar", "gpsimd"]:
            w("qa1", S["qa1"].n, who)
            w("qa2", S["qa2"].n, who)

    with nc.sbuf_tensor("b2pan", [128, 2 * INNER], F32) as pan, \
         nc.sbuf_tensor("b2tern", [128, INNER], BF16) as tern, \
         nc.sbuf_tensor("b2panT", [128, 2 * INNER], BF16) as panT, \
         nc.sbuf_tensor("b2tmpa", [128, 2048], F32) as tmpa, \
         nc.sbuf_tensor("b2tmpb", [128, 2048], F32) as tmpb, \
         nc.psum_tensor("b2ps", [128, 2048], BF16) as trp:
        prog.b2 = (pan, tern, panT, tmpa, tmpb, trp)
        prog.run_block(emit)
    del prog.b2


def ffn_pass(prog, src_name, dst_name, tag):
    nc, io, P, sems = prog.nc, prog.io, prog.P, prog.sems
    src_dram, dst_dram = io[src_name], io[dst_name]

    def emit(role, eng, ctx):
        R, S, vlink, dma, w = ctx["R"], ctx["S"], ctx["vlink"], ctx["dma"], ctx["w"]
        (g, xq, xqT, w1g, w2s, hqT, tmpa, tmpb, sgrid, ot,
         psA, psB, psC) = prog.fb
        stat = sgrid[:, 0:40]
        sqg = sgrid[:, 40:104]
        mxg = sgrid[:, 104:168]
        xt = ot[:, 0:D]
        sqdump = ot[:, D:D + 512]
        prev = {"v": 0, "s": 0, "st": 0, "stt": 0, "ffnout": 0}
        last_xq_tr = None
        for sup in range(NSUP):
            t0 = sup * SUP
            stat_done = []
            for t in range(SUP):
                if t >= 1:
                    w("vec", stat_done[t - 1][0], "sync")
                    w("sc", stat_done[t - 1][1], "sync")
                elif sup > 0:
                    w("vec", prev["v"], "sync")
                    w("sc", prev["s"], "sync")
                    w("qa0", prev["st"], "sync")
                cd = dma("qs0", xt, src_dram[(t0 + t) * 128:(t0 + t + 1) * 128, :])
                cv = S["vec"].bump()
                if R("vector"):
                    eng.wait_ge(sems["qs0"], cd)
                    eng.reduce_sum(stat[:, t:t + 1], xt, AX.X)
                    eng.reduce_max(stat[:, 4 + t:5 + t], xt, AX.X)
                    eng.tensor_reduce(stat[:, 8 + t:9 + t], xt, AX.X,
                                      ALU.min).then_inc(sems["vec"], 1)
                cs = S["sc"].bump()
                if R("scalar"):
                    eng.wait_ge(sems["qs0"], cd)
                    for ch in range(4):
                        li = eng.activation(
                            sqdump[:, 0:512],
                            xt[:, ch * 512:(ch + 1) * 512], AF.Square,
                            accum_out=sqg[:, t * 16 + ch:t * 16 + ch + 1])
                    li.then_inc(sems["sc"], 1)
                stat_done.append((cv, cs))
            w("vec", S["vec"].n, "vector")
            w("sc", S["sc"].n, "vector")
            for t in range(SUP):
                vlink(lambda t=t: eng.reduce_sum(stat[:, 12 + t:13 + t],
                                                 sqg[:, t * 16:t * 16 + 4], AX.X))
            vlink(lambda: eng.tensor_scalar(stat[:, 16:20], stat[:, 0:4],
                                            1.0 / D, None, ALU.mult))
            vlink(lambda: eng.tensor_tensor(stat[:, 36:40], stat[:, 16:20],
                                            stat[:, 16:20], ALU.mult))
            vlink(lambda: eng.scalar_tensor_tensor(
                stat[:, 36:40], stat[:, 12:16], 1.0 / D, stat[:, 36:40],
                ALU.mult, ALU.subtract))
            c_var = S["vec"].bump()
            if R("vector"):
                eng.tensor_copy(stat[:, 32:33], stat[:, 36:37]).then_inc(sems["vec"], 1)
            c_sd = S["sc"].bump()
            if R("scalar"):
                eng.wait_ge(sems["vec"], c_var)
                eng.activation(stat[:, 36:40], stat[:, 36:40],
                               AF.Sqrt).then_inc(sems["sc"], 1)
            w("sc", c_sd, "vector")
            vlink(lambda: eng.tensor_sub(stat[:, 20:24], stat[:, 4:8], stat[:, 16:20]))
            vlink(lambda: eng.tensor_sub(stat[:, 32:36], stat[:, 16:20], stat[:, 8:12]))
            vlink(lambda: eng.tensor_max(stat[:, 20:24], stat[:, 20:24], stat[:, 32:36]))
            vlink(lambda: eng.reciprocal(stat[:, 32:36], stat[:, 36:40]))
            vlink(lambda: eng.tensor_mul(stat[:, 20:24], stat[:, 20:24], stat[:, 32:36]))
            vlink(lambda: eng.tensor_scalar(stat[:, 20:24], stat[:, 20:24], EPS,
                                            None, ALU.max))
            vlink(lambda: eng.reciprocal(stat[:, 28:32], stat[:, 20:24]))
            vlink(lambda: eng.tensor_scalar(stat[:, 28:32], stat[:, 28:32], 127.0,
                                            None, ALU.mult))
            vlink(lambda: eng.tensor_mul(stat[:, 28:32], stat[:, 28:32],
                                         stat[:, 32:36]))
            vlink(lambda: eng.tensor_scalar(stat[:, 24:28], stat[:, 20:24],
                                            1.0 / 127.0, None, ALU.mult))
            vlink(lambda: eng.tensor_scalar(stat[:, 24:28], stat[:, 24:28],
                                            P["c_wsc"][:, 1:2], None, ALU.mult))
            c_xchain = S["vec"].bump()
            if R("vector"):
                eng.tensor_copy(stat[:, 32:33], stat[:, 24:25]).then_inc(sems["vec"], 1)
            xqT_done = 0
            xev = []
            qprev = 0
            for t in range(SUP):
                if t >= 1:
                    w("vec", qprev, "sync")
                else:
                    w("vec", stat_done[-1][0], "sync")
                    w("sc", stat_done[-1][1], "sync")
                cd = dma("qs0", xt, src_dram[(t0 + t) * 128:(t0 + t + 1) * 128, :])
                for grp in range(8):
                    cv = S["vec"].bump()
                    if R("vector"):
                        if grp == 0:
                            eng.wait_ge(sems["qs0"], cd)
                        if last_xq_tr is not None:
                            eng.wait_ge(sems["mm"], last_xq_tr)
                        xsl = xt[:, grp * 256:(grp + 1) * 256]
                        eng.tensor_scalar(tmpa[:, 0:256], xsl,
                                          stat[:, 16 + t:17 + t],
                                          stat[:, 28 + t:29 + t],
                                          ALU.subtract, ALU.mult)
                        eng.tensor_scalar(tmpb[:, 0:256], tmpa[:, 0:256],
                                          C64, -C64, ALU.add, ALU.add)
                        eng.tensor_scalar(tmpa[:, 0:256], tmpa[:, 0:256],
                                          C32, -C32, ALU.add, ALU.add)
                        eng.scalar_tensor_tensor(
                            xq[:, 0:256], tmpa[:, 0:256], P["c_flag"][:, 0:1],
                            tmpb[:, 0:256], ALU.mult, ALU.add).then_inc(sems["vec"], 1)
                    for kk in range(2):
                        k = grp * 2 + kk
                        nx = len(xev)
                        slot = nx % 2
                        c_tr = S["mm"].bump()
                        if R("tensor"):
                            eng.wait_ge(sems["vec"], cv)
                            if sup > 0 and t == 0 and grp == 0 and kk == 0:
                                eng.wait_ge(sems["sc"], prev["stt"])
                            if nx >= 2:
                                eng.wait_ge(sems["sc"], xev[nx - 2])
                            eng.transpose(psC[:, slot * 1024:slot * 1024 + 128],
                                          xq[:, kk * 128:(kk + 1) * 128],
                                          P["c_identb"][:]).then_inc(sems["mm"], 1)
                        c_ev = S["sc"].bump()
                        if R("scalar"):
                            eng.wait_ge(sems["mm"], c_tr)
                            eng.activation(
                                xqT[:, k * 512 + t * 128:k * 512 + (t + 1) * 128],
                                psC[:, slot * 1024:slot * 1024 + 128],
                                AF.Identity).then_inc(sems["sc"], 1)
                        xev.append(c_ev)
                    last_xq_tr = c_tr
                qprev = S["vec"].n
                xqT_done = xev[-1]
            mm_evac = {}
            w1g_used = {}
            for gg in range(16):
                par = gg % 2
                qn = "qw0" if par == 0 else "qw1"
                if gg >= 2:
                    w("mm", w1g_used[gg - 2], "sync")
                cd = dma(qn, w1g[:, par * KD * 512:(par + 1) * KD * 512].rearrange(
                             "p (k c) -> p k c", c=512),
                         io["w1qT"][:, gg * 512:(gg + 1) * 512].rearrange(
                             "(k p) c -> p k c", p=128))
                for t in range(SUP):
                    c_mm = S["mm"].bump()
                    if R("tensor"):
                        if t == 0:
                            eng.wait_ge(sems[qn], cd)
                            if gg == 0:
                                eng.wait_ge(sems["sc"], xqT_done)
                                if sup > 0:
                                    eng.wait_ge(sems["vec"], prev["ffnout"])
                        if gg >= 1:
                            eng.wait_ge(sems["sc"], mm_evac[(gg - 1, t)])
                        for k in range(KD):
                            li = eng.matmul(
                                psA[:, t * 512:(t + 1) * 512],
                                xqT[:, k * 512 + t * 128:k * 512 + (t + 1) * 128],
                                w1g[:, par * KD * 512 + k * 512:
                                    par * KD * 512 + (k + 1) * 512],
                                start=(k == 0), stop=(k == KD - 1))
                        li.then_inc(sems["mm"], 1)
                    gsl = g[:, t * INNER + gg * 512:t * INNER + (gg + 1) * 512]
                    cs = S["sc"].bump()
                    if R("scalar"):
                        eng.wait_ge(sems["mm"], c_mm)
                        if gg == 0 and t == 0:
                            eng.wait_ge(sems["vec"], c_xchain)
                        eng.activation(gsl, psA[:, t * 512:(t + 1) * 512],
                                       AF.Gelu, scale=stat[:, 24 + t:25 + t])
                        eng.activation(gsl, gsl, AF.Gelu)
                        eng.activation(sqdump[:, 0:512], gsl, AF.Square,
                                       accum_out=sqg[:, t * 16 + gg:t * 16 + gg + 1]
                                       ).then_inc(sems["sc"], 1)
                    mm_evac[(gg, t)] = cs
                    cv = S["vec"].bump()
                    if R("vector"):
                        eng.wait_ge(sems["sc"], cs)
                        eng.reduce_max(mxg[:, t * 16 + gg:t * 16 + gg + 1],
                                       gsl, AX.X,
                                       apply_absolute_value=True).then_inc(sems["vec"], 1)
                w1g_used[gg] = S["mm"].n
            w("vec", S["vec"].n, "vector")
            w("sc", S["sc"].n, "vector")
            for t in range(SUP):
                vlink(lambda t=t: eng.reduce_sum(stat[:, 0 + t:1 + t],
                                                 sqg[:, t * 16:(t + 1) * 16], AX.X))
                vlink(lambda t=t: eng.reduce_max(stat[:, 4 + t:5 + t],
                                                 mxg[:, t * 16:(t + 1) * 16], AX.X))
            c_hn = S["vec"].bump()
            if R("vector"):
                eng.tensor_copy(stat[:, 32:33], stat[:, 0:1]).then_inc(sems["vec"], 1)
            c_hsd = S["sc"].bump()
            if R("scalar"):
                eng.wait_ge(sems["vec"], c_hn)
                eng.activation(stat[:, 8:12], stat[:, 0:4],
                               AF.Sqrt).then_inc(sems["sc"], 1)
            w("sc", c_hsd, "vector")
            vlink(lambda: eng.tensor_scalar(stat[:, 8:12], stat[:, 8:12], 1e-12,
                                            None, ALU.max))
            vlink(lambda: eng.reciprocal(stat[:, 12:16], stat[:, 8:12]))
            vlink(lambda: eng.tensor_scalar(stat[:, 12:16], stat[:, 12:16],
                                            float(np.sqrt(INNER)), None, ALU.mult))
            vlink(lambda: eng.tensor_mul(stat[:, 20:24], stat[:, 4:8],
                                         stat[:, 12:16]))
            vlink(lambda: eng.tensor_scalar(stat[:, 20:24], stat[:, 20:24], EPS,
                                            None, ALU.max))
            vlink(lambda: eng.reciprocal(stat[:, 28:32], stat[:, 20:24]))
            vlink(lambda: eng.tensor_scalar(stat[:, 28:32], stat[:, 28:32],
                                            127.0, None, ALU.mult))
            vlink(lambda: eng.tensor_mul(stat[:, 28:32], stat[:, 28:32],
                                         stat[:, 12:16]))
            vlink(lambda: eng.tensor_scalar(stat[:, 24:28], stat[:, 20:24],
                                            0.5 / 127.0, None, ALU.mult))
            vlink(lambda: eng.tensor_scalar(stat[:, 24:28], stat[:, 24:28],
                                            P["c_wsc"][:, 3:4], None, ALU.mult))
            rnd = []
            for t in range(SUP):
                cv = S["vec"].bump()
                if R("vector"):
                    for ch in range(16):
                        gsl = g[:, t * INNER + ch * 512:t * INNER + (ch + 1) * 512]
                        eng.tensor_scalar(tmpa[:, 0:512], gsl,
                                          stat[:, 28 + t:29 + t], None, ALU.mult)
                        eng.tensor_scalar(tmpb[:, 0:512], tmpa[:, 0:512],
                                          C64, -C64, ALU.add, ALU.add)
                        eng.tensor_scalar(tmpa[:, 0:512], tmpa[:, 0:512],
                                          C32, -C32, ALU.add, ALU.add)
                        li = eng.scalar_tensor_tensor(
                            gsl, tmpa[:, 0:512], P["c_flag"][:, 0:1],
                            tmpb[:, 0:512], ALU.mult, ALU.add)
                    li.then_inc(sems["vec"], 1)
                rnd.append(cv)
            last_add = 0
            sts = []
            slab_gate = []
            for half in range(2):
                tA_, tB_ = t0 + 2 * half, t0 + 2 * half + 1
                xres = []
                for tt in range(2):
                    if half == 0:
                        w("vec", rnd[-1], "gpsimd")
                        if sup > 0:
                            w("qa0", prev["st"], "gpsimd")
                    else:
                        w("qa0", sts[-1], "gpsimd")
                    xres.append(dma("qg0", ot[:, tt * D:(tt + 1) * D],
                                    src_dram[(tA_ + tt) * 128:(tA_ + tt + 1) * 128, :]))
                for dmh in range(2):
                    tr_evs = []
                    mm2_end = None
                    for icb in range(32):
                        par2 = len(slab_gate) % 2
                        qn2 = "qw0" if par2 == 0 else "qw1"
                        if len(slab_gate) >= 2:
                            w("mm", slab_gate[-2], "sync")
                        cd = dma(qn2,
                                 w2s[:, par2 * 2 * 1024:(par2 + 1) * 2 * 1024].rearrange(
                                     "p (i c) -> p i c", c=1024),
                                 io["w2qT"][icb * 256:(icb + 1) * 256,
                                            dmh * 1024:(dmh + 1) * 1024].rearrange(
                                     "(i p) c -> p i c", p=128))
                        for ici in range(2):
                            ic = icb * 2 + ici
                            evs2 = []
                            if dmh == 0:
                                for tt in range(2):
                                    t = tA_ + tt - t0
                                    ntr = len(tr_evs)
                                    slot = ntr % 2
                                    c_tr = S["mm"].bump()
                                    if R("tensor"):
                                        if ic == 0 and tt == 0:
                                            eng.wait_ge(sems["vec"],
                                                        rnd[2 * half])
                                            eng.wait_ge(sems["vec"],
                                                        rnd[2 * half + 1])
                                            if half == 0:
                                                eng.wait_ge(sems["sc"],
                                                            mm_evac[(15, SUP - 1)])
                                        if ntr >= 2:
                                            eng.wait_ge(sems["sc"],
                                                        tr_evs[ntr - 2])
                                        eng.transpose(
                                            psB[:, slot * 512:slot * 512 + 128],
                                            g[:, t * INNER + ic * 128:
                                              t * INNER + (ic + 1) * 128],
                                            P["c_identf"][:]).then_inc(sems["mm"], 1)
                                    c_ev = S["sc"].bump()
                                    if R("scalar"):
                                        eng.wait_ge(sems["mm"], c_tr)
                                        eng.activation(
                                            w1g[:, ic * 256 + tt * 128:
                                                ic * 256 + (tt + 1) * 128],
                                            psB[:, slot * 512:slot * 512 + 128],
                                            AF.Identity).then_inc(sems["sc"], 1)
                                    evs2.append(c_ev)
                                    tr_evs.append(c_ev)
                            c_mm = S["mm"].bump()
                            if R("tensor"):
                                if dmh == 0:
                                    eng.wait_ge(sems["sc"], evs2[-1])
                                if ici == 0:
                                    eng.wait_ge(sems[qn2], cd)
                                if ic == 0:
                                    eng.wait_ge(sems["vec"], last_add)
                                for tt in range(2):
                                    for nn in range(2):
                                        li = eng.matmul(
                                            psA[:, (tt * 2 + nn) * 512:
                                                (tt * 2 + nn + 1) * 512],
                                            w1g[:, ic * 256 + tt * 128:
                                                ic * 256 + (tt + 1) * 128],
                                            w2s[:, par2 * 2 * 1024 + ici * 1024 +
                                                nn * 512:
                                                par2 * 2 * 1024 + ici * 1024 +
                                                (nn + 1) * 512],
                                            start=(ic == 0), stop=(ic == KI - 1))
                                li.then_inc(sems["mm"], 1)
                            mm2_end = S["mm"].n
                        slab_gate.append(S["mm"].n)
                    for tt in range(2):
                        t = tA_ + tt - t0
                        for nn in range(2):
                            c_add = S["vec"].bump()
                            if R("vector"):
                                eng.wait_ge(sems["mm"], mm2_end)
                                if dmh == 0 and tt == 0 and nn == 0:
                                    eng.wait_ge(sems["qg0"], xres[-1])
                                osl = ot[:, tt * D + dmh * 1024 + nn * 512:
                                         tt * D + dmh * 1024 + (nn + 1) * 512]
                                eng.scalar_tensor_tensor(
                                    osl, psA[:, (tt * 2 + nn) * 512:
                                             (tt * 2 + nn + 1) * 512],
                                    stat[:, 24 + t:25 + t], osl,
                                    ALU.mult, ALU.add).then_inc(sems["vec"], 1)
                            last_add = c_add
                w("vec", last_add, "scalar")
                for tt in range(2):
                    sts.append(dma("qa0",
                                   dst_dram[(tA_ + tt) * 128:(tA_ + tt + 1) * 128, :],
                                   ot[:, tt * D:(tt + 1) * D]))
            prev = {"v": S["vec"].n, "s": S["sc"].n, "st": sts[-1],
                    "stt": S["sc"].n, "ffnout": last_add}
        for who in ["sync", "vector", "tensor", "scalar", "gpsimd"]:
            w("qa0", S["qa0"].n, who)

    with contextlib.ExitStack() as stk:
        sb, pt = prog.nc.sbuf_tensor, prog.nc.psum_tensor
        prog.fb = (
            stk.enter_context(sb(f"g{tag}", [128, SUP * INNER], F32)),
            stk.enter_context(sb(f"xq{tag}", [128, 256], BF16)),
            stk.enter_context(sb(f"xqT{tag}", [128, KD * 512], BF16)),
            stk.enter_context(sb(f"w1g{tag}", [128, 2 * KD * 512], BF16)),
            stk.enter_context(sb(f"w2s{tag}", [128, 2 * 2 * 1024], BF16)),
            stk.enter_context(sb(f"hqT{tag}", [128, 2 * 256], BF16)),
            stk.enter_context(sb(f"tA{tag}", [128, 512], F32)),
            stk.enter_context(sb(f"tB{tag}", [128, 512], F32)),
            stk.enter_context(sb(f"sg{tag}", [128, 168], F32)),
            stk.enter_context(sb(f"ot{tag}", [128, 2 * D], F32)),
            stk.enter_context(pt(f"pA{tag}", [128, SUP * 512], F32)),
            stk.enter_context(pt(f"pB{tag}", [128, 1024], F32)),
            stk.enter_context(pt(f"pC{tag}", [128, 2048], BF16)),
        )
        prog.run_block(emit)
    del prog.fb


def b4_qkv(prog, src_name):
    nc, io, P, sems = prog.nc, prog.io, prog.P, prog.sems
    src_dram = io[src_name]

    def emit(role, eng, ctx):
        R, S, vlink, dma, w = ctx["R"], ctx["S"], ctx["vlink"], ctx["dma"], ctx["w"]
        (xt, xnb, xnT, wb, stage, sgrid, psQ, psK, psV, psT) = prog.b4
        if B4_SERIAL:
            _dma0 = ctx["dma"]
            def dma(q, dst, srcap):
                for sname in ["vec", "sc", "mm"]:
                    w(sname, S[sname].n, {"qs0": "sync", "qs1": "sync",
                                          "qa0": "scalar", "qg0": "gpsimd",
                                          "qk0": "sync", "qk1": "sync"}[q])
                return _dma0(q, dst, srcap)
            _sb = Sem.bump
            def serial_guard(which):
                for sname in ["vec", "sc", "mm"]:
                    if sname != which:
                        w(sname, S[sname].n, {"vec": "vector", "sc": "scalar",
                                              "mm": "tensor"}[which])
            class _SemWrap:
                pass
            ctx["serial_guard"] = serial_guard
        else:
            ctx["serial_guard"] = lambda which: None
        serial_guard = ctx["serial_guard"]
        stat = sgrid[:, 0:40]
        sqg = sgrid[:, 40:104]
        GT = 8
        prevv = {"v": 0, "s": 0}
        for gp in range(2):
            t0 = gp * GT
            stat_done = []
            for t in range(GT):
                if t >= 1:
                    w("vec", stat_done[t - 1][0], "sync")
                    w("sc", stat_done[t - 1][1], "sync")
                elif gp > 0:
                    w("vec", prevv["v"], "sync")
                    w("sc", prevv["s"], "sync")
                cd = dma("qs0", xt[:, 0:D],
                         src_dram[(t0 + t) * 128:(t0 + t + 1) * 128, :])
                serial_guard("vec")
                cv = S["vec"].bump()
                if R("vector"):
                    eng.wait_ge(sems["qs0"], cd)
                    eng.reduce_sum(stat[:, t:t + 1], xt[:, 0:D],
                                   AX.X).then_inc(sems["vec"], 1)
                cs = S["sc"].bump()
                if R("scalar"):
                    eng.wait_ge(sems["qs0"], cd)
                    for ch in range(4):
                        li = eng.activation(
                            stage[:, 0:512],
                            xt[:, ch * 512:(ch + 1) * 512], AF.Square,
                            accum_out=sqg[:, t * 4 + ch:t * 4 + ch + 1])
                    li.then_inc(sems["sc"], 1)
                stat_done.append((cv, cs))
            w("vec", S["vec"].n, "vector")
            w("sc", S["sc"].n, "vector")
            for t in range(GT):
                vlink(lambda t=t: eng.reduce_sum(stat[:, 16 + t:17 + t],
                                                 sqg[:, t * 4:t * 4 + 4], AX.X))
            vlink(lambda: eng.tensor_scalar(stat[:, 0:8], stat[:, 0:8],
                                            1.0 / D, None, ALU.mult))
            vlink(lambda: eng.tensor_tensor(stat[:, 24:32], stat[:, 0:8],
                                            stat[:, 0:8], ALU.mult))
            vlink(lambda: eng.scalar_tensor_tensor(
                stat[:, 24:32], stat[:, 16:24], 1.0 / D, stat[:, 24:32],
                ALU.mult, ALU.subtract))
            vlink(lambda: eng.tensor_scalar(stat[:, 24:32], stat[:, 24:32],
                                            EPS, None, ALU.add))
            c_var = S["vec"].bump()
            if R("vector"):
                eng.tensor_copy(stat[:, 32:33], stat[:, 24:25]).then_inc(sems["vec"], 1)
            c_sd = S["sc"].bump()
            if R("scalar"):
                eng.wait_ge(sems["vec"], c_var)
                eng.activation(stat[:, 24:32], stat[:, 24:32],
                               AF.Sqrt).then_inc(sems["sc"], 1)
            w("sc", c_sd, "vector")
            vlink(lambda: eng.reciprocal(stat[:, 8:16], stat[:, 24:32]))
            xev = []
            qprev = 0
            for t in range(GT):
                if t >= 1:
                    w("vec", qprev, "sync")
                else:
                    w("vec", stat_done[-1][0], "sync")
                    w("sc", stat_done[-1][1], "sync")
                cd = dma("qs0", xt[:, 0:D],
                         src_dram[(t0 + t) * 128:(t0 + t + 1) * 128, :])
                serial_guard("vec")
                cv = S["vec"].bump()
                if R("vector"):
                    eng.wait_ge(sems["qs0"], cd)
                    eng.tensor_scalar(xnb[:, 0:D], xt[:, 0:D], stat[:, t:t + 1],
                                      stat[:, 8 + t:9 + t],
                                      ALU.subtract, ALU.mult).then_inc(sems["vec"], 1)
                qprev = cv
                for k in range(KD):
                    nx = len(xev)
                    slot = nx % 2
                    serial_guard("mm")
                    c_tr = S["mm"].bump()
                    if R("tensor"):
                        eng.wait_ge(sems["vec"], cv)
                        if nx >= 2:
                            eng.wait_ge(sems["vec"], xev[nx - 2])
                        eng.transpose(psT[:, slot * 512:slot * 512 + 128],
                                      xnb[:, k * 128:(k + 1) * 128],
                                      P["c_identf"][:]).then_inc(sems["mm"], 1)
                    serial_guard("vec")
                    c_ev = S["vec"].bump()
                    if R("vector"):
                        eng.wait_ge(sems["mm"], c_tr)
                        eng.tensor_copy(
                            xnT[:, k * 1024 + t * 128:k * 1024 + (t + 1) * 128],
                            psT[:, slot * 512:slot * 512 + 128]).then_inc(sems["vec"], 1)
                    xev.append(c_ev)
            xnT_done = xev[-1]
            if DBG_B4:
                w("vec", xnT_done, "gpsimd")
                dma("qg0", io["dbgxnT"][gp * 128:(gp + 1) * 128, :],
                    xnT[:, 0:KD * 1024])
                dma("qg0", io["dbgstat"][gp * 128:(gp + 1) * 128, :],
                    sgrid[:, 0:104])
            wprev = {}
            stq_gate = []
            for h in range(H):
                parw = h % 2
                if h >= 2:
                    w("mm", wprev[(h - 2, "mm")], "sync")
                qwn = "qw0" if parw == 0 else "qw1"
                cds = []
                for wi, wn in enumerate(["wq", "wk", "wv"]):
                    cds.append(dma(
                        qwn,
                        wb[:, parw * 3 * KD * 128 + wi * KD * 128:
                           parw * 3 * KD * 128 + (wi + 1) * KD * 128].rearrange(
                            "p (k c) -> p k c", c=128),
                        io[wn][h * D:(h + 1) * D, :].rearrange(
                            "(k p) c -> p k c", p=128)))
                for wi, (ps_, dstd, bT) in enumerate(
                        [(psQ, io["qsc"], "c_bqT"), (psK, io["ksc"], "c_bkT")]):
                    serial_guard("mm")
                    c_mm = S["mm"].bump()
                    if R("tensor"):
                        eng.wait_ge(sems[qwn], cds[-1])
                        if h == 0:
                            eng.wait_ge(sems["vec"], xnT_done)
                        if h >= 1:
                            eng.wait_ge(sems["sc"], wprev[(h - 1, "sc")])
                        for nn in range(2):
                            for k in range(KD):
                                li = eng.matmul(
                                    ps_[:, nn * 512:(nn + 1) * 512],
                                    wb[:, parw * 3 * KD * 128 + wi * KD * 128 +
                                       k * 128:parw * 3 * KD * 128 +
                                       wi * KD * 128 + (k + 1) * 128],
                                    xnT[:, k * 1024 + nn * 512:
                                        k * 1024 + (nn + 1) * 512],
                                    start=(k == 0), stop=(k == KD - 1))
                        li.then_inc(sems["mm"], 1)
                    serial_guard("sc")
                    cs = S["sc"].bump()
                    if R("scalar"):
                        eng.wait_ge(sems["mm"], c_mm)
                        if len(stq_gate) >= 3:
                            eng.wait_ge(sems[stq_gate[-3][0]], stq_gate[-3][1])
                        eng.activation(stage[:, wi * 1024:(wi + 1) * 1024],
                                       ps_[:, 0:1024], AF.Identity,
                                       bias=P[bT][:, h:h + 1]).then_inc(sems["sc"], 1)
                    w("sc", cs, "scalar")
                    qan = ["qa0", "qa1"][wi]
                    stq_gate.append((qan, dma(qan, dstd[h * 128:(h + 1) * 128,
                                              gp * 1024:(gp + 1) * 1024],
                                              stage[:, wi * 1024:(wi + 1) * 1024])))
                vev = []
                for t in range(GT):
                    slotv = t % 2
                    serial_guard("mm")
                    c_mm = S["mm"].bump()
                    if R("tensor"):
                        if t >= 2:
                            eng.wait_ge(sems["sc"], vev[t - 2])
                        if t <= 1 and h >= 1:
                            eng.wait_ge(sems["sc"], wprev[(h - 1, "sc")])
                        for k in range(KD):
                            li = eng.matmul(
                                psV[:, slotv * 512:slotv * 512 + 128],
                                xnT[:, k * 1024 + t * 128:k * 1024 + (t + 1) * 128],
                                wb[:, parw * 3 * KD * 128 + 2 * KD * 128 +
                                   k * 128:parw * 3 * KD * 128 +
                                   2 * KD * 128 + (k + 1) * 128],
                                start=(k == 0), stop=(k == KD - 1))
                        li.then_inc(sems["mm"], 1)
                    serial_guard("sc")
                    cs = S["sc"].bump()
                    if R("scalar"):
                        eng.wait_ge(sems["mm"], c_mm)
                        if t == 0 and len(stq_gate) >= 3:
                            eng.wait_ge(sems[stq_gate[-3][0]], stq_gate[-3][1])
                        eng.activation(stage[:, 2048 + t * 128:2048 + (t + 1) * 128],
                                       psV[:, slotv * 512:slotv * 512 + 128],
                                       AF.Identity).then_inc(sems["sc"], 1)
                    vev.append(cs)
                w("sc", vev[-1], "scalar")
                stq_gate.append(("qa2", dma(
                    "qa2", io["vsc"][h * T + gp * 1024:h * T + (gp + 1) * 1024, :
                                     ].rearrange("(t p) c -> p t c", p=128),
                    stage[:, 2048:2048 + GT * 128].rearrange(
                        "p (t c) -> p t c", c=128))))
                wprev[(h, "mm")] = S["mm"].n
                wprev[(h, "sc")] = S["sc"].n
            prevv = {"v": S["vec"].n, "s": S["sc"].n}
        for who in ["sync", "vector", "tensor", "scalar", "gpsimd"]:
            w("qa0", S["qa0"].n, who)
            w("qa1", S["qa1"].n, who)
            w("qa2", S["qa2"].n, who)

    with contextlib.ExitStack() as stk:
        sb, pt = prog.nc.sbuf_tensor, prog.nc.psum_tensor
        prog.b4 = (
            stk.enter_context(sb("b4xt", [128, D], F32)),
            stk.enter_context(sb("b4xn", [128, D], F32)),
            stk.enter_context(sb("b4xnT", [128, KD * 1024], F32)),
            stk.enter_context(sb("b4wb", [128, 2 * 3 * KD * 128], F32)),
            stk.enter_context(sb("b4stage", [128, 3072], F32)),
            stk.enter_context(sb("b4sg", [128, 40 + 64], F32)),
            stk.enter_context(pt("b4psQ", [128, 1024], F32)),
            stk.enter_context(pt("b4psK", [128, 1024], F32)),
            stk.enter_context(pt("b4psV", [128, 1024], F32)),
            stk.enter_context(pt("b4psT", [128, 1024], F32)),
        )
        prog.run_block(emit)
    del prog.b4


def b5_attn(prog, src_name, dst_name):
    nc, io, P, sems = prog.nc, prog.io, prog.P, prog.sems
    src_dram, dst_dram = io[src_name], io[dst_name]

    def emit(role, eng, ctx):
        R, S, vlink, dma, w = ctx["R"], ctx["S"], ctx["vlink"], ctx["dma"], ctx["w"]
        (kh, vh, qh, pbuf, ptb, ystr, dstat, diag, caus, borep,
         xsl, ost, wos, psA, psB, psD, psE) = prog.b5
        c_c = dma("qs0", caus[:], io["caus"][:, :])
        c_b = dma("qs0", borep[0:1, 0:D], io["bo"][:, :])
        c_bc = S["mm"].bump()
        if R("tensor"):
            eng.wait_ge(sems["qs0"], max(c_c, c_b))
            for nn in range(4):
                li = eng.matmul(psA[:, nn * 512:(nn + 1) * 512],
                                P["c_ones"][0:1, 0:128],
                                borep[0:1, nn * 512:(nn + 1) * 512],
                                start=True, stop=True)
            li.then_inc(sems["mm"], 1)
        c_bv = S["vec"].bump()
        if R("vector"):
            eng.wait_ge(sems["mm"], c_bc)
            eng.tensor_copy(borep[:], psA[:, 0:2048]).then_inc(sems["vec"], 1)
        hk_prev = {}
        last_wo_add = c_bv
        ost_gate = []
        xsl_gate = []
        for sp in range(NSTRIPE):
            q0 = sp * (STRIPE // 128)
            for h in range(H):
                hg = sp * H + h
                parh = hg % 2
                if hg >= 2:
                    w("mm", hk_prev[(hg - 2, "mm")], "sync")
                    w("sc", hk_prev[(hg - 2, "sc")], "sync")
                qk = "qk0" if parh == 0 else "qk1"
                dma(qk, kh[:, parh * T:(parh + 1) * T],
                    io["ksc"][h * 128:(h + 1) * 128, :])
                dma(qk, vh[:, parh * NT * 128:(parh + 1) * NT * 128].rearrange(
                        "p (t c) -> p t c", c=128),
                    io["vsc"][h * T:(h + 1) * T, :].rearrange(
                        "(t p) c -> p t c", p=128))
                cdq = dma(qk, qh[:, parh * STRIPE:(parh + 1) * STRIPE],
                          io["qsc"][h * 128:(h + 1) * 128,
                                    sp * STRIPE:(sp + 1) * STRIPE])
                first_of_head = True
                for qi_l in range(STRIPE // 128):
                    qi = q0 + qi_l
                    nkeys = (qi + 1) * 128
                    nck = (nkeys + 511) // 512
                    c_s = S["mm"].bump()
                    if R("tensor"):
                        if first_of_head:
                            eng.wait_ge(sems[qk], cdq)
                            if sp == 0 and h == 0:
                                eng.wait_ge(sems["vec"], c_bv)
                        eng.wait_ge(sems["sc"], S["sc"].n)
                        if first_of_head and h == 0:
                            eng.wait_ge(sems["vec"], last_wo_add)
                        for ck in range(nck):
                            a = ck * 512
                            b = min(nkeys, a + 512)
                            li = eng.matmul(
                                psA[:, a:b],
                                qh[:, parh * STRIPE + qi_l * 128:
                                   parh * STRIPE + (qi_l + 1) * 128],
                                kh[:, parh * T + a:parh * T + b],
                                start=True, stop=True)
                        li.then_inc(sems["mm"], 1)
                    first_of_head = False
                    c_mask = S["vec"].bump()
                    if R("vector"):
                        eng.wait_ge(sems["mm"], c_s)
                        eng.tensor_add(psA[:, qi * 128:nkeys],
                                       psA[:, qi * 128:nkeys],
                                       caus[:]).then_inc(sems["vec"], 1)
                    vlink(lambda: eng.reduce_max(dstat[:, 0:1],
                                                 psA[:, 0:nkeys], AX.X))
                    vlink(lambda: eng.tensor_scalar(dstat[:, 1:2], dstat[:, 0:1],
                                                    -ATT_SCALE, None, ALU.mult))
                    c_negm = S["vch"].n
                    cs_exp = S["sc"].bump()
                    if R("scalar"):
                        eng.wait_ge(sems["vch"], c_negm)
                        eng.activation(pbuf[:, 0:nkeys], psA[:, 0:nkeys],
                                       AF.Exp, bias=dstat[:, 1:2],
                                       scale=ATT_SCALE,
                                       accum_out=dstat[:, 2:3]).then_inc(sems["sc"], 1)
                    w("sc", cs_exp, "vector")
                    vlink(lambda: eng.reciprocal(dstat[:, 3:4], dstat[:, 2:3]))
                    vlink(lambda: eng.tensor_scalar(pbuf[:, 0:nkeys],
                                                    pbuf[:, 0:nkeys],
                                                    dstat[:, 3:4], None, ALU.mult))
                    c_diag = S["vch"].n
                    pev = []
                    for kc in range(qi + 1):
                        slot = kc % 3
                        tdst = (psB[:, slot * 512:slot * 512 + 128] if slot < 2
                                else psE[:, 0:128])
                        c_tr = S["mm"].bump()
                        if R("tensor"):
                            if kc == 0:
                                eng.wait_ge(sems["vch"], c_diag)
                            if kc >= 3:
                                eng.wait_ge(sems["vec"], pev[kc - 3])
                            eng.transpose(tdst,
                                          pbuf[:, kc * 128:(kc + 1) * 128],
                                          P["c_identf"][:]).then_inc(sems["mm"], 1)
                        c_ev = S["vec"].bump()
                        if R("vector"):
                            eng.wait_ge(sems["mm"], c_tr)
                            eng.tensor_copy(ptb[:, kc * 128:(kc + 1) * 128],
                                            tdst).then_inc(sems["vec"], 1)
                        pev.append(c_ev)
                    c_av = S["mm"].bump()
                    if R("tensor"):
                        eng.wait_ge(sems["vec"], pev[-1])
                        for kc in range(qi + 1):
                            li = eng.matmul(
                                psD[:, 0:128],
                                vh[:, parh * NT * 128 + kc * 128:
                                   parh * NT * 128 + (kc + 1) * 128],
                                ptb[:, kc * 128:(kc + 1) * 128],
                                start=(kc == 0), stop=(kc == qi))
                        li.then_inc(sems["mm"], 1)
                    cs_y = S["sc"].bump()
                    if R("scalar"):
                        eng.wait_ge(sems["mm"], c_av)
                        eng.activation(ystr[:, h * STRIPE + qi_l * 128:
                                            h * STRIPE + (qi_l + 1) * 128],
                                       psD[:, 0:128], AF.Identity,
                                       bias=P["c_bvT"][:, h:h + 1]).then_inc(sems["sc"], 1)
                hk_prev[(hg, "mm")] = S["mm"].n
                hk_prev[(hg, "sc")] = S["sc"].n
            c_heads = S["sc"].n
            wo_banks = [psA[:, i * 512:(i + 1) * 512] for i in range(4)] + \
                       [psB[:, i * 512:(i + 1) * 512] for i in range(2)] + \
                       [psD[:, 0:512], psE[:, 0:512]]
            wo_gate = []
            for nn in range(4):
                wo_last = None
                for k in range(KD):
                    j = len(wo_gate)
                    parw = j % 2
                    qwn = "qw0" if parw == 0 else "qw1"
                    if j >= 2:
                        w("mm", wo_gate[j - 2], "sync")
                    cd = dma(qwn, wos[:, parw * 512:(parw + 1) * 512],
                             io["wo"][k * 128:(k + 1) * 128,
                                      nn * 512:(nn + 1) * 512])
                    c_mm = S["mm"].bump()
                    if R("tensor"):
                        eng.wait_ge(sems[qwn], cd)
                        if k == 0:
                            if nn == 0:
                                eng.wait_ge(sems["sc"], c_heads)
                            eng.wait_ge(sems["vec"], last_wo_add)
                        for qb in range(8):
                            li = eng.matmul(
                                wo_banks[qb],
                                ystr[:, k * STRIPE + qb * 128:
                                     k * STRIPE + (qb + 1) * 128],
                                wos[:, parw * 512:(parw + 1) * 512],
                                start=(k == 0), stop=(k == KD - 1))
                        li.then_inc(sems["mm"], 1)
                    wo_gate.append(S["mm"].n)
                wo_last = wo_gate[-1]
                for qb in range(8):
                    parx = len(ost_gate) % 2
                    qgn = "qg0" if parx == 0 else "qg1"
                    qan = "qa0" if parx == 0 else "qa1"
                    if len(xsl_gate) >= 2:
                        w("vec", xsl_gate[-2], "gpsimd")
                    cdx = dma(qgn, xsl[:, parx * 512:(parx + 1) * 512],
                              src_dram[(q0 + qb) * 128:(q0 + qb + 1) * 128,
                                       nn * 512:(nn + 1) * 512])
                    c_a1 = S["vec"].bump(2)
                    if R("vector"):
                        eng.wait_ge(sems["mm"], wo_last)
                        eng.wait_ge(sems[qgn], cdx)
                        if len(ost_gate) >= 2:
                            eng.wait_ge(sems[qan], ost_gate[-2])
                        eng.tensor_add(ost[:, parx * 512:(parx + 1) * 512],
                                       wo_banks[qb],
                                       xsl[:, parx * 512:(parx + 1) * 512])
                        eng.tensor_add(ost[:, parx * 512:(parx + 1) * 512],
                                       ost[:, parx * 512:(parx + 1) * 512],
                                       borep[:, nn * 512:(nn + 1) * 512]
                                       ).then_inc(sems["vec"], 2)
                    last_wo_add = c_a1
                    xsl_gate.append(c_a1)
                    w("vec", c_a1, "scalar")
                    cst = dma(qan,
                              dst_dram[(q0 + qb) * 128:(q0 + qb + 1) * 128,
                                       nn * 512:(nn + 1) * 512],
                              ost[:, parx * 512:(parx + 1) * 512])
                    ost_gate.append(cst)
        for who in ["sync", "vector", "tensor", "scalar", "gpsimd"]:
            w("qa0", S["qa0"].n, who)
            w("qa1", S["qa1"].n, who)

    with contextlib.ExitStack() as stk:
        sb, pt = prog.nc.sbuf_tensor, prog.nc.psum_tensor
        prog.b5 = (
            stk.enter_context(sb("b5kh", [128, 2 * T], F32)),
            stk.enter_context(sb("b5vh", [128, 2 * NT * 128], F32)),
            stk.enter_context(sb("b5qh", [128, 2 * STRIPE], F32)),
            stk.enter_context(sb("b5p", [128, T], F32)),
            stk.enter_context(sb("b5pt", [128, T], F32)),
            stk.enter_context(sb("b5y", [128, H * STRIPE], F32)),
            stk.enter_context(sb("b5ds", [128, 8], F32)),
            stk.enter_context(sb("b5diag", [128, 128], F32)),
            stk.enter_context(sb("b5caus", [128, 128], F32)),
            stk.enter_context(sb("b5bo", [128, D], F32)),
            stk.enter_context(sb("b5xs", [128, 2 * 512], F32)),
            stk.enter_context(sb("b5os", [128, 2 * 512], F32)),
            stk.enter_context(sb("b5wo", [128, 2 * 512], F32)),
            stk.enter_context(pt("b5psA", [128, 2048], F32)),
            stk.enter_context(pt("b5psB", [128, 1024], F32)),
            stk.enter_context(pt("b5psD", [128, 512], F32)),
            stk.enter_context(pt("b5psE", [128, 512], F32)),
        )
        prog.run_block(emit)
    del prog.b5


_CACHE = {}


def _build():
    if "nc" in _CACHE:
        return _CACHE["nc"]
    prog = Prog()
    b0_consts(prog)
    b1_wscale(prog)
    b2_wquant(prog)
    if STAGE == 1:
        ffn_pass(prog, "x", "y", "f1")
    elif STAGE == 2:
        ffn_pass(prog, "x", "xr1", "f1")
        b4_qkv(prog, "xr1")
        b5_attn(prog, "xr1", "y")
    else:
        ffn_pass(prog, "x", "xr1", "f1")
        b4_qkv(prog, "xr1")
        b5_attn(prog, "xr1", "xr2")
        ffn_pass(prog, "xr2", "y", "f3")
    prog.es.close()
    _CACHE["nc"] = prog.nc
    return prog.nc


def kernel(**inputs):
    nc = _build()
    x = np.ascontiguousarray(np.asarray(inputs["L0_attn_x"], np.float32))
    sh = dict(
        w1=np.ascontiguousarray(np.asarray(inputs["L0_ffn_w1"], np.float32)),
        w2=np.ascontiguousarray(np.asarray(inputs["L0_ffn_w2"], np.float32)),
        wq=np.ascontiguousarray(np.asarray(inputs["L0_attn_Wq"], np.float32).reshape(H * D, HD)),
        wk=np.ascontiguousarray(np.asarray(inputs["L0_attn_Wk"], np.float32).reshape(H * D, HD)),
        wv=np.ascontiguousarray(np.asarray(inputs["L0_attn_Wv"], np.float32).reshape(H * D, HD)),
        wo=np.ascontiguousarray(np.asarray(inputs["L0_attn_Wo"], np.float32)),
        bq=np.ascontiguousarray(np.asarray(inputs["L0_attn_bq"], np.float32)),
        bk=np.ascontiguousarray(np.asarray(inputs["L0_attn_bk"], np.float32)),
        bv=np.ascontiguousarray(np.asarray(inputs["L0_attn_bv"], np.float32)),
        bo=np.ascontiguousarray(np.asarray(inputs["L0_attn_bo"], np.float32).reshape(1, D)),
        identf=np.eye(128, dtype=np.float32),
        identb=np.eye(128).astype(ml_dtypes.bfloat16),
        ones=np.ones((128, 128), np.float32),
        caus=np.ascontiguousarray(np.where(
            np.arange(128)[:, None] >= np.arange(128)[None, :],
            np.float32(0), np.finfo(np.float32).min).astype(np.float32)),
    )
    in_maps = [dict(sh, x=np.ascontiguousarray(x[c])) for c in range(N_CORES)]
    res = run_bass_kernel_spmd(nc, in_maps, core_ids=list(range(N_CORES)))
    return np.stack([r["y"] for r in res.results], axis=0)


# version 07 (final)
# speedup vs baseline: 1.0858x; 1.0165x over previous
import contextlib
import numpy as np
import ml_dtypes
import concourse.bass as bass
import concourse.mybir as mybir
from concourse.bass_utils import run_bass_kernel_spmd

F32 = mybir.dt.float32
BF16 = mybir.dt.bfloat16
AF = mybir.ActivationFunctionType
ALU = mybir.AluOpType
AX = mybir.AxisListType

T = 2048
D = 2048
INNER = 8192
H = 16
HD = 128
NT = T // 128
KD = D // 128
KI = INNER // 128
SUP = 4
NSUP = NT // SUP
STRIPE = 1024
NSTRIPE = T // STRIPE
EPS = 1e-5
C32 = float(1.5 * 2**23)
C64 = float(1.5 * 2**52)
ATT_SCALE = float(HD) ** -0.5
N_CORES = 8

STAGE = 3
DBG_SPILL = False
B4_SERIAL = False
DBG_B4 = False


class Sem:
    def __init__(self, h):
        self.h = h
        self.n = 0

    def bump(self, k=1):
        self.n += k
        return self.n


class Prog:
    def __init__(self):
        nc = bass.Bass("TRN2", target_bir_lowering=False,
                       detect_race_conditions=False)
        self.nc = nc
        self.es = contextlib.ExitStack()
        io = {}
        io["x"] = nc.dram_tensor("x", [T, D], F32, kind="ExternalInput")
        io["w1"] = nc.dram_tensor("w1", [INNER, D], F32, kind="ExternalInput")
        io["w2"] = nc.dram_tensor("w2", [D, INNER], F32, kind="ExternalInput")
        io["wq"] = nc.dram_tensor("wq", [H * D, HD], F32, kind="ExternalInput")
        io["wk"] = nc.dram_tensor("wk", [H * D, HD], F32, kind="ExternalInput")
        io["wv"] = nc.dram_tensor("wv", [H * D, HD], F32, kind="ExternalInput")
        io["wo"] = nc.dram_tensor("wo", [D, D], F32, kind="ExternalInput")
        io["bq"] = nc.dram_tensor("bq", [H, HD], F32, kind="ExternalInput")
        io["bk"] = nc.dram_tensor("bk", [H, HD], F32, kind="ExternalInput")
        io["bv"] = nc.dram_tensor("bv", [H, HD], F32, kind="ExternalInput")
        io["bo"] = nc.dram_tensor("bo", [1, D], F32, kind="ExternalInput")
        io["identf"] = nc.dram_tensor("identf", [128, 128], F32, kind="ExternalInput")
        io["identb"] = nc.dram_tensor("identb", [128, 128], BF16, kind="ExternalInput")
        io["ones"] = nc.dram_tensor("ones", [128, 128], F32, kind="ExternalInput")
        io["caus"] = nc.dram_tensor("caus", [128, 128], F32, kind="ExternalInput")
        io["y"] = nc.dram_tensor("y", [T, D], F32, kind="ExternalOutput")
        io["w1qT"] = nc.dram_tensor("w1qT", [D, INNER], BF16)
        io["w2qT"] = nc.dram_tensor("w2qT", [INNER, D], BF16)
        io["xr1"] = nc.dram_tensor("xr1", [T, D], F32,
                                   kind="ExternalOutput" if DBG_B4 else "Internal")
        io["xr2"] = nc.dram_tensor("xr2", [T, D], F32)
        kk = "ExternalOutput" if DBG_SPILL else "Internal"
        if DBG_B4:
            io["dbgxnT"] = nc.dram_tensor("dbgxnT", [2 * 128, KD * 1024], F32,
                                          kind="ExternalOutput")
            io["dbgstat"] = nc.dram_tensor("dbgstat", [2 * 128, 104], F32,
                                           kind="ExternalOutput")
        io["qsc"] = nc.dram_tensor("qsc", [H * HD, T], F32, kind=kk)
        io["ksc"] = nc.dram_tensor("ksc", [H * HD, T], F32, kind=kk)
        io["vsc"] = nc.dram_tensor("vsc", [H * T, HD], F32, kind=kk)
        self.io = io

        self.sems = {n: self.es.enter_context(nc.semaphore(n)) for n in
                     ["qs0", "qs1", "qw0", "qw1", "qa0", "qa1", "qa2",
                      "qg0", "qg1", "qk0", "qk1", "vec", "vch", "mm", "sc"]}
        self.counts = {k: 0 for k in self.sems}

        P = {}
        def psb(name, shape, dtype=F32):
            P[name] = self.es.enter_context(nc.sbuf_tensor(name, shape, dtype))
        psb("c_identf", [128, 128])
        psb("c_identb", [128, 128], BF16)
        psb("c_ones", [128, 128])
        psb("c_flag", [128, 1])
        psb("c_wsc", [128, 8])
        psb("c_bqT", [128, H])
        psb("c_bkT", [128, H])
        psb("c_bvT", [128, H])
        psb("c_catt", [128, 1])
        nc.const_aps.register(F32, ATT_SCALE, P["c_catt"][:, 0:1])
        self.P = P

    def run_block(self, emit):
        final = {}
        with self.nc.Block() as block:
            for role, reg in [("sync", block.sync), ("gpsimd", block.gpsimd),
                              ("vector", block.vector), ("tensor", block.tensor),
                              ("scalar", block.scalar)]:
                def f(eng, role=role):
                    ctx = self._ctx(role, eng)
                    emit(role, eng, ctx)
                    final.clear()
                    final.update({k: s.n for k, s in ctx["S"].items()})
                reg(f)
        self.counts = dict(final)

    def _ctx(self, role, eng):
        sems = self.sems
        S = {k: Sem(h) for k, h in sems.items()}
        for k, v in self.counts.items():
            S[k].n = v
        ctx = {"S": S}

        def R(r):
            return role == r
        ctx["R"] = R

        def vlink(mk):
            c = S["vch"].bump()
            if R("vector"):
                mk().then_inc(sems["vch"], 1)
                eng.wait_ge(sems["vch"], c)
        ctx["vlink"] = vlink

        QOWNER = {"qs0": "sync", "qs1": "sync", "qw0": "sync", "qw1": "sync",
                  "qa0": "scalar", "qa1": "scalar", "qa2": "scalar",
                  "qg0": "gpsimd", "qg1": "gpsimd",
                  "qk0": "sync", "qk1": "sync"}

        def dma(q, dst, src):
            c = S[q].bump(16)
            if R(QOWNER[q]):
                eng.dma_start(dst, src).then_inc(sems[q], 16)
            return c
        ctx["dma"] = dma

        def w(semname, val, who):
            if R(who):
                eng.wait_ge(sems[semname], val)
        ctx["w"] = w
        return ctx



def b0_consts(prog):
    nc, io, P, sems = prog.nc, prog.io, prog.P, prog.sems

    def emit(role, eng, ctx):
        R, S, vlink, dma, w = ctx["R"], ctx["S"], ctx["vlink"], ctx["dma"], ctx["w"]
        btmp, bps = prog.b0
        for name in ["identf", "identb", "ones"]:
            dma("qs0", P["c_" + name][:], io[name][:, :])
        for i, nm in enumerate(["bq", "bk", "bv"]):
            dma("qs0", btmp[0:H, i * 128:(i + 1) * 128], io[nm][:, :])
        c_all = S["qs0"].n
        w("qs0", c_all, "vector")
        vlink(lambda: eng.tensor_scalar(P["c_flag"][:], P["c_ones"][:, 0:1],
                                        2.0**-30, -1.0, ALU.add, ALU.add))
        vlink(lambda: eng.tensor_scalar(P["c_flag"][:], P["c_flag"][:],
                                        -(2.0**30), 1.0, ALU.mult, ALU.add))
        vlink(lambda: eng.tensor_scalar(P["c_catt"][:], P["c_ones"][:, 0:1],
                                        ATT_SCALE, None, ALU.mult))
        c_tr = S["mm"].bump()
        if R("tensor"):
            eng.wait_ge(sems["qs0"], c_all)
            for i in range(3):
                li = eng.transpose(bps[:, i * 128:i * 128 + H],
                                   btmp[0:H, i * 128:(i + 1) * 128],
                                   P["c_identf"][0:H, 0:H])
            li.then_inc(sems["mm"], 1)
        c_ev = S["vec"].bump()
        if R("vector"):
            eng.wait_ge(sems["mm"], c_tr)
            for i, nm in enumerate(["c_bqT", "c_bkT", "c_bvT"]):
                li = eng.tensor_copy(P[nm][:, 0:H], bps[:, i * 128:i * 128 + H])
            li.then_inc(sems["vec"], 1)
        for who in ["sync", "scalar", "tensor", "gpsimd"]:
            w("vec", c_ev, who)

    with nc.sbuf_tensor("b0tmp", [128, 3 * 128], F32) as btmp, \
         nc.psum_tensor("b0ps", [128, 512], F32) as bps:
        prog.b0 = (btmp, bps)
        prog.run_block(emit)
    del prog.b0


def b1_wscale(prog):
    nc, io, P, sems = prog.nc, prog.io, prog.P, prog.sems

    def emit(role, eng, ctx):
        R, S, vlink, dma, w = ctx["R"], ctx["S"], ctx["vlink"], ctx["dma"], ctx["w"]
        blk, acc, tot, ps = prog.b1
        col = 0
        vred = []
        for wname, nb, ncol in [("w1", INNER // 128, D), ("w2", D // 128, INNER)]:
            for b in range(nb):
                par = col % 2
                qn = "qw0" if par == 0 else "qw1"
                if col >= 2:
                    w("vec", vred[col - 2], "sync")
                cd = dma(qn, blk[:, par * INNER:par * INNER + ncol],
                         io[wname][b * 128:(b + 1) * 128, :])
                cv = S["vec"].bump()
                if R("vector"):
                    eng.wait_ge(sems[qn], cd)
                    eng.tensor_reduce(acc[:, col:col + 1],
                                      blk[:, par * INNER:par * INNER + ncol],
                                      AX.X, ALU.add,
                                      apply_absolute_value=True).then_inc(sems["vec"], 1)
                vred.append(cv)
                col += 1
        w("vec", S["vec"].n, "vector")
        vlink(lambda: eng.tensor_reduce(tot[:, 0:1], acc[:, 0:64], AX.X, ALU.add))
        vlink(lambda: eng.tensor_reduce(tot[:, 1:2], acc[:, 64:80], AX.X, ALU.add))
        c_tot = S["vec"].bump()
        if R("vector"):
            eng.tensor_copy(tot[:, 2:3], tot[:, 0:1]).then_inc(sems["vec"], 1)
        c_bc = S["mm"].bump()
        if R("tensor"):
            eng.wait_ge(sems["vec"], c_tot)
            eng.matmul(ps[:, 0:2], P["c_ones"][:, 0:128], tot[:, 0:2],
                       start=True, stop=True).then_inc(sems["mm"], 1)
        w("mm", c_bc, "vector")
        for j in range(2):
            vlink(lambda j=j: eng.tensor_scalar(
                P["c_wsc"][:, 2 * j + 1:2 * j + 2], ps[:, j:j + 1],
                1.0 / (INNER * D), EPS, ALU.mult, ALU.max))
            vlink(lambda j=j: eng.reciprocal(
                P["c_wsc"][:, 2 * j:2 * j + 1], P["c_wsc"][:, 2 * j + 1:2 * j + 2]))
            vlink(lambda j=j: eng.tensor_scalar(
                P["c_wsc"][:, 4 + 2 * j:5 + 2 * j],
                P["c_wsc"][:, 2 * j + 1:2 * j + 2], 0.5, None, ALU.mult))
            vlink(lambda j=j: eng.tensor_scalar(
                P["c_wsc"][:, 5 + 2 * j:6 + 2 * j],
                P["c_wsc"][:, 2 * j + 1:2 * j + 2], -0.5, None, ALU.mult))
        c_done = S["vec"].bump()
        if R("vector"):
            eng.tensor_copy(tot[:, 2:3], P["c_wsc"][:, 0:1]).then_inc(sems["vec"], 1)
        for who in ["sync", "scalar", "tensor", "gpsimd"]:
            w("vec", c_done, who)

    with nc.sbuf_tensor("b1blk", [128, 2 * INNER], F32) as blk, \
         nc.sbuf_tensor("b1acc", [128, 80], F32) as acc, \
         nc.sbuf_tensor("b1tot", [128, 3], F32) as tot, \
         nc.psum_tensor("b1ps", [128, 512], F32) as ps:
        prog.b1 = (blk, acc, tot, ps)
        prog.run_block(emit)
    del prog.b1


def b2_wquant(prog):
    nc, io, P, sems = prog.nc, prog.io, prog.P, prog.sems

    def emit(role, eng, ctx):
        R, S, vlink, dma, w = ctx["R"], ctx["S"], ctx["vlink"], ctx["dma"], ctx["w"]
        pan, tern, panT, tmpa, tmpb, trp = prog.b2
        jobs = [("w1", pc, INNER // 128) for pc in range(KD)] + \
               [("w2", pc, KD) for pc in range(KI)]
        st_after = []
        evac_cnt = []
        for jn, (wname, pc, nib) in enumerate(jobs):
            par = jn % 2
            wsccol = 0 if wname == "w1" else 2
            src = io["w1"] if wname == "w1" else io["w2"]
            dst = io["w1qT"] if wname == "w1" else io["w2qT"]
            qn = "qw0" if par == 0 else "qw1"
            qsn = "qa1" if par == 0 else "qa2"
            if jn >= 2:
                w(qsn, st_after[jn - 2], "sync")
            cd = dma(qn,
                     pan[:, par * INNER:par * INNER + nib * 128].rearrange(
                         "p (i c) -> p i c", c=128),
                     src[:, pc * 128:(pc + 1) * 128].rearrange(
                         "(i p) c -> p i c", p=128))
            c_tern = S["vec"].bump()
            if R("vector"):
                eng.wait_ge(sems[qn], cd)
                if jn >= 1:
                    eng.wait_ge(sems["mm"], tern_tr_last)
                thrc = 4 + wsccol
                for ch in range((nib * 128 + 2047) // 2048):
                    a = ch * 2048
                    b = min((ch + 1) * 2048, nib * 128)
                    n = b - a
                    psl = slice(par * INNER + a, par * INNER + b)
                    eng.tensor_scalar(tmpb[:, 0:n], pan[:, psl],
                                      P["c_wsc"][:, thrc + 1:thrc + 2], None,
                                      ALU.is_lt)
                    li = eng.scalar_tensor_tensor(
                        tern[:, a:b], pan[:, psl],
                        P["c_wsc"][:, thrc:thrc + 1], tmpb[:, 0:n],
                        ALU.is_gt, ALU.subtract)
                li.then_inc(sems["vec"], 1)
            for ib in range(nib):
                slot = len(evac_cnt) % 2
                c_tr = S["mm"].bump()
                if R("tensor"):
                    if ib == 0:
                        eng.wait_ge(sems["vec"], c_tern)
                    if len(evac_cnt) >= 2:
                        eng.wait_ge(sems["sc"], evac_cnt[-2])
                    eng.transpose(trp[:, slot * 1024:slot * 1024 + 128],
                                  tern[:, ib * 128:(ib + 1) * 128],
                                  P["c_identb"][:]).then_inc(sems["mm"], 1)
                c_ev = S["sc"].bump()
                if R("scalar"):
                    eng.wait_ge(sems["mm"], c_tr)
                    eng.activation(
                        panT[:, par * INNER + ib * 128:par * INNER + (ib + 1) * 128],
                        trp[:, slot * 1024:slot * 1024 + 128],
                        AF.Identity).then_inc(sems["sc"], 1)
                evac_cnt.append(c_ev)
            tern_tr_last = S["mm"].n
            w("sc", evac_cnt[-1], "scalar")
            cs = dma(qsn, dst[pc * 128:(pc + 1) * 128, :],
                     panT[:, par * INNER:par * INNER + nib * 128])
            st_after.append(cs)
        for who in ["sync", "vector", "tensor", "scalar", "gpsimd"]:
            w("qa1", S["qa1"].n, who)
            w("qa2", S["qa2"].n, who)

    with nc.sbuf_tensor("b2pan", [128, 2 * INNER], F32) as pan, \
         nc.sbuf_tensor("b2tern", [128, INNER], BF16) as tern, \
         nc.sbuf_tensor("b2panT", [128, 2 * INNER], BF16) as panT, \
         nc.sbuf_tensor("b2tmpa", [128, 2048], F32) as tmpa, \
         nc.sbuf_tensor("b2tmpb", [128, 2048], F32) as tmpb, \
         nc.psum_tensor("b2ps", [128, 2048], BF16) as trp:
        prog.b2 = (pan, tern, panT, tmpa, tmpb, trp)
        prog.run_block(emit)
    del prog.b2


def ffn_pass(prog, src_name, dst_name, tag):
    nc, io, P, sems = prog.nc, prog.io, prog.P, prog.sems
    src_dram, dst_dram = io[src_name], io[dst_name]

    def emit(role, eng, ctx):
        R, S, vlink, dma, w = ctx["R"], ctx["S"], ctx["vlink"], ctx["dma"], ctx["w"]
        (g, xq, xqT, w1g, w2s, hqT, tmpa, tmpb, sgrid, ot,
         psA, psB, psC) = prog.fb
        stat = sgrid[:, 0:40]
        sqg = sgrid[:, 40:104]
        mxg = sgrid[:, 104:168]
        xt = ot[:, 0:D]
        sqdump = ot[:, D:D + 512]
        prev = {"v": 0, "s": 0, "st": 0, "stt": 0, "ffnout": 0}
        last_xq_tr = None
        for sup in range(NSUP):
            t0 = sup * SUP
            stat_done = []
            for t in range(SUP):
                if t >= 1:
                    w("vec", stat_done[t - 1][0], "sync")
                    w("sc", stat_done[t - 1][1], "sync")
                elif sup > 0:
                    w("vec", prev["v"], "sync")
                    w("sc", prev["s"], "sync")
                    w("qa0", prev["st"], "sync")
                cd = dma("qs0", xt, src_dram[(t0 + t) * 128:(t0 + t + 1) * 128, :])
                cv = S["vec"].bump()
                if R("vector"):
                    eng.wait_ge(sems["qs0"], cd)
                    eng.reduce_sum(stat[:, t:t + 1], xt, AX.X)
                    eng.reduce_max(stat[:, 4 + t:5 + t], xt, AX.X)
                    eng.tensor_reduce(stat[:, 8 + t:9 + t], xt, AX.X,
                                      ALU.min).then_inc(sems["vec"], 1)
                cs = S["sc"].bump()
                if R("scalar"):
                    eng.wait_ge(sems["qs0"], cd)
                    for ch in range(4):
                        li = eng.activation(
                            sqdump[:, 0:512],
                            xt[:, ch * 512:(ch + 1) * 512], AF.Square,
                            accum_out=sqg[:, t * 16 + ch:t * 16 + ch + 1])
                    li.then_inc(sems["sc"], 1)
                stat_done.append((cv, cs))
            w("vec", S["vec"].n, "vector")
            w("sc", S["sc"].n, "vector")
            for t in range(SUP):
                vlink(lambda t=t: eng.reduce_sum(stat[:, 12 + t:13 + t],
                                                 sqg[:, t * 16:t * 16 + 4], AX.X))
            vlink(lambda: eng.tensor_scalar(stat[:, 16:20], stat[:, 0:4],
                                            1.0 / D, None, ALU.mult))
            vlink(lambda: eng.tensor_tensor(stat[:, 36:40], stat[:, 16:20],
                                            stat[:, 16:20], ALU.mult))
            vlink(lambda: eng.scalar_tensor_tensor(
                stat[:, 36:40], stat[:, 12:16], 1.0 / D, stat[:, 36:40],
                ALU.mult, ALU.subtract))
            c_var = S["vec"].bump()
            if R("vector"):
                eng.tensor_copy(stat[:, 32:33], stat[:, 36:37]).then_inc(sems["vec"], 1)
            c_sd = S["sc"].bump()
            if R("scalar"):
                eng.wait_ge(sems["vec"], c_var)
                eng.activation(stat[:, 36:40], stat[:, 36:40],
                               AF.Sqrt).then_inc(sems["sc"], 1)
            w("sc", c_sd, "vector")
            vlink(lambda: eng.tensor_sub(stat[:, 20:24], stat[:, 4:8], stat[:, 16:20]))
            vlink(lambda: eng.tensor_sub(stat[:, 32:36], stat[:, 16:20], stat[:, 8:12]))
            vlink(lambda: eng.tensor_max(stat[:, 20:24], stat[:, 20:24], stat[:, 32:36]))
            vlink(lambda: eng.reciprocal(stat[:, 32:36], stat[:, 36:40]))
            vlink(lambda: eng.tensor_mul(stat[:, 20:24], stat[:, 20:24], stat[:, 32:36]))
            vlink(lambda: eng.tensor_scalar(stat[:, 20:24], stat[:, 20:24], EPS,
                                            None, ALU.max))
            vlink(lambda: eng.reciprocal(stat[:, 28:32], stat[:, 20:24]))
            vlink(lambda: eng.tensor_scalar(stat[:, 28:32], stat[:, 28:32], 127.0,
                                            None, ALU.mult))
            vlink(lambda: eng.tensor_mul(stat[:, 28:32], stat[:, 28:32],
                                         stat[:, 32:36]))
            vlink(lambda: eng.tensor_scalar(stat[:, 24:28], stat[:, 20:24],
                                            1.0 / 127.0, None, ALU.mult))
            vlink(lambda: eng.tensor_scalar(stat[:, 24:28], stat[:, 24:28],
                                            P["c_wsc"][:, 1:2], None, ALU.mult))
            c_xchain = S["vec"].bump()
            if R("vector"):
                eng.tensor_copy(stat[:, 32:33], stat[:, 24:25]).then_inc(sems["vec"], 1)
            xqT_done = 0
            xev = []
            qprev = 0
            for t in range(SUP):
                if t >= 1:
                    w("vec", qprev, "sync")
                else:
                    w("vec", stat_done[-1][0], "sync")
                    w("sc", stat_done[-1][1], "sync")
                cd = dma("qs0", xt, src_dram[(t0 + t) * 128:(t0 + t + 1) * 128, :])
                for grp in range(8):
                    cv = S["vec"].bump()
                    if R("vector"):
                        if grp == 0:
                            eng.wait_ge(sems["qs0"], cd)
                        if last_xq_tr is not None:
                            eng.wait_ge(sems["mm"], last_xq_tr)
                        xsl = xt[:, grp * 256:(grp + 1) * 256]
                        eng.tensor_scalar(tmpa[:, 0:256], xsl,
                                          stat[:, 16 + t:17 + t],
                                          stat[:, 28 + t:29 + t],
                                          ALU.subtract, ALU.mult)
                        eng.tensor_scalar(tmpb[:, 0:256], tmpa[:, 0:256],
                                          C64, -C64, ALU.add, ALU.add)
                        eng.tensor_scalar(tmpa[:, 0:256], tmpa[:, 0:256],
                                          C32, -C32, ALU.add, ALU.add)
                        eng.scalar_tensor_tensor(
                            xq[:, 0:256], tmpa[:, 0:256], P["c_flag"][:, 0:1],
                            tmpb[:, 0:256], ALU.mult, ALU.add).then_inc(sems["vec"], 1)
                    for kk in range(2):
                        k = grp * 2 + kk
                        nx = len(xev)
                        slot = nx % 2
                        c_tr = S["mm"].bump()
                        if R("tensor"):
                            eng.wait_ge(sems["vec"], cv)
                            if sup > 0 and t == 0 and grp == 0 and kk == 0:
                                eng.wait_ge(sems["sc"], prev["stt"])
                            if nx >= 2:
                                eng.wait_ge(sems["sc"], xev[nx - 2])
                            eng.transpose(psC[:, slot * 1024:slot * 1024 + 128],
                                          xq[:, kk * 128:(kk + 1) * 128],
                                          P["c_identb"][:]).then_inc(sems["mm"], 1)
                        c_ev = S["sc"].bump()
                        if R("scalar"):
                            eng.wait_ge(sems["mm"], c_tr)
                            eng.activation(
                                xqT[:, k * 512 + t * 128:k * 512 + (t + 1) * 128],
                                psC[:, slot * 1024:slot * 1024 + 128],
                                AF.Identity).then_inc(sems["sc"], 1)
                        xev.append(c_ev)
                    last_xq_tr = c_tr
                qprev = S["vec"].n
                xqT_done = xev[-1]
            mm_evac = {}
            w1g_used = {}
            for gg in range(16):
                par = gg % 2
                qn = "qw0" if par == 0 else "qw1"
                if gg >= 2:
                    w("mm", w1g_used[gg - 2], "sync")
                cd = dma(qn, w1g[:, par * KD * 512:(par + 1) * KD * 512].rearrange(
                             "p (k c) -> p k c", c=512),
                         io["w1qT"][:, gg * 512:(gg + 1) * 512].rearrange(
                             "(k p) c -> p k c", p=128))
                for t in range(SUP):
                    c_mm = S["mm"].bump()
                    if R("tensor"):
                        if t == 0:
                            eng.wait_ge(sems[qn], cd)
                            if gg == 0:
                                eng.wait_ge(sems["sc"], xqT_done)
                                if sup > 0:
                                    eng.wait_ge(sems["vec"], prev["ffnout"])
                        if gg >= 1:
                            eng.wait_ge(sems["sc"], mm_evac[(gg - 1, t)])
                        for k in range(KD):
                            li = eng.matmul(
                                psA[:, t * 512:(t + 1) * 512],
                                xqT[:, k * 512 + t * 128:k * 512 + (t + 1) * 128],
                                w1g[:, par * KD * 512 + k * 512:
                                    par * KD * 512 + (k + 1) * 512],
                                start=(k == 0), stop=(k == KD - 1))
                        li.then_inc(sems["mm"], 1)
                    gsl = g[:, t * INNER + gg * 512:t * INNER + (gg + 1) * 512]
                    cs = S["sc"].bump()
                    if R("scalar"):
                        eng.wait_ge(sems["mm"], c_mm)
                        if gg == 0 and t == 0:
                            eng.wait_ge(sems["vec"], c_xchain)
                        eng.activation(gsl, psA[:, t * 512:(t + 1) * 512],
                                       AF.Gelu, scale=stat[:, 24 + t:25 + t])
                        eng.activation(gsl, gsl, AF.Gelu)
                        eng.activation(sqdump[:, 0:512], gsl, AF.Square,
                                       accum_out=sqg[:, t * 16 + gg:t * 16 + gg + 1]
                                       ).then_inc(sems["sc"], 1)
                    mm_evac[(gg, t)] = cs
                    cv = S["vec"].bump()
                    if R("vector"):
                        eng.wait_ge(sems["sc"], cs)
                        eng.reduce_max(mxg[:, t * 16 + gg:t * 16 + gg + 1],
                                       gsl, AX.X,
                                       apply_absolute_value=True).then_inc(sems["vec"], 1)
                w1g_used[gg] = S["mm"].n
            w("vec", S["vec"].n, "vector")
            w("sc", S["sc"].n, "vector")
            for t in range(SUP):
                vlink(lambda t=t: eng.reduce_sum(stat[:, 0 + t:1 + t],
                                                 sqg[:, t * 16:(t + 1) * 16], AX.X))
                vlink(lambda t=t: eng.reduce_max(stat[:, 4 + t:5 + t],
                                                 mxg[:, t * 16:(t + 1) * 16], AX.X))
            c_hn = S["vec"].bump()
            if R("vector"):
                eng.tensor_copy(stat[:, 32:33], stat[:, 0:1]).then_inc(sems["vec"], 1)
            c_hsd = S["sc"].bump()
            if R("scalar"):
                eng.wait_ge(sems["vec"], c_hn)
                eng.activation(stat[:, 8:12], stat[:, 0:4],
                               AF.Sqrt).then_inc(sems["sc"], 1)
            w("sc", c_hsd, "vector")
            vlink(lambda: eng.tensor_scalar(stat[:, 8:12], stat[:, 8:12], 1e-12,
                                            None, ALU.max))
            vlink(lambda: eng.reciprocal(stat[:, 12:16], stat[:, 8:12]))
            vlink(lambda: eng.tensor_scalar(stat[:, 12:16], stat[:, 12:16],
                                            float(np.sqrt(INNER)), None, ALU.mult))
            vlink(lambda: eng.tensor_mul(stat[:, 20:24], stat[:, 4:8],
                                         stat[:, 12:16]))
            vlink(lambda: eng.tensor_scalar(stat[:, 20:24], stat[:, 20:24], EPS,
                                            None, ALU.max))
            vlink(lambda: eng.reciprocal(stat[:, 28:32], stat[:, 20:24]))
            vlink(lambda: eng.tensor_scalar(stat[:, 28:32], stat[:, 28:32],
                                            127.0, None, ALU.mult))
            vlink(lambda: eng.tensor_mul(stat[:, 28:32], stat[:, 28:32],
                                         stat[:, 12:16]))
            vlink(lambda: eng.tensor_scalar(stat[:, 24:28], stat[:, 20:24],
                                            0.5 / 127.0, None, ALU.mult))
            vlink(lambda: eng.tensor_scalar(stat[:, 24:28], stat[:, 24:28],
                                            P["c_wsc"][:, 3:4], None, ALU.mult))
            rnd = []
            for t in range(SUP):
                cv = S["vec"].bump()
                if R("vector"):
                    for ch in range(16):
                        gsl = g[:, t * INNER + ch * 512:t * INNER + (ch + 1) * 512]
                        eng.tensor_scalar(tmpa[:, 0:512], gsl,
                                          stat[:, 28 + t:29 + t], None, ALU.mult)
                        eng.tensor_scalar(tmpb[:, 0:512], tmpa[:, 0:512],
                                          C64, -C64, ALU.add, ALU.add)
                        eng.tensor_scalar(tmpa[:, 0:512], tmpa[:, 0:512],
                                          C32, -C32, ALU.add, ALU.add)
                        li = eng.scalar_tensor_tensor(
                            gsl, tmpa[:, 0:512], P["c_flag"][:, 0:1],
                            tmpb[:, 0:512], ALU.mult, ALU.add)
                    li.then_inc(sems["vec"], 1)
                rnd.append(cv)
            last_add = 0
            sts = []
            slab_gate = []
            for half in range(2):
                tA_, tB_ = t0 + 2 * half, t0 + 2 * half + 1
                xres = []
                for tt in range(2):
                    if half == 0:
                        w("vec", rnd[-1], "gpsimd")
                        if sup > 0:
                            w("qa0", prev["st"], "gpsimd")
                    else:
                        w("qa0", sts[-1], "gpsimd")
                    xres.append(dma("qg0", ot[:, tt * D:(tt + 1) * D],
                                    src_dram[(tA_ + tt) * 128:(tA_ + tt + 1) * 128, :]))
                for dmh in range(2):
                    tr_evs = []
                    mm2_end = None
                    for icb in range(32):
                        par2 = len(slab_gate) % 2
                        qn2 = "qw0" if par2 == 0 else "qw1"
                        if len(slab_gate) >= 2:
                            w("mm", slab_gate[-2], "sync")
                        cd = dma(qn2,
                                 w2s[:, par2 * 2 * 1024:(par2 + 1) * 2 * 1024].rearrange(
                                     "p (i c) -> p i c", c=1024),
                                 io["w2qT"][icb * 256:(icb + 1) * 256,
                                            dmh * 1024:(dmh + 1) * 1024].rearrange(
                                     "(i p) c -> p i c", p=128))
                        for ici in range(2):
                            ic = icb * 2 + ici
                            evs2 = []
                            if dmh == 0:
                                for tt in range(2):
                                    t = tA_ + tt - t0
                                    ntr = len(tr_evs)
                                    slot = ntr % 2
                                    c_tr = S["mm"].bump()
                                    if R("tensor"):
                                        if ic == 0 and tt == 0:
                                            eng.wait_ge(sems["vec"],
                                                        rnd[2 * half])
                                            eng.wait_ge(sems["vec"],
                                                        rnd[2 * half + 1])
                                            if half == 0:
                                                eng.wait_ge(sems["sc"],
                                                            mm_evac[(15, SUP - 1)])
                                        if ntr >= 2:
                                            eng.wait_ge(sems["sc"],
                                                        tr_evs[ntr - 2])
                                        eng.transpose(
                                            psB[:, slot * 512:slot * 512 + 128],
                                            g[:, t * INNER + ic * 128:
                                              t * INNER + (ic + 1) * 128],
                                            P["c_identf"][:]).then_inc(sems["mm"], 1)
                                    c_ev = S["sc"].bump()
                                    if R("scalar"):
                                        eng.wait_ge(sems["mm"], c_tr)
                                        eng.activation(
                                            w1g[:, ic * 256 + tt * 128:
                                                ic * 256 + (tt + 1) * 128],
                                            psB[:, slot * 512:slot * 512 + 128],
                                            AF.Identity).then_inc(sems["sc"], 1)
                                    evs2.append(c_ev)
                                    tr_evs.append(c_ev)
                            c_mm = S["mm"].bump()
                            if R("tensor"):
                                if dmh == 0:
                                    eng.wait_ge(sems["sc"], evs2[-1])
                                if ici == 0:
                                    eng.wait_ge(sems[qn2], cd)
                                if ic == 0:
                                    eng.wait_ge(sems["vec"], last_add)
                                for tt in range(2):
                                    for nn in range(2):
                                        li = eng.matmul(
                                            psA[:, (tt * 2 + nn) * 512:
                                                (tt * 2 + nn + 1) * 512],
                                            w1g[:, ic * 256 + tt * 128:
                                                ic * 256 + (tt + 1) * 128],
                                            w2s[:, par2 * 2 * 1024 + ici * 1024 +
                                                nn * 512:
                                                par2 * 2 * 1024 + ici * 1024 +
                                                (nn + 1) * 512],
                                            start=(ic == 0), stop=(ic == KI - 1))
                                li.then_inc(sems["mm"], 1)
                            mm2_end = S["mm"].n
                        slab_gate.append(S["mm"].n)
                    for tt in range(2):
                        t = tA_ + tt - t0
                        for nn in range(2):
                            c_add = S["vec"].bump()
                            if R("vector"):
                                eng.wait_ge(sems["mm"], mm2_end)
                                if dmh == 0 and tt == 0 and nn == 0:
                                    eng.wait_ge(sems["qg0"], xres[-1])
                                osl = ot[:, tt * D + dmh * 1024 + nn * 512:
                                         tt * D + dmh * 1024 + (nn + 1) * 512]
                                eng.scalar_tensor_tensor(
                                    osl, psA[:, (tt * 2 + nn) * 512:
                                             (tt * 2 + nn + 1) * 512],
                                    stat[:, 24 + t:25 + t], osl,
                                    ALU.mult, ALU.add).then_inc(sems["vec"], 1)
                            last_add = c_add
                w("vec", last_add, "scalar")
                for tt in range(2):
                    sts.append(dma("qa0",
                                   dst_dram[(tA_ + tt) * 128:(tA_ + tt + 1) * 128, :],
                                   ot[:, tt * D:(tt + 1) * D]))
            prev = {"v": S["vec"].n, "s": S["sc"].n, "st": sts[-1],
                    "stt": S["sc"].n, "ffnout": last_add}
        for who in ["sync", "vector", "tensor", "scalar", "gpsimd"]:
            w("qa0", S["qa0"].n, who)

    with contextlib.ExitStack() as stk:
        sb, pt = prog.nc.sbuf_tensor, prog.nc.psum_tensor
        prog.fb = (
            stk.enter_context(sb(f"g{tag}", [128, SUP * INNER], F32)),
            stk.enter_context(sb(f"xq{tag}", [128, 256], BF16)),
            stk.enter_context(sb(f"xqT{tag}", [128, KD * 512], BF16)),
            stk.enter_context(sb(f"w1g{tag}", [128, 2 * KD * 512], BF16)),
            stk.enter_context(sb(f"w2s{tag}", [128, 2 * 2 * 1024], BF16)),
            stk.enter_context(sb(f"hqT{tag}", [128, 2 * 256], BF16)),
            stk.enter_context(sb(f"tA{tag}", [128, 512], F32)),
            stk.enter_context(sb(f"tB{tag}", [128, 512], F32)),
            stk.enter_context(sb(f"sg{tag}", [128, 168], F32)),
            stk.enter_context(sb(f"ot{tag}", [128, 2 * D], F32)),
            stk.enter_context(pt(f"pA{tag}", [128, SUP * 512], F32)),
            stk.enter_context(pt(f"pB{tag}", [128, 1024], F32)),
            stk.enter_context(pt(f"pC{tag}", [128, 2048], BF16)),
        )
        prog.run_block(emit)
    del prog.fb


def b4_qkv(prog, src_name):
    nc, io, P, sems = prog.nc, prog.io, prog.P, prog.sems
    src_dram = io[src_name]

    def emit(role, eng, ctx):
        R, S, vlink, dma, w = ctx["R"], ctx["S"], ctx["vlink"], ctx["dma"], ctx["w"]
        (xt, xnb, xnT, wb, stage, sgrid, psQ, psK, psV, psT) = prog.b4
        if B4_SERIAL:
            _dma0 = ctx["dma"]
            def dma(q, dst, srcap):
                for sname in ["vec", "sc", "mm"]:
                    w(sname, S[sname].n, {"qs0": "sync", "qs1": "sync",
                                          "qa0": "scalar", "qg0": "gpsimd",
                                          "qk0": "sync", "qk1": "sync"}[q])
                return _dma0(q, dst, srcap)
            _sb = Sem.bump
            def serial_guard(which):
                for sname in ["vec", "sc", "mm"]:
                    if sname != which:
                        w(sname, S[sname].n, {"vec": "vector", "sc": "scalar",
                                              "mm": "tensor"}[which])
            class _SemWrap:
                pass
            ctx["serial_guard"] = serial_guard
        else:
            ctx["serial_guard"] = lambda which: None
        serial_guard = ctx["serial_guard"]
        stat = sgrid[:, 0:40]
        sqg = sgrid[:, 40:104]
        GT = 8
        prevv = {"v": 0, "s": 0}
        for gp in range(2):
            t0 = gp * GT
            stat_done = []
            for t in range(GT):
                if t >= 1:
                    w("vec", stat_done[t - 1][0], "sync")
                    w("sc", stat_done[t - 1][1], "sync")
                elif gp > 0:
                    w("vec", prevv["v"], "sync")
                    w("sc", prevv["s"], "sync")
                cd = dma("qs0", xt[:, 0:D],
                         src_dram[(t0 + t) * 128:(t0 + t + 1) * 128, :])
                serial_guard("vec")
                cv = S["vec"].bump()
                if R("vector"):
                    eng.wait_ge(sems["qs0"], cd)
                    eng.reduce_sum(stat[:, t:t + 1], xt[:, 0:D],
                                   AX.X).then_inc(sems["vec"], 1)
                cs = S["sc"].bump()
                if R("scalar"):
                    eng.wait_ge(sems["qs0"], cd)
                    for ch in range(4):
                        li = eng.activation(
                            stage[:, 0:512],
                            xt[:, ch * 512:(ch + 1) * 512], AF.Square,
                            accum_out=sqg[:, t * 4 + ch:t * 4 + ch + 1])
                    li.then_inc(sems["sc"], 1)
                stat_done.append((cv, cs))
            w("vec", S["vec"].n, "vector")
            w("sc", S["sc"].n, "vector")
            for t in range(GT):
                vlink(lambda t=t: eng.reduce_sum(stat[:, 16 + t:17 + t],
                                                 sqg[:, t * 4:t * 4 + 4], AX.X))
            vlink(lambda: eng.tensor_scalar(stat[:, 0:8], stat[:, 0:8],
                                            1.0 / D, None, ALU.mult))
            vlink(lambda: eng.tensor_tensor(stat[:, 24:32], stat[:, 0:8],
                                            stat[:, 0:8], ALU.mult))
            vlink(lambda: eng.scalar_tensor_tensor(
                stat[:, 24:32], stat[:, 16:24], 1.0 / D, stat[:, 24:32],
                ALU.mult, ALU.subtract))
            vlink(lambda: eng.tensor_scalar(stat[:, 24:32], stat[:, 24:32],
                                            EPS, None, ALU.add))
            c_var = S["vec"].bump()
            if R("vector"):
                eng.tensor_copy(stat[:, 32:33], stat[:, 24:25]).then_inc(sems["vec"], 1)
            c_sd = S["sc"].bump()
            if R("scalar"):
                eng.wait_ge(sems["vec"], c_var)
                eng.activation(stat[:, 24:32], stat[:, 24:32],
                               AF.Sqrt).then_inc(sems["sc"], 1)
            w("sc", c_sd, "vector")
            vlink(lambda: eng.reciprocal(stat[:, 8:16], stat[:, 24:32]))
            xev = []
            qprev = 0
            for t in range(GT):
                if t >= 1:
                    w("vec", qprev, "sync")
                else:
                    w("vec", stat_done[-1][0], "sync")
                    w("sc", stat_done[-1][1], "sync")
                cd = dma("qs0", xt[:, 0:D],
                         src_dram[(t0 + t) * 128:(t0 + t + 1) * 128, :])
                serial_guard("vec")
                cv = S["vec"].bump()
                if R("vector"):
                    eng.wait_ge(sems["qs0"], cd)
                    eng.tensor_scalar(xnb[:, 0:D], xt[:, 0:D], stat[:, t:t + 1],
                                      stat[:, 8 + t:9 + t],
                                      ALU.subtract, ALU.mult).then_inc(sems["vec"], 1)
                qprev = cv
                for k in range(KD):
                    nx = len(xev)
                    slot = nx % 2
                    serial_guard("mm")
                    c_tr = S["mm"].bump()
                    if R("tensor"):
                        eng.wait_ge(sems["vec"], cv)
                        if nx >= 2:
                            eng.wait_ge(sems["vec"], xev[nx - 2])
                        eng.transpose(psT[:, slot * 512:slot * 512 + 128],
                                      xnb[:, k * 128:(k + 1) * 128],
                                      P["c_identf"][:]).then_inc(sems["mm"], 1)
                    serial_guard("vec")
                    c_ev = S["vec"].bump()
                    if R("vector"):
                        eng.wait_ge(sems["mm"], c_tr)
                        eng.tensor_copy(
                            xnT[:, k * 1024 + t * 128:k * 1024 + (t + 1) * 128],
                            psT[:, slot * 512:slot * 512 + 128]).then_inc(sems["vec"], 1)
                    xev.append(c_ev)
            xnT_done = xev[-1]
            if DBG_B4:
                w("vec", xnT_done, "gpsimd")
                dma("qg0", io["dbgxnT"][gp * 128:(gp + 1) * 128, :],
                    xnT[:, 0:KD * 1024])
                dma("qg0", io["dbgstat"][gp * 128:(gp + 1) * 128, :],
                    sgrid[:, 0:104])
            wprev = {}
            stq_gate = []
            for h in range(H):
                parw = h % 2
                if h >= 2:
                    w("mm", wprev[(h - 2, "mm")], "sync")
                qwn = "qw0" if parw == 0 else "qw1"
                cds = []
                for wi, wn in enumerate(["wq", "wk", "wv"]):
                    cds.append(dma(
                        qwn,
                        wb[:, parw * 3 * KD * 128 + wi * KD * 128:
                           parw * 3 * KD * 128 + (wi + 1) * KD * 128].rearrange(
                            "p (k c) -> p k c", c=128),
                        io[wn][h * D:(h + 1) * D, :].rearrange(
                            "(k p) c -> p k c", p=128)))
                for wi, (ps_, dstd, bT) in enumerate(
                        [(psQ, io["qsc"], "c_bqT"), (psK, io["ksc"], "c_bkT")]):
                    serial_guard("mm")
                    c_mm = S["mm"].bump()
                    if R("tensor"):
                        eng.wait_ge(sems[qwn], cds[-1])
                        if h == 0:
                            eng.wait_ge(sems["vec"], xnT_done)
                        if h >= 1:
                            eng.wait_ge(sems["sc"], wprev[(h - 1, "sc")])
                        for nn in range(2):
                            for k in range(KD):
                                li = eng.matmul(
                                    ps_[:, nn * 512:(nn + 1) * 512],
                                    wb[:, parw * 3 * KD * 128 + wi * KD * 128 +
                                       k * 128:parw * 3 * KD * 128 +
                                       wi * KD * 128 + (k + 1) * 128],
                                    xnT[:, k * 1024 + nn * 512:
                                        k * 1024 + (nn + 1) * 512],
                                    start=(k == 0), stop=(k == KD - 1))
                        li.then_inc(sems["mm"], 1)
                    serial_guard("sc")
                    cs = S["sc"].bump()
                    if R("scalar"):
                        eng.wait_ge(sems["mm"], c_mm)
                        if len(stq_gate) >= 3:
                            eng.wait_ge(sems[stq_gate[-3][0]], stq_gate[-3][1])
                        eng.activation(stage[:, wi * 1024:(wi + 1) * 1024],
                                       ps_[:, 0:1024], AF.Identity,
                                       bias=P[bT][:, h:h + 1]).then_inc(sems["sc"], 1)
                    w("sc", cs, "scalar")
                    qan = ["qa0", "qa1"][wi]
                    stq_gate.append((qan, dma(qan, dstd[h * 128:(h + 1) * 128,
                                              gp * 1024:(gp + 1) * 1024],
                                              stage[:, wi * 1024:(wi + 1) * 1024])))
                vev = []
                for t in range(GT):
                    slotv = t % 2
                    serial_guard("mm")
                    c_mm = S["mm"].bump()
                    if R("tensor"):
                        if t >= 2:
                            eng.wait_ge(sems["sc"], vev[t - 2])
                        if t <= 1 and h >= 1:
                            eng.wait_ge(sems["sc"], wprev[(h - 1, "sc")])
                        for k in range(KD):
                            li = eng.matmul(
                                psV[:, slotv * 512:slotv * 512 + 128],
                                xnT[:, k * 1024 + t * 128:k * 1024 + (t + 1) * 128],
                                wb[:, parw * 3 * KD * 128 + 2 * KD * 128 +
                                   k * 128:parw * 3 * KD * 128 +
                                   2 * KD * 128 + (k + 1) * 128],
                                start=(k == 0), stop=(k == KD - 1))
                        li.then_inc(sems["mm"], 1)
                    serial_guard("sc")
                    cs = S["sc"].bump()
                    if R("scalar"):
                        eng.wait_ge(sems["mm"], c_mm)
                        if t == 0 and len(stq_gate) >= 3:
                            eng.wait_ge(sems[stq_gate[-3][0]], stq_gate[-3][1])
                        eng.activation(stage[:, 2048 + t * 128:2048 + (t + 1) * 128],
                                       psV[:, slotv * 512:slotv * 512 + 128],
                                       AF.Identity).then_inc(sems["sc"], 1)
                    vev.append(cs)
                w("sc", vev[-1], "scalar")
                stq_gate.append(("qa2", dma(
                    "qa2", io["vsc"][h * T + gp * 1024:h * T + (gp + 1) * 1024, :
                                     ].rearrange("(t p) c -> p t c", p=128),
                    stage[:, 2048:2048 + GT * 128].rearrange(
                        "p (t c) -> p t c", c=128))))
                wprev[(h, "mm")] = S["mm"].n
                wprev[(h, "sc")] = S["sc"].n
            prevv = {"v": S["vec"].n, "s": S["sc"].n}
        for who in ["sync", "vector", "tensor", "scalar", "gpsimd"]:
            w("qa0", S["qa0"].n, who)
            w("qa1", S["qa1"].n, who)
            w("qa2", S["qa2"].n, who)

    with contextlib.ExitStack() as stk:
        sb, pt = prog.nc.sbuf_tensor, prog.nc.psum_tensor
        prog.b4 = (
            stk.enter_context(sb("b4xt", [128, D], F32)),
            stk.enter_context(sb("b4xn", [128, D], F32)),
            stk.enter_context(sb("b4xnT", [128, KD * 1024], F32)),
            stk.enter_context(sb("b4wb", [128, 2 * 3 * KD * 128], F32)),
            stk.enter_context(sb("b4stage", [128, 3072], F32)),
            stk.enter_context(sb("b4sg", [128, 40 + 64], F32)),
            stk.enter_context(pt("b4psQ", [128, 1024], F32)),
            stk.enter_context(pt("b4psK", [128, 1024], F32)),
            stk.enter_context(pt("b4psV", [128, 1024], F32)),
            stk.enter_context(pt("b4psT", [128, 1024], F32)),
        )
        prog.run_block(emit)
    del prog.b4


def b5_attn(prog, src_name, dst_name):
    nc, io, P, sems = prog.nc, prog.io, prog.P, prog.sems
    src_dram, dst_dram = io[src_name], io[dst_name]

    def emit(role, eng, ctx):
        R, S, vlink, dma, w = ctx["R"], ctx["S"], ctx["vlink"], ctx["dma"], ctx["w"]
        (kh, vh, qh, pbuf, ptb, ystr, dstat, diag, caus, borep,
         xsl, ost, wos, psA, psB, psD, psE) = prog.b5
        c_c = dma("qs0", caus[:], io["caus"][:, :])
        c_b = dma("qs0", borep[0:1, 0:D], io["bo"][:, :])
        c_bc = S["mm"].bump()
        if R("tensor"):
            eng.wait_ge(sems["qs0"], max(c_c, c_b))
            for nn in range(4):
                li = eng.matmul(psA[:, nn * 512:(nn + 1) * 512],
                                P["c_ones"][0:1, 0:128],
                                borep[0:1, nn * 512:(nn + 1) * 512],
                                start=True, stop=True)
            li.then_inc(sems["mm"], 1)
        c_bv = S["vec"].bump()
        if R("vector"):
            eng.wait_ge(sems["mm"], c_bc)
            eng.tensor_copy(borep[:], psA[:, 0:2048]).then_inc(sems["vec"], 1)
        hk_prev = {}
        last_wo_add = c_bv
        ost_gate = []
        xsl_gate = []
        for sp in range(NSTRIPE):
            q0 = sp * (STRIPE // 128)
            for h in range(H):
                hg = sp * H + h
                parh = hg % 2
                if hg >= 2:
                    w("mm", hk_prev[(hg - 2, "mm")], "sync")
                    w("sc", hk_prev[(hg - 2, "sc")], "sync")
                qk = "qk0" if parh == 0 else "qk1"
                dma(qk, kh[:, parh * T:(parh + 1) * T],
                    io["ksc"][h * 128:(h + 1) * 128, :])
                dma(qk, vh[:, parh * NT * 128:(parh + 1) * NT * 128].rearrange(
                        "p (t c) -> p t c", c=128),
                    io["vsc"][h * T:(h + 1) * T, :].rearrange(
                        "(t p) c -> p t c", p=128))
                cdq = dma(qk, qh[:, parh * STRIPE:(parh + 1) * STRIPE],
                          io["qsc"][h * 128:(h + 1) * 128,
                                    sp * STRIPE:(sp + 1) * STRIPE])
                first_of_head = True
                for qi_l in range(STRIPE // 128):
                    qi = q0 + qi_l
                    nkeys = (qi + 1) * 128
                    nck = (nkeys + 511) // 512
                    c_s = S["mm"].bump()
                    if R("tensor"):
                        if first_of_head:
                            eng.wait_ge(sems[qk], cdq)
                            if sp == 0 and h == 0:
                                eng.wait_ge(sems["vec"], c_bv)
                        eng.wait_ge(sems["sc"], S["sc"].n)
                        if first_of_head and h == 0:
                            eng.wait_ge(sems["vec"], last_wo_add)
                        for ck in range(nck):
                            a = ck * 512
                            b = min(nkeys, a + 512)
                            li = eng.matmul(
                                psA[:, a:b],
                                qh[:, parh * STRIPE + qi_l * 128:
                                   parh * STRIPE + (qi_l + 1) * 128],
                                kh[:, parh * T + a:parh * T + b],
                                start=True, stop=True)
                        li.then_inc(sems["mm"], 1)
                    first_of_head = False
                    c_mask = S["vec"].bump()
                    if R("vector"):
                        eng.wait_ge(sems["mm"], c_s)
                        eng.tensor_add(psA[:, qi * 128:nkeys],
                                       psA[:, qi * 128:nkeys],
                                       caus[:]).then_inc(sems["vec"], 1)
                    vlink(lambda: eng.reduce_max(dstat[:, 0:1],
                                                 psA[:, 0:nkeys], AX.X))
                    vlink(lambda: eng.tensor_scalar(dstat[:, 1:2], dstat[:, 0:1],
                                                    -ATT_SCALE, None, ALU.mult))
                    c_negm = S["vch"].n
                    cs_exp = S["sc"].bump()
                    if R("scalar"):
                        eng.wait_ge(sems["vch"], c_negm)
                        eng.activation(pbuf[:, 0:nkeys], psA[:, 0:nkeys],
                                       AF.Exp, bias=dstat[:, 1:2],
                                       scale=ATT_SCALE,
                                       accum_out=dstat[:, 2:3]).then_inc(sems["sc"], 1)
                    w("sc", cs_exp, "vector")
                    vlink(lambda: eng.reciprocal(dstat[:, 3:4], dstat[:, 2:3]))
                    vlink(lambda: eng.tensor_scalar(pbuf[:, 0:nkeys],
                                                    pbuf[:, 0:nkeys],
                                                    dstat[:, 3:4], None, ALU.mult))
                    c_diag = S["vch"].n
                    pev = []
                    for kc in range(qi + 1):
                        slot = kc % 3
                        tdst = (psB[:, slot * 512:slot * 512 + 128] if slot < 2
                                else psE[:, 0:128])
                        c_tr = S["mm"].bump()
                        if R("tensor"):
                            if kc == 0:
                                eng.wait_ge(sems["vch"], c_diag)
                            if kc >= 3:
                                eng.wait_ge(sems["vec"], pev[kc - 3])
                            eng.transpose(tdst,
                                          pbuf[:, kc * 128:(kc + 1) * 128],
                                          P["c_identf"][:]).then_inc(sems["mm"], 1)
                        c_ev = S["vec"].bump()
                        if R("vector"):
                            eng.wait_ge(sems["mm"], c_tr)
                            eng.tensor_copy(ptb[:, kc * 128:(kc + 1) * 128],
                                            tdst).then_inc(sems["vec"], 1)
                        pev.append(c_ev)
                    c_av = S["mm"].bump()
                    if R("tensor"):
                        eng.wait_ge(sems["vec"], pev[-1])
                        for kc in range(qi + 1):
                            li = eng.matmul(
                                psD[:, 0:128],
                                vh[:, parh * NT * 128 + kc * 128:
                                   parh * NT * 128 + (kc + 1) * 128],
                                ptb[:, kc * 128:(kc + 1) * 128],
                                start=(kc == 0), stop=(kc == qi))
                        li.then_inc(sems["mm"], 1)
                    cs_y = S["sc"].bump()
                    if R("scalar"):
                        eng.wait_ge(sems["mm"], c_av)
                        eng.activation(ystr[:, h * STRIPE + qi_l * 128:
                                            h * STRIPE + (qi_l + 1) * 128],
                                       psD[:, 0:128], AF.Identity,
                                       bias=P["c_bvT"][:, h:h + 1]).then_inc(sems["sc"], 1)
                hk_prev[(hg, "mm")] = S["mm"].n
                hk_prev[(hg, "sc")] = S["sc"].n
            c_heads = S["sc"].n
            wo_banks = [psA[:, i * 512:(i + 1) * 512] for i in range(4)] + \
                       [psB[:, i * 512:(i + 1) * 512] for i in range(2)] + \
                       [psD[:, 0:512], psE[:, 0:512]]
            wo_gate = []
            for nn in range(4):
                wo_last = None
                for k in range(KD):
                    j = len(wo_gate)
                    parw = j % 2
                    qwn = "qw0" if parw == 0 else "qw1"
                    if j >= 2:
                        w("mm", wo_gate[j - 2], "sync")
                    cd = dma(qwn, wos[:, parw * 512:(parw + 1) * 512],
                             io["wo"][k * 128:(k + 1) * 128,
                                      nn * 512:(nn + 1) * 512])
                    c_mm = S["mm"].bump()
                    if R("tensor"):
                        eng.wait_ge(sems[qwn], cd)
                        if k == 0:
                            if nn == 0:
                                eng.wait_ge(sems["sc"], c_heads)
                            eng.wait_ge(sems["vec"], last_wo_add)
                        for qb in range(8):
                            li = eng.matmul(
                                wo_banks[qb],
                                ystr[:, k * STRIPE + qb * 128:
                                     k * STRIPE + (qb + 1) * 128],
                                wos[:, parw * 512:(parw + 1) * 512],
                                start=(k == 0), stop=(k == KD - 1))
                        li.then_inc(sems["mm"], 1)
                    wo_gate.append(S["mm"].n)
                wo_last = wo_gate[-1]
                for qb in range(8):
                    parx = len(ost_gate) % 2
                    qgn = "qg0" if parx == 0 else "qg1"
                    qan = "qa0" if parx == 0 else "qa1"
                    if len(xsl_gate) >= 2:
                        w("vec", xsl_gate[-2], "gpsimd")
                    cdx = dma(qgn, xsl[:, parx * 512:(parx + 1) * 512],
                              src_dram[(q0 + qb) * 128:(q0 + qb + 1) * 128,
                                       nn * 512:(nn + 1) * 512])
                    c_a1 = S["vec"].bump(2)
                    if R("vector"):
                        eng.wait_ge(sems["mm"], wo_last)
                        eng.wait_ge(sems[qgn], cdx)
                        if len(ost_gate) >= 2:
                            eng.wait_ge(sems[qan], ost_gate[-2])
                        eng.tensor_add(ost[:, parx * 512:(parx + 1) * 512],
                                       wo_banks[qb],
                                       xsl[:, parx * 512:(parx + 1) * 512])
                        eng.tensor_add(ost[:, parx * 512:(parx + 1) * 512],
                                       ost[:, parx * 512:(parx + 1) * 512],
                                       borep[:, nn * 512:(nn + 1) * 512]
                                       ).then_inc(sems["vec"], 2)
                    last_wo_add = c_a1
                    xsl_gate.append(c_a1)
                    w("vec", c_a1, "scalar")
                    cst = dma(qan,
                              dst_dram[(q0 + qb) * 128:(q0 + qb + 1) * 128,
                                       nn * 512:(nn + 1) * 512],
                              ost[:, parx * 512:(parx + 1) * 512])
                    ost_gate.append(cst)
        for who in ["sync", "vector", "tensor", "scalar", "gpsimd"]:
            w("qa0", S["qa0"].n, who)
            w("qa1", S["qa1"].n, who)

    with contextlib.ExitStack() as stk:
        sb, pt = prog.nc.sbuf_tensor, prog.nc.psum_tensor
        prog.b5 = (
            stk.enter_context(sb("b5kh", [128, 2 * T], F32)),
            stk.enter_context(sb("b5vh", [128, 2 * NT * 128], F32)),
            stk.enter_context(sb("b5qh", [128, 2 * STRIPE], F32)),
            stk.enter_context(sb("b5p", [128, T], F32)),
            stk.enter_context(sb("b5pt", [128, T], F32)),
            stk.enter_context(sb("b5y", [128, H * STRIPE], F32)),
            stk.enter_context(sb("b5ds", [128, 8], F32)),
            stk.enter_context(sb("b5diag", [128, 128], F32)),
            stk.enter_context(sb("b5caus", [128, 128], F32)),
            stk.enter_context(sb("b5bo", [128, D], F32)),
            stk.enter_context(sb("b5xs", [128, 2 * 512], F32)),
            stk.enter_context(sb("b5os", [128, 2 * 512], F32)),
            stk.enter_context(sb("b5wo", [128, 2 * 512], F32)),
            stk.enter_context(pt("b5psA", [128, 2048], F32)),
            stk.enter_context(pt("b5psB", [128, 1024], F32)),
            stk.enter_context(pt("b5psD", [128, 512], F32)),
            stk.enter_context(pt("b5psE", [128, 512], F32)),
        )
        prog.run_block(emit)
    del prog.b5


_CACHE = {}


def _build():
    if "nc" in _CACHE:
        return _CACHE["nc"]
    prog = Prog()
    b0_consts(prog)
    b1_wscale(prog)
    b2_wquant(prog)
    if STAGE == 1:
        ffn_pass(prog, "x", "y", "f1")
    elif STAGE == 2:
        ffn_pass(prog, "x", "xr1", "f1")
        b4_qkv(prog, "xr1")
        b5_attn(prog, "xr1", "y")
    else:
        ffn_pass(prog, "x", "xr1", "f1")
        b4_qkv(prog, "xr1")
        b5_attn(prog, "xr1", "xr2")
        ffn_pass(prog, "xr2", "y", "f3")
    prog.es.close()
    _CACHE["nc"] = prog.nc
    return prog.nc


def kernel(**inputs):
    nc = _build()
    x = np.ascontiguousarray(np.asarray(inputs["L0_attn_x"], np.float32))
    sh = dict(
        w1=np.ascontiguousarray(np.asarray(inputs["L0_ffn_w1"], np.float32)),
        w2=np.ascontiguousarray(np.asarray(inputs["L0_ffn_w2"], np.float32)),
        wq=np.ascontiguousarray(np.asarray(inputs["L0_attn_Wq"], np.float32).reshape(H * D, HD)),
        wk=np.ascontiguousarray(np.asarray(inputs["L0_attn_Wk"], np.float32).reshape(H * D, HD)),
        wv=np.ascontiguousarray(np.asarray(inputs["L0_attn_Wv"], np.float32).reshape(H * D, HD)),
        wo=np.ascontiguousarray(np.asarray(inputs["L0_attn_Wo"], np.float32)),
        bq=np.ascontiguousarray(np.asarray(inputs["L0_attn_bq"], np.float32)),
        bk=np.ascontiguousarray(np.asarray(inputs["L0_attn_bk"], np.float32)),
        bv=np.ascontiguousarray(np.asarray(inputs["L0_attn_bv"], np.float32)),
        bo=np.ascontiguousarray(np.asarray(inputs["L0_attn_bo"], np.float32).reshape(1, D)),
        identf=np.eye(128, dtype=np.float32),
        identb=np.eye(128).astype(ml_dtypes.bfloat16),
        ones=np.ones((128, 128), np.float32),
        caus=np.ascontiguousarray(np.where(
            np.arange(128)[:, None] >= np.arange(128)[None, :],
            np.float32(0), np.finfo(np.float32).min).astype(np.float32)),
    )
    in_maps = [dict(sh, x=np.ascontiguousarray(x[c])) for c in range(N_CORES)]
    res = run_bass_kernel_spmd(nc, in_maps, core_ids=list(range(N_CORES)))
    return np.stack([r["y"] for r in res.results], axis=0)
